# version 09
# baseline (speedup 1.0000x reference)
import contextlib

import numpy as np
import ml_dtypes

import concourse.bass as bass
import concourse.mybir as mybir
from concourse.bass_utils import run_bass_kernel_spmd

F32 = mybir.dt.float32
BF16 = mybir.dt.bfloat16
AF = mybir.ActivationFunctionType
ALU = mybir.AluOpType
AX = mybir.AxisListType

P = 128
S = 2048
D = 2048
NT = 16
OWN = 8
HD = 64
KC = 16

O_IDENTBF = 0
O_IDENTF = 256
O_ONESBF = 768
O_ONESF = 800
O_CONSTS = 1312
O_SMALLS = 1376
O_LNST = 1632
O_QGF = 1760
O_QGBD = 1888
O_ALPHT = 2912
O_ALPHF = 4960

O_A = 16128
O_B = 48896
O_C = 81664
O_D = 114432
O_E = 147200
O_F = 179968
O_G = 196352
ARENA = 212736

OC_XT0, OC_XT1 = O_C, O_C + 8192
OC_XNB0, OC_XNB1 = O_C + 16384, O_C + 20480
OC_SC1, OC_SC2, OC_SC3 = O_C, O_C + 4096, O_C + 8192
OC_QB0, OC_QB1 = O_C + 12288, O_C + 13312
OC_WS = O_C + 14848
OD_W, OD_W2, OD_DUMP, OD_SCB = O_D, O_D + 8192, O_D + 16384, O_D + 20480
OE_XR0, OE_XR1 = O_A, O_A + 4096
OE_OS0, OE_OS1 = O_A + 8192, O_A + 12288
OF_COS, OF_SIN = O_F, O_F + 4096
OF_KKF = O_F
OF_AK0, OF_AK1 = O_F, O_F + 4096
OF_XP0, OF_XP1 = O_F, O_F + 2048
O_YSB = O_F


class SemL:
    def __init__(self, ctx, nc, names):
        self.h = {n: ctx.enter_context(nc.semaphore(n)) for n in names}
        self.c = {n: 0 for n in names}

    def bump(self, instr, name, amt=1):
        instr.then_inc(self.h[name], amt)
        self.c[name] += amt
        return self.c[name]

    def wait(self, eng, name, v):
        if v > 0:
            eng.wait_ge(self.h[name], v)


SEMS = [
    "s_x", "s_w", "s_out", "s_bar",
    "p1_st", "p1_mv", "p1_rs", "p1_ln", "p1_tp", "p1_ev",
    "kk_mm", "kk_pf", "kk_m0", "kk_e", "kk_kk",
    "kv_mm", "kv_vev",
    "q_mm", "q_pf", "q_m0", "q_e", "q_qk", "q_qg", "q_tp", "q_ev",
    "b_tpk", "b_tke", "b_bd", "b_lg", "b_mx", "b_ex", "b_al", "b_atp", "b_aev",
    "c_ak", "c_mm", "c_oev",
    "d_phi", "d_xev", "d_res", "d_ctx",
    "f_mm", "f_yev", "f_v", "f_a", "f_bc", "f_bc2",
    "g_si", "wo_mm", "wo_fin", "v_rc",
]


def _build(nph=99):
    nc = bass.Bass("TRN2", target_bir_lowering=False)

    x_d = nc.dram_tensor("x", [S, D], F32, kind="ExternalInput").ap()
    wq_d = nc.dram_tensor("wq", [D, D], F32, kind="ExternalInput").ap()
    wk_d = nc.dram_tensor("wk", [D, 512], F32, kind="ExternalInput").ap()
    wv_d = nc.dram_tensor("wv", [D, 512], F32, kind="ExternalInput").ap()
    wphi_d = nc.dram_tensor("wphi", [D, D], F32, kind="ExternalInput").ap()
    wffn_d = nc.dram_tensor("wffn", [D, D], F32, kind="ExternalInput").ap()
    wo_d = nc.dram_tensor("wo", [D, D], F32, kind="ExternalInput").ap()
    scale_d = nc.dram_tensor("scale", [1, D], F32, kind="ExternalInput").ap()
    cosr_d = nc.dram_tensor("cosr", [P, NT * HD], F32, kind="ExternalInput").ap()
    sinr_d = nc.dram_tensor("sinr", [P, NT * HD], F32, kind="ExternalInput").ap()
    identbf_d = nc.dram_tensor("identbf", [P, P], BF16, kind="ExternalInput").ap()
    identf_d = nc.dram_tensor("identf", [P, P], F32, kind="ExternalInput").ap()
    onesbf_d = nc.dram_tensor("onesbf", [P, 1], BF16, kind="ExternalInput").ap()
    onesf_d = nc.dram_tensor("onesf", [1, P], F32, kind="ExternalInput").ap()
    consts_d = nc.dram_tensor("consts", [P, 8], F32, kind="ExternalInput").ap()
    out_d = nc.dram_tensor("out", [S // 2, D], F32, kind="ExternalOutput").ap()

    wq_v = wq_d.rearrange("(k p) n -> p k n", p=P)
    wk_v = wk_d.rearrange("(k p) n -> p k n", p=P)
    wv_v = wv_d.rearrange("(k p) n -> p k n", p=P)
    wphi_v = wphi_d.rearrange("(k p) n -> p k n", p=P)
    wffn_v = wffn_d.rearrange("(k p) n -> p k n", p=P)
    wo_v = wo_d.rearrange("(k p) n -> p k n", p=P)

    ctx = contextlib.ExitStack()
    with ctx:
        ar = ctx.enter_context(nc.sbuf_tensor("arena", [P, ARENA], mybir.dt.uint8))

        def vw(off, dt, nelem, rows=None):
            nb = nelem * mybir.dt.size(dt)
            a = ar[:, off:off + nb] if rows is None else ar[0:rows, off:off + nb]
            return a.bitcast(dt)

        identbf = vw(O_IDENTBF, BF16, 128)
        identf = vw(O_IDENTF, F32, 128)
        onesbf = vw(O_ONESBF, BF16, 1)
        onesf = vw(O_ONESF, F32, 128, rows=1)
        constsv = vw(O_CONSTS, F32, 8)
        smalls = vw(O_SMALLS, F32, 64)
        lnst = vw(O_LNST, F32, 24)
        QgF = vw(O_QGF, F32, 16)
        qgbdp = vw(O_QGBD, BF16, 512)
        alpha_tm = vw(O_ALPHT, F32, 512)
        alpha_pad = vw(O_ALPHF, F32, 2048)
        alpha_fm = vw(O_ALPHF, F32, 2048, rows=8)
        outer_pad = vw(O_ALPHF, BF16, 2048)

        xnT_own = vw(O_A, BF16, KC * 1024).rearrange("p (k n) -> p k n", n=1024)
        xnT_oth = vw(O_B, BF16, KC * 1024).rearrange("p (k n) -> p k n", n=1024)
        QkF = vw(O_D, BF16, KC * 1024).rearrange("p (c n) -> p c n", n=1024)
        KkT = vw(O_E, BF16, NT * 512).rearrange("p (t n) -> p t n", n=512)
        VT = vw(O_E + 16384, BF16, NT * 512).rearrange("p (t n) -> p t n", n=512)
        KkF = vw(OF_KKF, BF16, 4 * 2048).rearrange("p (c n) -> p c n", n=2048)
        ctxb = vw(O_E, BF16, KC * 1024).rearrange("p (c n) -> p c n", n=1024)
        ysb = vw(O_YSB, BF16, OWN * D).rearrange("p (t n) -> p t n", n=D)

        xt = [vw(OC_XT0, F32, D), vw(OC_XT1, F32, D)]
        xnb = [vw(OC_XNB0, BF16, D), vw(OC_XNB1, BF16, D)]
        sc1 = vw(OC_SC1, F32, 1024)
        sc2 = vw(OC_SC2, F32, 1024)
        sc3 = vw(OC_SC3, F32, 1024)
        qbuf = [vw(OC_QB0, BF16, 512), vw(OC_QB1, BF16, 512)]
        slotG = vw(O_G, BF16, KC * 512).rearrange("p (k n) -> p k n", n=512)
        slotC = vw(OC_WS, BF16, KC * 512).rearrange("p (k n) -> p k n", n=512)
        S0 = vw(O_B, BF16, KC * 512).rearrange("p (k n) -> p k n", n=512)
        S1 = vw(O_B + 16384, BF16, KC * 512).rearrange("p (k n) -> p k n", n=512)
        S2 = vw(O_C, BF16, KC * 512).rearrange("p (k n) -> p k n", n=512)
        S3 = vw(O_C + 16384, BF16, KC * 512).rearrange("p (k n) -> p k n", n=512)
        SS = [S0, S1, S2, S3]
        cosr = vw(OF_COS, F32, NT * HD)
        sinr = vw(OF_SIN, F32, NT * HD)
        akbuf = [vw(OF_AK0, BF16, 2048), vw(OF_AK1, BF16, 2048)]
        xphi = [vw(OF_XP0, BF16, 1024), vw(OF_XP1, BF16, 1024)]
        wbuf = vw(OD_W, F32, D)
        w2buf = vw(OD_W2, F32, D)
        dump = vw(OD_DUMP, BF16, D)
        scale_bc = vw(OD_SCB, F32, D)
        sgbuf = vw(OD_W, F32, 1024)
        xrel = [vw(OE_XR0, F32, 1024), vw(OE_XR1, F32, 1024)]
        outst = [vw(OE_OS0, F32, 1024), vw(OE_OS1, F32, 1024)]

        S_ = SemL(ctx, nc, SEMS)
        block = ctx.enter_context(nc.Block())
        nbar = [0]

        def barrier():
            nbar[0] += 3

            @block.tensor
            def _(t):
                t.drain().then_inc(S_.h["s_bar"], 1)
                t.wait_ge(S_.h["s_bar"], nbar[0])

            @block.vector
            def _(v):
                v.drain().then_inc(S_.h["s_bar"], 1)
                v.wait_ge(S_.h["s_bar"], nbar[0])

            @block.scalar
            def _(s):
                s.drain().then_inc(S_.h["s_bar"], 1)
                s.wait_ge(S_.h["s_bar"], nbar[0])

        def _finish():
            @block.sync
            def _(q):
                S_.wait(q, "s_bar", nbar[0])
                for r in range(8):
                    S_.bump(q.dma_start(
                        out=out_d[r * P:(r + 1) * P, :],
                        in_=ar[:, O_A:O_A + D * 4].bitcast(F32)), "s_out", 16)
                q.wait_ge(S_.h["s_out"], S_.c["s_out"])

        def xnT_k_t(k, t):
            if t < OWN:
                return xnT_own[:, k, t * P:(t + 1) * P]
            return xnT_oth[:, k, (t - OWN) * P:(t - OWN) * P + P]

        def rope_kappa_front(v, ps, t, pf_sem):
            ps3 = ps.ap()[:, 0:512].rearrange("p (h d) -> p h d", d=HD)
            cosb = cosr[:, t * HD:(t + 1) * HD].unsqueeze(1) \
                .broadcast_to([P, 8, HD])
            sin_a = sinr[:, t * HD:t * HD + 32].unsqueeze(1) \
                .broadcast_to([P, 8, 32])
            sin_b = sinr[:, t * HD + 32:t * HD + 64].unsqueeze(1) \
                .broadcast_to([P, 8, 32])
            s1 = sc1[:, 0:512].rearrange("p (h d) -> p h d", d=HD)
            s2 = sc2[:, 0:512].rearrange("p (h d) -> p h d", d=HD)
            v.tensor_tensor(s1, ps3, cosb, ALU.mult)
            v.tensor_tensor(s2[:, :, 0:32], ps3[:, :, 32:64], sin_a, ALU.mult)
            S_.bump(v.tensor_tensor(s2[:, :, 32:64], ps3[:, :, 0:32], sin_b,
                                    ALU.mult), pf_sem)
            v.tensor_tensor(sc1[:, 0:512], sc1[:, 0:512], sc2[:, 0:512], ALU.add)

        psA = contextlib.ExitStack()
        tpb = [psA.enter_context(nc.psum_tensor(f"tpb{i}", [P, 1024], BF16))
               for i in range(4)]
        psqg = psA.enter_context(nc.psum_tensor("psqg", [P, 16], F32))
        pp = [psA.enter_context(nc.psum_tensor(f"pp{i}", [P, 512], F32))
              for i in range(3)]

        @block.sync
        def _(q):
            S_.bump(q.dma_start(out=identbf, in_=identbf_d), "s_x", 16)
            S_.bump(q.dma_start(out=identf, in_=identf_d), "s_x", 16)
            S_.bump(q.dma_start(out=onesbf, in_=onesbf_d), "s_x", 16)
            S_.bump(q.dma_start(out=onesf, in_=onesf_d), "s_x", 16)
            S_.bump(q.dma_start(out=constsv, in_=consts_d), "s_x", 16)
            S_.bump(q.dma_start(out=cosr, in_=cosr_d), "s_x", 16)
            S_.bump(q.dma_start(out=sinr, in_=sinr_d), "s_x", 16)
            for t in range(NT):
                if t >= 2:
                    S_.wait(q, "p1_ln", t - 1)
                S_.bump(q.dma_start(out=xt[t % 2],
                                    in_=x_d[t * P:(t + 1) * P, :]), "s_x", 16)
        XBASE = 7 * 16

        @block.gpsimd
        def _(g):
            if nph >= 2:
                S_.bump(g.dma_start(out=slotG, in_=wk_v), "s_w", 16)
                S_.wait(g, "p1_tp", 2 * NT)
                S_.bump(g.dma_start(out=slotC, in_=wv_v), "s_w", 16)
            if nph >= 3:
                S_.wait(g, "kk_mm", NT)
                S_.bump(g.dma_start(out=slotG, in_=wq_v[:, :, 0:512]), "s_w", 16)
                S_.wait(g, "kv_mm", NT)
                S_.bump(g.dma_start(out=slotC, in_=wq_v[:, :, 512:1024]),
                        "s_w", 16)
                S_.wait(g, "q_mm", NT)
                S_.bump(g.dma_start(out=slotG, in_=wq_v[:, :, 1024:1536]),
                        "s_w", 16)
                S_.wait(g, "q_mm", 2 * NT)
                S_.bump(g.dma_start(out=slotC, in_=wq_v[:, :, 1536:2048]),
                        "s_w", 16)
            if nph >= 6:
                S_.wait(g, "q_mm", 4 * NT)
                S_.wait(g, "q_qg", 4 * NT)
                S_.wait(g, "q_qk", 4 * NT)
                for s in range(4):
                    S_.bump(g.dma_start(out=SS[s],
                                        in_=wphi_v[:, :, 512 * s:512 * s + 512]),
                            "s_w", 16)
            if nph >= 7:
                for s in range(4):
                    S_.wait(g, "d_phi", 4 * (s + 1))
                    S_.bump(g.dma_start(out=SS[s],
                                        in_=wffn_v[:, :, 512 * s:512 * s + 512]),
                            "s_w", 16)
            if nph >= 8:
                for s in range(4):
                    S_.wait(g, "f_mm", 8 * (s // 2 + 1))
                    S_.bump(g.dma_start(out=SS[s],
                                        in_=wo_v[:, :, 512 * s:512 * s + 512]),
                            "s_w", 16)

        def _ln_apply(v, t):
            par = t % 2
            S_.wait(v, "p1_rs", t + 1)
            v.tensor_scalar(xnb[t % 2], xt[t % 2],
                            smalls[:, 4 * par:4 * par + 1],
                            smalls[:, 4 * par + 2:4 * par + 3],
                            ALU.subtract, ALU.mult)
            S_.bump(v.drain(), "p1_ln")
            for gidx in range(2):
                S_.wait(v, "p1_tp", 2 * t + gidx + 1)
                dst3 = xnT_own if t < OWN else xnT_oth
                tl = t if t < OWN else t - OWN
                dst = dst3[:, 8 * gidx:8 * gidx + 8, tl * P:tl * P + P]
                srcp = tpb[(t % 2) * 2 + gidx].ap() \
                    .rearrange("p (c n) -> p c n", n=P)
                S_.bump(v.tensor_copy(dst, srcp), "p1_ev")

        @block.vector
        def _(v):
            for t in range(NT):
                S_.wait(v, "s_x", XBASE + 16 * (t + 1))
                if t >= 2:
                    S_.wait(v, "p1_tp", 2 * (t - 2) + 2)
                par = t % 2
                lslot = lnst[:, 0:24] if par == 0 else smalls[:, 32:56]
                xb = xt[t % 2]
                for j in range(4):
                    ins = v.bn_stats(lslot[:, 6 * j:6 * j + 6],
                                     xb[:, 512 * j:512 * j + 512])
                S_.bump(ins, "p1_st")
                S_.wait(v, "p1_st", t + 1)
                v.bn_aggr(smalls[:, 4 * par:4 * par + 2], lslot)
                S_.bump(v.drain(), "p1_mv")
                if t >= 1:
                    _ln_apply(v, t - 1)
            _ln_apply(v, NT - 1)

        @block.scalar
        def _(s):
            for t in range(NT):
                par = t % 2
                S_.wait(s, "p1_mv", t + 1)
                s.activation(smalls[:, 3:4], smalls[:, 4 * par + 1:4 * par + 2],
                             AF.Ln, bias=constsv[:, 0:1])
                s.activation(smalls[:, 4 * par + 2:4 * par + 3],
                             smalls[:, 3:4], AF.Exp, scale=-0.5)
                S_.bump(s.drain(), "p1_rs")

        @block.tensor
        def _(t):
            for tt in range(NT):
                S_.wait(t, "p1_ln", tt + 1)
                for gidx in range(2):
                    S_.wait(t, "p1_ev", max(0, 2 * (tt - 2) + gidx + 2))
                    for c in range(8):
                        cc = 8 * gidx + c
                        ins = t.transpose(
                            tpb[(tt % 2) * 2 + gidx][:, c * P:(c + 1) * P],
                            xnb[tt % 2][:, cc * P:(cc + 1) * P], identbf)
                    S_.bump(ins, "p1_tp")

        barrier()
        if nph < 2:
            _finish()
            return nc

        @block.tensor
        def _(t):
            t.wait_ge(S_.h["s_w"], 16)
            for tt in range(NT):
                if tt >= 2:
                    S_.wait(t, "kk_pf", tt - 1)
                for k in range(KC):
                    ins = t.matmul(pp[tt % 3][:], xnT_k_t(k, tt),
                                   slotG[:, k, :], start=(k == 0),
                                   stop=(k == KC - 1))
                S_.bump(ins, "kk_mm")

        @block.vector
        def _(v):
            for tt in range(NT):
                S_.wait(v, "kk_mm", tt + 1)
                rope_kappa_front(v, pp[tt % 3], tt, "kk_pf")
                v.tensor_scalar(sc2[:, 0:512], sc1[:, 0:512], 0.0, None,
                                ALU.min)
                S_.bump(v.drain(), "kk_m0")
                S_.wait(v, "kk_e", tt + 1)
                v.scalar_tensor_tensor(KkT[:, tt, :], sc1[:, 0:512], 0.0,
                                       sc3[:, 0:512], ALU.max, ALU.add)
                S_.bump(v.drain(), "kk_kk")

        @block.scalar
        def _(s):
            for tt in range(NT):
                S_.wait(s, "kk_m0", tt + 1)
                if tt >= 1:
                    S_.wait(s, "kk_kk", tt)
                s.activation(sc3[:, 0:512], sc2[:, 0:512], AF.Exp)
                S_.bump(s.drain(), "kk_e")

        @block.tensor
        def _(t):
            t.wait_ge(S_.h["s_w"], 32)
            S_.wait(t, "kk_pf", NT)
            S_.wait(t, "kk_kk", NT)
            for tt in range(NT):
                if tt >= 2:
                    S_.wait(t, "kv_vev", tt - 1)
                for k in range(KC):
                    ins = t.matmul(pp[tt % 3][:], xnT_k_t(k, tt),
                                   slotC[:, k, :], start=(k == 0),
                                   stop=(k == KC - 1))
                S_.bump(ins, "kv_mm")

        @block.scalar
        def _(s):
            for tt in range(NT):
                S_.wait(s, "kv_mm", tt + 1)
                S_.bump(s.copy(VT[:, tt, :], pp[tt % 3][:]), "kv_vev")

        barrier()
        if nph < 3:
            _finish()
            return nc

        QTH = [48, 64, 80, 96]
        QSL = [slotG, slotC, slotG, slotC]

        def _q_back(t, i, nown):
            Q, tt = divmod(i, NT)
            S_.wait(t, "q_qk", i + 1)
            ins = None
            for c in range(4):
                col = 4 * Q + c
                ins = t.matmul(psqg[:, col:col + 1],
                               qbuf[i % 2][:, c * P:(c + 1) * P], onesbf,
                               start=(tt == 0 and c == 0),
                               stop=(tt == NT - 1 and c == 3),
                               skip_group_check=True)
            if tt < OWN:
                S_.wait(t, "q_ev", max(0, nown - 1))
                for c in range(4):
                    ins = t.transpose(tpb[nown % 4][:, c * P:(c + 1) * P],
                                      qbuf[i % 2][:, c * P:(c + 1) * P],
                                      identbf)
                S_.bump(ins, "q_tp")
                nown += 1
                S_.bump(t.drain(), "q_qg")
            else:
                S_.bump(ins, "q_qg")
            return nown

        @block.tensor
        def _(t):
            nown = 0
            for Q in range(4):
                t.wait_ge(S_.h["s_w"], QTH[Q])
                for tt in range(NT):
                    i = Q * NT + tt
                    if i < 2:
                        S_.wait(t, "kv_vev", 14 + i)
                    else:
                        S_.wait(t, "q_pf", i - 1)
                    for k in range(KC):
                        ins = t.matmul(pp[(i + 1) % 3][:], xnT_k_t(k, tt),
                                       QSL[Q][:, k, :], start=(k == 0),
                                       stop=(k == KC - 1))
                    S_.bump(ins, "q_mm")
                    if i >= 1:
                        nown = _q_back(t, i - 1, nown)
            _q_back(t, 4 * NT - 1, nown)

        @block.vector
        def _(v):
            nown = 0
            for Q in range(4):
                for tt in range(NT):
                    i = Q * NT + tt
                    S_.wait(v, "q_mm", i + 1)
                    rope_kappa_front(v, pp[(i + 1) % 3], tt, "q_pf")
                    v.tensor_scalar(sc2[:, 0:512], sc1[:, 0:512], 0.0,
                                    None, ALU.min)
                    S_.bump(v.drain(), "q_m0")
                    S_.wait(v, "q_e", i + 1)
                    S_.wait(v, "q_qg", max(0, i - 1))
                    v.scalar_tensor_tensor(qbuf[i % 2], sc1[:, 0:512],
                                           0.0, sc3[:, 0:512], ALU.max,
                                           ALU.add)
                    S_.bump(v.drain(), "q_qk")
                    if tt < OWN:
                        S_.wait(v, "q_tp", nown + 1)
                        dst = QkF[:, 4 * Q:4 * Q + 4, tt * P:tt * P + P]
                        src = tpb[nown % 4].ap()[:, 0:512] \
                            .rearrange("p (c n) -> p c n", n=P)
                        S_.bump(v.tensor_copy(dst, src), "q_ev")
                        nown += 1

        @block.scalar
        def _(s):
            for i in range(4 * NT):
                S_.wait(s, "q_m0", i + 1)
                S_.wait(s, "q_qk", i)
                s.activation(sc3[:, 0:512], sc2[:, 0:512], AF.Exp)
                S_.bump(s.drain(), "q_e")
            s.wait_ge(S_.h["q_qg"], 4 * NT)
            s.copy(QgF, psqg[:])
            s.mul(QgF, QgF, 1.0 / float(S))

        barrier()
        if nph < 4:
            _finish()
            return nc
        psA.pop_all().close()

        psB = contextlib.ExitStack()
        pslg = psB.enter_context(nc.psum_tensor("pslg", [P, 2048], F32))
        psal = [psB.enter_context(nc.psum_tensor(f"psal{i}", [P, 512], F32))
                for i in range(2)]
        tpk = psB.enter_context(nc.psum_tensor("tpk", [P, 2048], BF16))

        def _a_tp(t, g, jb):
            m = 4 * g + jb
            S_.wait(t, "b_al", g + 1)
            S_.wait(t, "b_aev", m)
            ins = None
            for i in range(4):
                tt = 4 * jb + i
                ins = t.transpose(psal[m % 2][:, i * P:(i + 1) * P],
                                  alpha_pad[:, tt * P:(tt + 1) * P], identf)
            S_.bump(ins, "b_atp")

        @block.tensor
        def _(t):
            for j in range(8):
                S_.wait(t, "b_tke", j)
                base = 1024 * (j % 2)
                for half in range(2):
                    tt = 2 * j + half
                    for c in range(4):
                        off = base + half * 512 + c * P
                        ins = t.transpose(tpk[:, off:off + P],
                                          KkT[:, tt, c * P:(c + 1) * P],
                                          identbf)
                S_.bump(ins, "b_tpk")
            S_.wait(t, "b_bd", 1)
            S_.wait(t, "b_tke", 8)
            for g in range(4):
                S_.wait(t, "b_ex", g)
                for st in range(4):
                    ins = t.matmul(pslg[:, 512 * st:512 * st + 512],
                                   qgbdp[:, 128 * g:128 * g + 128],
                                   KkF[:, g, 512 * st:512 * st + 512],
                                   start=True, stop=True)
                S_.bump(ins, "b_lg")
                if g >= 1:
                    for jb in range(4):
                        _a_tp(t, g - 1, jb)
            for jb in range(4):
                _a_tp(t, 3, jb)

        @block.vector
        def _(v):
            for j in range(8):
                S_.wait(v, "b_tpk", j + 1)
                base = 1024 * (j % 2)
                for half in range(2):
                    tt = 2 * j + half
                    dst = KkF[:, 0:4, tt * P:tt * P + P]
                    srcv = tpk.ap()[:, base + half * 512:base + half * 512 + 512] \
                        .rearrange("p (c n) -> p c n", n=P)
                    v.tensor_copy(dst, srcv)
                S_.bump(v.drain(), "b_tke")
            v.memset(qgbdp, 0.0)
            v.memset(alpha_pad, 0.0)
            for g in range(4):
                for j in range(8):
                    h = 8 * g + j
                    par = h % 2
                    srcq = QgF[64 * par:64 * par + 64, h // 2:h // 2 + 1]
                    dst = qgbdp[64 * (j // 4):64 * (j // 4) + 64,
                                128 * g + j:128 * g + j + 1]
                    v.tensor_copy(dst, srcq)
            S_.bump(v.drain(), "b_bd")
            for g in range(4):
                S_.wait(v, "b_lg", g + 1)
                v.reduce_max(smalls[0:8, 8:9], pslg[0:8, :], axis=AX.X,
                             negate=True)
                S_.bump(v.drain(), "b_mx")
                S_.wait(v, "b_ex", g + 1)
                S_.bump(v.reciprocal(smalls[0:8, 10:11], smalls[0:8, 9:10]),
                        "v_rc")
                S_.wait(v, "v_rc", S_.c["v_rc"])
                v.tensor_scalar(alpha_fm, alpha_fm, smalls[0:8, 10:11],
                                float(S), ALU.mult, ALU.mult)
                S_.bump(v.drain(), "b_al")
                for jb in range(4):
                    m = 4 * g + jb
                    S_.wait(v, "b_atp", m + 1)
                    srca = psal[m % 2].ap() \
                        .rearrange("p (i n) -> p i n", n=P)[:, :, 0:8]
                    dstv = alpha_tm.rearrange("p (t c) -> p t c", c=32)[
                        :, 4 * jb:4 * jb + 4, 8 * g:8 * g + 8]
                    S_.bump(v.tensor_copy(dstv, srca), "b_aev")

        @block.scalar
        def _(s):
            for g in range(4):
                S_.wait(s, "b_mx", g + 1)
                if g >= 1:
                    S_.wait(s, "b_al", g)
                    S_.wait(s, "b_atp", 4 * g)
                s.activation(alpha_fm, pslg[0:8, :], AF.Exp,
                             bias=smalls[0:8, 8:9],
                             accum_out=smalls[0:8, 9:10])
                S_.bump(s.drain(), "b_ex")

        barrier()
        if nph < 5:
            _finish()
            return nc
        psB.pop_all().close()

        psC = contextlib.ExitStack()
        psot = psC.enter_context(nc.psum_tensor("psot", [P, 1024], F32))

        @block.vector
        def _(v):
            v.memset(outer_pad, 0.0)
            for tt in range(NT):
                if tt >= 2:
                    S_.wait(v, "c_mm", tt - 1)
                kk = KkT[:, tt, :].rearrange("p (h d) -> p h d", d=HD) \
                    .unsqueeze(2).broadcast_to([P, 8, 4, HD])
                al = alpha_tm[:, tt * 32:(tt + 1) * 32] \
                    .rearrange("p (h j) -> p h j", j=4) \
                    .unsqueeze(3).broadcast_to([P, 8, 4, HD])
                dst = akbuf[tt % 2].rearrange("p (h j d) -> p h j d", j=4, d=HD)
                v.tensor_tensor(dst, kk, al, ALU.mult)
                S_.bump(v.drain(), "c_ak")

        @block.tensor
        def _(t):
            for tt in range(NT):
                S_.wait(t, "c_ak", tt + 1)
                for j in range(16):
                    ins = t.matmul(
                        psot[:, 64 * j:64 * j + 64],
                        akbuf[tt % 2][:, 128 * j:128 * j + 128],
                        VT[:, tt, 64 * (j // 2):64 * (j // 2) + 64],
                        start=(tt == 0 and j in (0, 8)),
                        stop=(tt == NT - 1), skip_group_check=True)
                S_.bump(ins, "c_mm")

        @block.scalar
        def _(s):
            S_.wait(s, "c_mm", NT)
            ope = outer_pad[0:64, :].rearrange("p (h e d) -> p h e d",
                                               e=2, d=HD)[:, :, 0, :]
            opo = outer_pad[64:128, :].rearrange("p (h e d) -> p h e d",
                                                 e=2, d=HD)[:, :, 1, :]
            s.copy(ope, psot[0:64, :].rearrange("p (h d) -> p h d", d=HD))
            S_.bump(s.copy(opo, psot[64:128, :]
                           .rearrange("p (h d) -> p h d", d=HD)), "c_oev")

        barrier()
        if nph < 6:
            _finish()
            return nc
        psC.pop_all().close()

        psD = contextlib.ExitStack()
        psph = [psD.enter_context(nc.psum_tensor(f"psph{i}", [P, 1024], F32))
                for i in range(2)]
        psrs = [psD.enter_context(nc.psum_tensor(f"psrs{i}", [P, 1024], F32))
                for i in range(2)]

        def _res_mm(t, m):
            S_.wait(t, "d_ctx", max(0, m - 1))
            S_.wait(t, "d_xev", m + 1)
            ins = None
            for st in range(2):
                ins = t.matmul(psrs[m % 2][:, 512 * st:512 * st + 512],
                               outer_pad[:, 128 * m:128 * m + 128],
                               QkF[:, m, 512 * st:512 * st + 512],
                               start=True, stop=True)
            S_.bump(ins, "d_res")

        @block.tensor
        def _(t):
            for m in range(KC):
                S_.wait(t, "s_w", 112 + 16 * (m // 4))
                S_.wait(t, "d_xev", max(0, m - 1))
                for k in range(KC):
                    for st in range(2):
                        ins = t.matmul(psph[m % 2][:, 512 * st:512 * st + 512],
                                       SS[m // 4][:, k,
                                                  128 * (m % 4):128 * (m % 4) + 128],
                                       xnT_own[:, k, 512 * st:512 * st + 512],
                                       start=(k == 0), stop=(k == KC - 1))
                S_.bump(ins, "d_phi")
                if m >= 1:
                    _res_mm(t, m - 1)
            _res_mm(t, KC - 1)

        @block.scalar
        def _(s):
            S_.wait(s, "c_oev", 1)
            for m in range(KC):
                S_.wait(s, "d_phi", m + 1)
                if m >= 2:
                    S_.wait(s, "d_ctx", m - 1)
                s.copy(xphi[m % 2], psph[m % 2][:])
                S_.bump(s.drain(), "d_xev")

        @block.vector
        def _(v):
            for m in range(KC):
                S_.wait(v, "d_res", m + 1)
                v.tensor_tensor(ctxb[:, m, :], psrs[m % 2][:],
                                xphi[m % 2], ALU.mult)
                S_.bump(v.drain(), "d_ctx")

        barrier()
        if nph < 7:
            _finish()
            return nc
        psD.pop_all().close()

        psE = contextlib.ExitStack()
        psf = [psE.enter_context(nc.psum_tensor(f"psf{i}", [P, 1024], F32))
               for i in range(3)]
        psbc = [psE.enter_context(nc.psum_tensor(f"psbc{i}", [P, 512], F32))
                for i in range(2)]

        @block.sync
        def _(q):
            S_.wait(q, "d_ctx", KC)
            S_.bump(q.dma_start(out=scale_bc[0:1, :], in_=scale_d), "s_x", 16)
        SCROW = S_.c["s_x"]

        @block.vector
        def _(v):
            v.memset(scale_bc, 0.0)
            v.memset(wbuf[:, 0:128], 1.0)
            S_.bump(v.drain(), "f_bc2")
            for st in range(4):
                S_.wait(v, "f_bc", st + 1)
                v.tensor_copy(scale_bc[:, 512 * st:512 * st + 512],
                              psbc[st % 2][:])
                S_.bump(v.drain(), "f_bc2")
            for tt in range(OWN):
                S_.wait(v, "f_yev", OWN + tt + 1)
                v.tensor_tensor(wbuf, ysb[:, tt, :], scale_bc, ALU.mult)
                S_.bump(v.drain(), "f_v")
                S_.wait(v, "f_a", 4 * tt + 1)
                v.tensor_scalar(smalls[:, 17:18], smalls[:, 16:17],
                                1e-14, None, ALU.max)
                S_.bump(v.drain(), "f_v")
                S_.wait(v, "f_a", 4 * tt + 2)
                v.tensor_scalar(smalls[:, 20:21], smalls[:, 19:20], -1.0, 1.0,
                                ALU.mult, ALU.add)
                v.tensor_scalar(smalls[:, 21:22], smalls[:, 19:20], 1.0, None,
                                ALU.add)
                S_.bump(v.tensor_scalar(smalls[:, 22:23], smalls[:, 18:19],
                                        1e-7, None, ALU.add), "v_rc")
                S_.wait(v, "v_rc", S_.c["v_rc"])
                S_.bump(v.tensor_tensor(smalls[:, 23:24], smalls[:, 21:22],
                                        smalls[:, 22:23], ALU.mult), "v_rc")
                S_.wait(v, "v_rc", S_.c["v_rc"])
                S_.bump(v.reciprocal(smalls[:, 24:25], smalls[:, 23:24]),
                        "v_rc")
                S_.wait(v, "v_rc", S_.c["v_rc"])
                S_.bump(v.tensor_tensor(smalls[:, 25:26], smalls[:, 20:21],
                                        smalls[:, 24:25], ALU.mult), "v_rc")
                S_.wait(v, "v_rc", S_.c["v_rc"])
                v.tensor_scalar(w2buf, wbuf, smalls[:, 25:26], None,
                                ALU.mult)
                S_.bump(v.drain(), "f_v")
                S_.wait(v, "f_a", 4 * tt + 3)
                v.tensor_scalar(smalls[:, 27:28], smalls[:, 26:27],
                                1e-14, None, ALU.max)
                S_.bump(v.drain(), "f_v")
                S_.wait(v, "f_a", 4 * tt + 4)
                S_.bump(v.reciprocal(smalls[:, 29:30], smalls[:, 28:29]),
                        "v_rc")
                S_.wait(v, "v_rc", S_.c["v_rc"])
                S_.bump(v.tensor_scalar(smalls[:, 30:31], smalls[:, 29:30],
                                        0.99, 1.0, ALU.mult, ALU.min), "v_rc")
                S_.wait(v, "v_rc", S_.c["v_rc"])
                v.tensor_scalar(ysb[:, tt, :], w2buf, smalls[:, 30:31],
                                None, ALU.mult)
                S_.bump(v.drain(), "f_v")

        @block.tensor
        def _(t):
            t.wait_ge(S_.h["s_x"], SCROW)
            for st in range(4):
                S_.wait(t, "f_bc2", st + 1)
                S_.bump(t.matmul(psbc[st % 2][:], wbuf[:, 0:128],
                                 scale_bc[:, 512 * st:512 * st + 512],
                                 start=True, stop=True), "f_bc")
            for half in range(2):
                t.wait_ge(S_.h["s_w"], 192 + 32 * half)
                t.wait_ge(S_.h["f_bc2"], 5)
                for tt in range(OWN):
                    i = half * OWN + tt
                    if i >= 2:
                        S_.wait(t, "f_yev", i - 1)
                    for k in range(KC):
                        for st in range(2):
                            sl = SS[2 * half + st]
                            ins = t.matmul(psf[i % 3][:, 512 * st:512 * st + 512],
                                           xnT_own[:, k, tt * P:(tt + 1) * P],
                                           sl[:, k, :], start=(k == 0),
                                           stop=(k == KC - 1))
                    S_.bump(ins, "f_mm")

        def _epi_act(s, tt):
            S_.wait(s, "f_v", 5 * tt + 1)
            s.activation(dump, wbuf, AF.Square,
                         accum_out=smalls[:, 16:17])
            S_.bump(s.drain(), "f_a")
            S_.wait(s, "f_v", 5 * tt + 2)
            s.activation(smalls[:, 31:32], smalls[:, 17:18], AF.Ln)
            s.activation(smalls[:, 18:19], smalls[:, 31:32], AF.Exp,
                         scale=0.5)
            s.activation(smalls[:, 19:20], smalls[:, 18:19], AF.Exp,
                         scale=-2.0)
            S_.bump(s.drain(), "f_a")
            S_.wait(s, "f_v", 5 * tt + 3)
            s.activation(dump, w2buf, AF.Square,
                         accum_out=smalls[:, 26:27])
            S_.bump(s.drain(), "f_a")
            S_.wait(s, "f_v", 5 * tt + 4)
            s.activation(smalls[:, 31:32], smalls[:, 27:28], AF.Ln)
            s.activation(smalls[:, 28:29], smalls[:, 31:32], AF.Exp,
                         scale=0.5)
            S_.bump(s.drain(), "f_a")
            S_.wait(s, "f_v", 5 * tt + 5)
            S_.bump(s.activation(ysb[:, tt, :], ysb[:, tt, :], AF.Silu),
                    "g_si")

        @block.scalar
        def _(s):
            for tt in range(OWN):
                S_.wait(s, "f_mm", tt + 1)
                S_.bump(s.copy(ysb[:, tt, 0:1024], psf[tt % 3][:]), "f_yev")
            for tt in range(OWN):
                i = OWN + tt
                S_.wait(s, "f_mm", i + 1)
                s.copy(ysb[:, tt, 1024:2048], psf[i % 3][:])
                S_.bump(s.drain(), "f_yev")
                if tt >= 1:
                    _epi_act(s, tt - 1)
            _epi_act(s, OWN - 1)

        barrier()
        if nph < 8:
            _finish()
            return nc
        psE.pop_all().close()

        psF = contextlib.ExitStack()
        pswo = [psF.enter_context(nc.psum_tensor(f"pswo{i}", [P, 1024], F32))
                for i in range(4)]

        @block.tensor
        def _(t):
            for wh in range(2):
                t.wait_ge(S_.h["s_w"], 256 + 32 * wh)
                for tt in range(OWN):
                    i = wh * OWN + tt
                    if i >= 4:
                        S_.wait(t, "wo_fin", i - 3)
                    for m in range(KC):
                        for st in range(2):
                            sl = SS[2 * wh + st]
                            ins = t.matmul(pswo[i % 4][:, 512 * st:512 * st + 512],
                                           ctxb[:, m, tt * P:(tt + 1) * P],
                                           sl[:, m, :], start=(m == 0),
                                           stop=(m == KC - 1))
                    S_.bump(ins, "wo_mm")

        @block.sync
        def _(q):
            S_.wait(q, "f_mm", 2 * OWN)
            for i in range(2 * OWN):
                wh, tt = divmod(i, OWN)
                if i >= 2:
                    S_.wait(q, "wo_fin", i - 1)
                S_.bump(q.dma_start(
                    out=xrel[i % 2],
                    in_=x_d[tt * P:(tt + 1) * P,
                            wh * 1024:wh * 1024 + 1024]), "s_x", 16)
                if i >= 1:
                    S_.wait(q, "wo_fin", i)
                    wh2, tt2 = divmod(i - 1, OWN)
                    S_.bump(q.dma_start(
                        out=out_d[tt2 * P:(tt2 + 1) * P,
                                  wh2 * 1024:wh2 * 1024 + 1024],
                        in_=outst[(i - 1) % 2]), "s_out", 16)
            S_.wait(q, "wo_fin", 2 * OWN)
            S_.bump(q.dma_start(
                out=out_d[(OWN - 1) * P:OWN * P, 1024:2048],
                in_=outst[(2 * OWN - 1) % 2]), "s_out", 16)
            q.wait_ge(S_.h["s_out"], 16 * 2 * OWN)

        XRBASE = SCROW

        @block.vector
        def _(v):
            for i in range(2 * OWN):
                wh, tt = divmod(i, OWN)
                S_.wait(v, "g_si", tt + 1)
                S_.wait(v, "wo_mm", i + 1)
                S_.wait(v, "s_x", XRBASE + 16 * (i + 1))
                if i >= 2:
                    S_.wait(v, "s_out", 16 * (i - 1))
                v.tensor_tensor(sgbuf, pswo[i % 4][:],
                                ysb[:, tt, wh * 1024:wh * 1024 + 1024],
                                ALU.mult)
                v.tensor_tensor(outst[i % 2], sgbuf, xrel[i % 2], ALU.add)
                S_.bump(v.drain(), "wo_fin")

        psF.pop_all().close()

    return nc


def kernel(L0_attn_hidden_states, L0_attn_Wq, L0_attn_Wk, L0_attn_Wv,
           L0_attn_Wphi, L0_attn_Wo, L0_ffn_weight, L0_ffn_scale):
    x = np.ascontiguousarray(np.asarray(L0_attn_hidden_states, dtype=np.float32))
    wq = np.ascontiguousarray(np.asarray(L0_attn_Wq, dtype=np.float32))
    wk = np.ascontiguousarray(np.asarray(L0_attn_Wk, dtype=np.float32))
    wv = np.ascontiguousarray(np.asarray(L0_attn_Wv, dtype=np.float32))
    wphi = np.ascontiguousarray(np.asarray(L0_attn_Wphi, dtype=np.float32))
    wo = np.ascontiguousarray(np.asarray(L0_attn_Wo, dtype=np.float32))
    wffn = np.ascontiguousarray(np.asarray(L0_ffn_weight, dtype=np.float32))
    scale = np.ascontiguousarray(
        np.asarray(L0_ffn_scale, dtype=np.float32)).reshape(1, D)

    identbf = np.eye(P, dtype=ml_dtypes.bfloat16)
    identf = np.eye(P, dtype=np.float32)
    onesbf = np.ones((P, 1), dtype=ml_dtypes.bfloat16)
    onesf = np.ones((1, P), dtype=np.float32)
    consts = np.zeros((P, 8), dtype=np.float32)
    consts[:, 0] = 1e-5

    inv = 1.0 / (10000.0 ** (np.arange(0, HD, 2, dtype=np.float64) / HD))
    tables = []
    for h in (0, 1):
        pos = (np.arange(S) + 1024 * h) % S
        ang = pos[:, None].astype(np.float64) * inv[None, :]
        cos = np.concatenate([np.cos(ang), np.cos(ang)], axis=1)
        sinp = np.concatenate([-np.sin(ang), np.sin(ang)], axis=1)
        cosr = cos.reshape(NT, P, HD).transpose(1, 0, 2).reshape(P, NT * HD)
        sinr = sinp.reshape(NT, P, HD).transpose(1, 0, 2).reshape(P, NT * HD)
        tables.append((np.ascontiguousarray(cosr.astype(np.float32)),
                       np.ascontiguousarray(sinr.astype(np.float32))))

    nc = _build()
    in_maps = []
    for c in range(8):
        b, h = divmod(c, 2)
        xc = np.ascontiguousarray(np.roll(x[b], -1024 * h, axis=0))
        in_maps.append({
            "x": xc, "wq": wq, "wk": wk, "wv": wv, "wphi": wphi,
            "wffn": wffn, "wo": wo, "scale": scale,
            "cosr": tables[h][0], "sinr": tables[h][1],
            "identbf": identbf, "identf": identf, "onesbf": onesbf,
            "onesf": onesf, "consts": consts,
        })
    res = run_bass_kernel_spmd(nc, in_maps, core_ids=list(range(8)))
    out = np.empty((4, S, D), dtype=np.float32)
    for c in range(8):
        b, h = divmod(c, 2)
        out[b, 1024 * h:1024 * h + 1024, :] = res.results[c]["out"]
    return out


# version 10
# speedup vs baseline: 1.0505x; 1.0505x over previous
import contextlib

import numpy as np
import ml_dtypes

import concourse.bass as bass
import concourse.mybir as mybir
from concourse.bass_utils import run_bass_kernel_spmd

F32 = mybir.dt.float32
BF16 = mybir.dt.bfloat16
AF = mybir.ActivationFunctionType
ALU = mybir.AluOpType
AX = mybir.AxisListType

P = 128
S = 2048
D = 2048
NT = 16
OWN = 8
HD = 64
KC = 16

O_IDENTBF = 0
O_IDENTF = 256
O_ONESBF = 768
O_ONESF = 800
O_CONSTS = 1312
O_SMALLS = 1376
O_LNST = 1632
O_QGF = 1760
O_QGBD = 1888
O_ALPHT = 2912
O_ALPHF = 4960

O_A = 16128
O_B = 48896
O_C = 81664
O_D = 114432
O_E = 147200
O_F = 179968
O_G = 196352
ARENA = 212736

OC_XT0, OC_XT1 = O_C, O_C + 8192
OC_XNB0, OC_XNB1 = O_C + 16384, O_C + 20480
OC_SC1, OC_SC2, OC_SC3 = O_C, O_C + 4096, O_C + 8192
OC_QB0, OC_QB1 = O_C + 12288, O_C + 13312
OC_WS = O_C + 14848
OD_W, OD_W2, OD_DUMP, OD_SCB = O_D, O_D + 8192, O_D + 16384, O_D + 20480
OE_XR0, OE_XR1 = O_A, O_A + 4096
OE_OS0, OE_OS1 = O_A + 8192, O_A + 12288
OF_COS, OF_SIN = O_F, O_F + 4096
OF_KKF = O_F
OF_AK0, OF_AK1 = O_F, O_F + 4096
OF_XP0, OF_XP1 = O_F, O_F + 2048
O_YSB = O_F


class SemL:
    def __init__(self, ctx, nc, names):
        self.h = {n: ctx.enter_context(nc.semaphore(n)) for n in names}
        self.c = {n: 0 for n in names}

    def bump(self, instr, name, amt=1):
        instr.then_inc(self.h[name], amt)
        self.c[name] += amt
        return self.c[name]

    def wait(self, eng, name, v):
        if v > 0:
            eng.wait_ge(self.h[name], v)


SEMS = [
    "s_x", "s_w", "s_out", "s_bar",
    "p1_st", "p1_mv", "p1_rs", "p1_ln", "p1_tp", "p1_ev",
    "kk_mm", "kk_pf", "kk_m0", "kk_e", "kk_kk",
    "kv_mm", "kv_vev",
    "q_mm", "q_pf", "q_m0", "q_e", "q_qk", "q_qg", "q_tp", "q_ev",
    "b_tpk", "b_tke", "b_bd", "b_lg", "b_mx", "b_ex", "b_al", "b_atp", "b_aev",
    "c_ak", "c_mm", "c_oev",
    "d_phi", "d_xev", "d_res", "d_ctx",
    "f_mm", "f_yev", "f_v", "f_a", "f_bc", "f_bc2",
    "g_si", "wo_mm", "wo_fin", "v_rc",
]


def _build(nph=99):
    nc = bass.Bass("TRN2", target_bir_lowering=False)

    x_d = nc.dram_tensor("x", [S, D], F32, kind="ExternalInput").ap()
    wq_d = nc.dram_tensor("wq", [D, D], F32, kind="ExternalInput").ap()
    wk_d = nc.dram_tensor("wk", [D, 512], F32, kind="ExternalInput").ap()
    wv_d = nc.dram_tensor("wv", [D, 512], F32, kind="ExternalInput").ap()
    wphi_d = nc.dram_tensor("wphi", [D, D], F32, kind="ExternalInput").ap()
    wffn_d = nc.dram_tensor("wffn", [D, D], F32, kind="ExternalInput").ap()
    wo_d = nc.dram_tensor("wo", [D, D], F32, kind="ExternalInput").ap()
    scale_d = nc.dram_tensor("scale", [1, D], F32, kind="ExternalInput").ap()
    cosr_d = nc.dram_tensor("cosr", [P, NT * HD], F32, kind="ExternalInput").ap()
    sinr_d = nc.dram_tensor("sinr", [P, NT * HD], F32, kind="ExternalInput").ap()
    identbf_d = nc.dram_tensor("identbf", [P, P], BF16, kind="ExternalInput").ap()
    identf_d = nc.dram_tensor("identf", [P, P], F32, kind="ExternalInput").ap()
    onesbf_d = nc.dram_tensor("onesbf", [P, 1], BF16, kind="ExternalInput").ap()
    onesf_d = nc.dram_tensor("onesf", [1, P], F32, kind="ExternalInput").ap()
    consts_d = nc.dram_tensor("consts", [P, 8], F32, kind="ExternalInput").ap()
    out_d = nc.dram_tensor("out", [S // 2, D], F32, kind="ExternalOutput").ap()

    wq_v = wq_d.rearrange("(k p) n -> p k n", p=P)
    wk_v = wk_d.rearrange("(k p) n -> p k n", p=P)
    wv_v = wv_d.rearrange("(k p) n -> p k n", p=P)
    wphi_v = wphi_d.rearrange("(k p) n -> p k n", p=P)
    wffn_v = wffn_d.rearrange("(k p) n -> p k n", p=P)
    wo_v = wo_d.rearrange("(k p) n -> p k n", p=P)

    ctx = contextlib.ExitStack()
    with ctx:
        ar = ctx.enter_context(nc.sbuf_tensor("arena", [P, ARENA], mybir.dt.uint8))

        def vw(off, dt, nelem, rows=None):
            nb = nelem * mybir.dt.size(dt)
            a = ar[:, off:off + nb] if rows is None else ar[0:rows, off:off + nb]
            return a.bitcast(dt)

        identbf = vw(O_IDENTBF, BF16, 128)
        identf = vw(O_IDENTF, F32, 128)
        onesbf = vw(O_ONESBF, BF16, 1)
        onesf = vw(O_ONESF, F32, 128, rows=1)
        constsv = vw(O_CONSTS, F32, 8)
        smalls = vw(O_SMALLS, F32, 64)
        lnst = vw(O_LNST, F32, 24)
        QgF = vw(O_QGF, F32, 16)
        qgbdp = vw(O_QGBD, BF16, 512)
        alpha_tm = vw(O_ALPHT, F32, 512)
        alpha_pad = vw(O_ALPHF, F32, 2048)
        alpha_fm = vw(O_ALPHF, F32, 2048, rows=8)
        outer_pad = vw(O_ALPHF, BF16, 2048)

        xnT_own = vw(O_A, BF16, KC * 1024).rearrange("p (k n) -> p k n", n=1024)
        xnT_oth = vw(O_B, BF16, KC * 1024).rearrange("p (k n) -> p k n", n=1024)
        QkF = vw(O_D, BF16, KC * 1024).rearrange("p (c n) -> p c n", n=1024)
        KkT = vw(O_E, BF16, NT * 512).rearrange("p (t n) -> p t n", n=512)
        VT = vw(O_E + 16384, BF16, NT * 512).rearrange("p (t n) -> p t n", n=512)
        KkF = vw(OF_KKF, BF16, 4 * 2048).rearrange("p (c n) -> p c n", n=2048)
        ctxb = vw(O_E, BF16, KC * 1024).rearrange("p (c n) -> p c n", n=1024)
        ysb = vw(O_YSB, BF16, OWN * D).rearrange("p (t n) -> p t n", n=D)

        xt = [vw(OC_XT0, F32, D), vw(OC_XT1, F32, D)]
        xnb = [vw(OC_XNB0, BF16, D), vw(OC_XNB1, BF16, D)]
        sc1 = vw(OC_SC1, F32, 1024)
        sc2 = vw(OC_SC2, F32, 1024)
        sc3 = vw(OC_SC3, F32, 1024)
        qbuf = [vw(OC_QB0, BF16, 512), vw(OC_QB1, BF16, 512)]
        slotG = vw(O_G, BF16, KC * 512).rearrange("p (k n) -> p k n", n=512)
        slotC = vw(OC_WS, BF16, KC * 512).rearrange("p (k n) -> p k n", n=512)
        S0 = vw(O_B, BF16, KC * 512).rearrange("p (k n) -> p k n", n=512)
        S1 = vw(O_B + 16384, BF16, KC * 512).rearrange("p (k n) -> p k n", n=512)
        S2 = vw(O_C, BF16, KC * 512).rearrange("p (k n) -> p k n", n=512)
        S3 = vw(O_C + 16384, BF16, KC * 512).rearrange("p (k n) -> p k n", n=512)
        SS = [S0, S1, S2, S3]
        cosr = vw(OF_COS, F32, NT * HD)
        sinr = vw(OF_SIN, F32, NT * HD)
        akbuf = [vw(OF_AK0, BF16, 2048), vw(OF_AK1, BF16, 2048)]
        xphi = [vw(OF_XP0, BF16, 1024), vw(OF_XP1, BF16, 1024)]
        wbuf = vw(OD_W, F32, D)
        w2buf = vw(OD_W2, F32, D)
        dump = vw(OD_DUMP, BF16, D)
        scale_bc = vw(OD_SCB, F32, D)
        sgbuf = vw(OD_W, F32, 1024)
        xrel = [vw(OE_XR0, F32, 1024), vw(OE_XR1, F32, 1024)]
        outst = [vw(OE_OS0, F32, 1024), vw(OE_OS1, F32, 1024)]

        S_ = SemL(ctx, nc, SEMS)
        block = ctx.enter_context(nc.Block())
        nbar = [0]

        def barrier():
            nbar[0] += 3

            @block.tensor
            def _(t):
                t.drain().then_inc(S_.h["s_bar"], 1)
                t.wait_ge(S_.h["s_bar"], nbar[0])

            @block.vector
            def _(v):
                v.drain().then_inc(S_.h["s_bar"], 1)
                v.wait_ge(S_.h["s_bar"], nbar[0])

            @block.scalar
            def _(s):
                s.drain().then_inc(S_.h["s_bar"], 1)
                s.wait_ge(S_.h["s_bar"], nbar[0])

        def _finish():
            @block.sync
            def _(q):
                S_.wait(q, "s_bar", nbar[0])
                for r in range(8):
                    S_.bump(q.dma_start(
                        out=out_d[r * P:(r + 1) * P, :],
                        in_=ar[:, O_A:O_A + D * 4].bitcast(F32)), "s_out", 16)
                q.wait_ge(S_.h["s_out"], S_.c["s_out"])

        def xnT_k_t(k, t):
            if t < OWN:
                return xnT_own[:, k, t * P:(t + 1) * P]
            return xnT_oth[:, k, (t - OWN) * P:(t - OWN) * P + P]

        def rope_kappa_front(v, ps, t, pf_sem):
            ps3 = ps.ap()[:, 0:512].rearrange("p (h d) -> p h d", d=HD)
            cosb = cosr[:, t * HD:(t + 1) * HD].unsqueeze(1) \
                .broadcast_to([P, 8, HD])
            sin_a = sinr[:, t * HD:t * HD + 32].unsqueeze(1) \
                .broadcast_to([P, 8, 32])
            sin_b = sinr[:, t * HD + 32:t * HD + 64].unsqueeze(1) \
                .broadcast_to([P, 8, 32])
            s1 = sc1[:, 0:512].rearrange("p (h d) -> p h d", d=HD)
            s2 = sc2[:, 0:512].rearrange("p (h d) -> p h d", d=HD)
            v.tensor_tensor(s1, ps3, cosb, ALU.mult)
            v.tensor_tensor(s2[:, :, 0:32], ps3[:, :, 32:64], sin_a, ALU.mult)
            S_.bump(v.tensor_tensor(s2[:, :, 32:64], ps3[:, :, 0:32], sin_b,
                                    ALU.mult), pf_sem)
            v.tensor_tensor(sc1[:, 0:512], sc1[:, 0:512], sc2[:, 0:512], ALU.add)

        psA = contextlib.ExitStack()
        tpb = [psA.enter_context(nc.psum_tensor(f"tpb{i}", [P, 1024], BF16))
               for i in range(4)]
        psqg = psA.enter_context(nc.psum_tensor("psqg", [P, 16], F32))
        pp = [psA.enter_context(nc.psum_tensor(f"pp{i}", [P, 512], F32))
              for i in range(3)]

        @block.sync
        def _(q):
            S_.bump(q.dma_start(out=identbf, in_=identbf_d), "s_x", 16)
            S_.bump(q.dma_start(out=identf, in_=identf_d), "s_x", 16)
            S_.bump(q.dma_start(out=onesbf, in_=onesbf_d), "s_x", 16)
            S_.bump(q.dma_start(out=onesf, in_=onesf_d), "s_x", 16)
            S_.bump(q.dma_start(out=constsv, in_=consts_d), "s_x", 16)
            S_.bump(q.dma_start(out=cosr, in_=cosr_d), "s_x", 16)
            S_.bump(q.dma_start(out=sinr, in_=sinr_d), "s_x", 16)
            for t in range(NT):
                if t >= 2:
                    S_.wait(q, "p1_ln", t - 1)
                S_.bump(q.dma_start(out=xt[t % 2],
                                    in_=x_d[t * P:(t + 1) * P, :]), "s_x", 16)
        XBASE = 7 * 16

        @block.gpsimd
        def _(g):
            if nph >= 2:
                S_.bump(g.dma_start(out=slotG, in_=wk_v), "s_w", 16)
                S_.wait(g, "p1_tp", 2 * NT)
                S_.bump(g.dma_start(out=slotC, in_=wv_v), "s_w", 16)
            if nph >= 3:
                S_.wait(g, "kk_mm", NT)
                S_.bump(g.dma_start(out=slotG, in_=wq_v[:, :, 0:512]), "s_w", 16)
                S_.wait(g, "kv_mm", NT)
                S_.bump(g.dma_start(out=slotC, in_=wq_v[:, :, 512:1024]),
                        "s_w", 16)
                S_.wait(g, "q_mm", NT)
                S_.bump(g.dma_start(out=slotG, in_=wq_v[:, :, 1024:1536]),
                        "s_w", 16)
                S_.wait(g, "q_mm", 2 * NT)
                S_.bump(g.dma_start(out=slotC, in_=wq_v[:, :, 1536:2048]),
                        "s_w", 16)
            if nph >= 6:
                S_.wait(g, "q_mm", 4 * NT)
                S_.wait(g, "q_qg", 4 * NT)
                S_.wait(g, "q_qk", 4 * NT)
                for s in range(4):
                    S_.bump(g.dma_start(out=SS[s],
                                        in_=wphi_v[:, :, 512 * s:512 * s + 512]),
                            "s_w", 16)
            if nph >= 7:
                for s in range(4):
                    S_.wait(g, "d_phi", 4 * (s + 1))
                    S_.bump(g.dma_start(out=SS[s],
                                        in_=wffn_v[:, :, 512 * s:512 * s + 512]),
                            "s_w", 16)
            if nph >= 8:
                for s in range(4):
                    S_.wait(g, "f_mm", 8 * (s // 2 + 1))
                    S_.bump(g.dma_start(out=SS[s],
                                        in_=wo_v[:, :, 512 * s:512 * s + 512]),
                            "s_w", 16)

        def _ln_apply(v, t):
            par = t % 2
            S_.wait(v, "p1_rs", t + 1)
            S_.bump(v.tensor_scalar(xnb[t % 2], xt[t % 2],
                                    smalls[:, 4 * par:4 * par + 1],
                                    smalls[:, 4 * par + 2:4 * par + 3],
                                    ALU.subtract, ALU.mult), "p1_ln")
            for gidx in range(2):
                S_.wait(v, "p1_tp", 2 * t + gidx + 1)
                dst3 = xnT_own if t < OWN else xnT_oth
                tl = t if t < OWN else t - OWN
                dst = dst3[:, 8 * gidx:8 * gidx + 8, tl * P:tl * P + P]
                srcp = tpb[(t % 2) * 2 + gidx].ap() \
                    .rearrange("p (c n) -> p c n", n=P)
                S_.bump(v.tensor_copy(dst, srcp), "p1_ev")

        @block.vector
        def _(v):
            for t in range(NT):
                S_.wait(v, "s_x", XBASE + 16 * (t + 1))
                if t >= 2:
                    S_.wait(v, "p1_tp", 2 * (t - 2) + 2)
                par = t % 2
                lslot = lnst[:, 0:24] if par == 0 else smalls[:, 32:56]
                xb = xt[t % 2]
                for j in range(4):
                    ins = v.bn_stats(lslot[:, 6 * j:6 * j + 6],
                                     xb[:, 512 * j:512 * j + 512])
                S_.bump(ins, "p1_st")
                S_.wait(v, "p1_st", t + 1)
                S_.bump(v.bn_aggr(smalls[:, 4 * par:4 * par + 2], lslot),
                        "p1_mv")
                if t >= 1:
                    _ln_apply(v, t - 1)
            _ln_apply(v, NT - 1)

        @block.scalar
        def _(s):
            for t in range(NT):
                par = t % 2
                S_.wait(s, "p1_mv", t + 1)
                s.activation(smalls[:, 3:4], smalls[:, 4 * par + 1:4 * par + 2],
                             AF.Ln, bias=constsv[:, 0:1])
                S_.bump(s.activation(smalls[:, 4 * par + 2:4 * par + 3],
                                     smalls[:, 3:4], AF.Exp, scale=-0.5),
                        "p1_rs")

        @block.tensor
        def _(t):
            for tt in range(NT):
                S_.wait(t, "p1_ln", tt + 1)
                for gidx in range(2):
                    S_.wait(t, "p1_ev", max(0, 2 * (tt - 2) + gidx + 2))
                    for c in range(8):
                        cc = 8 * gidx + c
                        t.transpose(
                            tpb[(tt % 2) * 2 + gidx][:, c * P:(c + 1) * P],
                            xnb[tt % 2][:, cc * P:(cc + 1) * P], identbf)
                    S_.bump(t.drain(), "p1_tp")

        barrier()
        if nph < 2:
            _finish()
            return nc

        @block.tensor
        def _(t):
            t.wait_ge(S_.h["s_w"], 16)
            for tt in range(NT):
                if tt >= 2:
                    S_.wait(t, "kk_pf", tt - 1)
                for k in range(KC):
                    ins = t.matmul(pp[tt % 3][:], xnT_k_t(k, tt),
                                   slotG[:, k, :], start=(k == 0),
                                   stop=(k == KC - 1))
                S_.bump(ins, "kk_mm")

        @block.vector
        def _(v):
            for tt in range(NT):
                S_.wait(v, "kk_mm", tt + 1)
                rope_kappa_front(v, pp[tt % 3], tt, "kk_pf")
                S_.bump(v.tensor_scalar(sc2[:, 0:512], sc1[:, 0:512], 0.0, None,
                                        ALU.min), "kk_m0")
                S_.wait(v, "kk_e", tt + 1)
                S_.bump(v.scalar_tensor_tensor(KkT[:, tt, :], sc1[:, 0:512], 0.0,
                                               sc3[:, 0:512], ALU.max, ALU.add),
                        "kk_kk")

        @block.scalar
        def _(s):
            for tt in range(NT):
                S_.wait(s, "kk_m0", tt + 1)
                if tt >= 1:
                    S_.wait(s, "kk_kk", tt)
                S_.bump(s.activation(sc3[:, 0:512], sc2[:, 0:512], AF.Exp),
                        "kk_e")

        @block.tensor
        def _(t):
            t.wait_ge(S_.h["s_w"], 32)
            S_.wait(t, "kk_pf", NT)
            S_.wait(t, "kk_kk", NT)
            for tt in range(NT):
                if tt >= 2:
                    S_.wait(t, "kv_vev", tt - 1)
                for k in range(KC):
                    ins = t.matmul(pp[tt % 3][:], xnT_k_t(k, tt),
                                   slotC[:, k, :], start=(k == 0),
                                   stop=(k == KC - 1))
                S_.bump(ins, "kv_mm")

        @block.scalar
        def _(s):
            for tt in range(NT):
                S_.wait(s, "kv_mm", tt + 1)
                S_.bump(s.copy(VT[:, tt, :], pp[tt % 3][:]), "kv_vev")

        barrier()
        if nph < 3:
            _finish()
            return nc

        QTH = [48, 64, 80, 96]
        QSL = [slotG, slotC, slotG, slotC]

        def _q_back(t, i, nown):
            Q, tt = divmod(i, NT)
            S_.wait(t, "q_qk", i + 1)
            ins = None
            for c in range(4):
                col = 4 * Q + c
                ins = t.matmul(psqg[:, col:col + 1],
                               qbuf[i % 2][:, c * P:(c + 1) * P], onesbf,
                               start=(tt == 0 and c == 0),
                               stop=(tt == NT - 1 and c == 3),
                               skip_group_check=True)
            if tt < OWN:
                S_.wait(t, "q_ev", max(0, nown - 1))
                for c in range(4):
                    t.transpose(tpb[nown % 4][:, c * P:(c + 1) * P],
                                qbuf[i % 2][:, c * P:(c + 1) * P],
                                identbf)
                S_.bump(t.drain(), "q_tp")
                nown += 1
                S_.bump(t.nop(), "q_qg")
            else:
                S_.bump(ins, "q_qg")
            return nown

        @block.tensor
        def _(t):
            nown = 0
            for Q in range(4):
                t.wait_ge(S_.h["s_w"], QTH[Q])
                for tt in range(NT):
                    i = Q * NT + tt
                    if i < 2:
                        S_.wait(t, "kv_vev", 14 + i)
                    else:
                        S_.wait(t, "q_pf", i - 1)
                    for k in range(KC):
                        ins = t.matmul(pp[(i + 1) % 3][:], xnT_k_t(k, tt),
                                       QSL[Q][:, k, :], start=(k == 0),
                                       stop=(k == KC - 1))
                    S_.bump(ins, "q_mm")
                    if i >= 1:
                        nown = _q_back(t, i - 1, nown)
            _q_back(t, 4 * NT - 1, nown)

        @block.vector
        def _(v):
            nown = 0
            for Q in range(4):
                for tt in range(NT):
                    i = Q * NT + tt
                    S_.wait(v, "q_mm", i + 1)
                    rope_kappa_front(v, pp[(i + 1) % 3], tt, "q_pf")
                    S_.bump(v.tensor_scalar(sc2[:, 0:512], sc1[:, 0:512], 0.0,
                                            None, ALU.min), "q_m0")
                    S_.wait(v, "q_e", i + 1)
                    S_.wait(v, "q_qg", max(0, i - 1))
                    S_.bump(v.scalar_tensor_tensor(qbuf[i % 2], sc1[:, 0:512],
                                                   0.0, sc3[:, 0:512], ALU.max,
                                                   ALU.add), "q_qk")
                    if tt < OWN:
                        S_.wait(v, "q_tp", nown + 1)
                        dst = QkF[:, 4 * Q:4 * Q + 4, tt * P:tt * P + P]
                        src = tpb[nown % 4].ap()[:, 0:512] \
                            .rearrange("p (c n) -> p c n", n=P)
                        S_.bump(v.tensor_copy(dst, src), "q_ev")
                        nown += 1

        @block.scalar
        def _(s):
            for i in range(4 * NT):
                S_.wait(s, "q_m0", i + 1)
                S_.wait(s, "q_qk", i)
                S_.bump(s.activation(sc3[:, 0:512], sc2[:, 0:512], AF.Exp),
                        "q_e")
            s.wait_ge(S_.h["q_qg"], 4 * NT)
            s.copy(QgF, psqg[:])
            s.mul(QgF, QgF, 1.0 / float(S))

        barrier()
        if nph < 4:
            _finish()
            return nc
        psA.pop_all().close()

        psB = contextlib.ExitStack()
        pslg = psB.enter_context(nc.psum_tensor("pslg", [P, 2048], F32))
        psal = [psB.enter_context(nc.psum_tensor(f"psal{i}", [P, 512], F32))
                for i in range(2)]
        tpk = psB.enter_context(nc.psum_tensor("tpk", [P, 2048], BF16))

        def _a_tp(t, g, jb):
            m = 4 * g + jb
            S_.wait(t, "b_al", g + 1)
            S_.wait(t, "b_aev", m)
            for i in range(4):
                tt = 4 * jb + i
                t.transpose(psal[m % 2][:, i * P:(i + 1) * P],
                            alpha_pad[:, tt * P:(tt + 1) * P], identf)
            S_.bump(t.drain(), "b_atp")

        @block.tensor
        def _(t):
            for j in range(8):
                S_.wait(t, "b_tke", j)
                base = 1024 * (j % 2)
                for half in range(2):
                    tt = 2 * j + half
                    for c in range(4):
                        off = base + half * 512 + c * P
                        t.transpose(tpk[:, off:off + P],
                                    KkT[:, tt, c * P:(c + 1) * P],
                                    identbf)
                S_.bump(t.drain(), "b_tpk")
            S_.wait(t, "b_bd", 1)
            S_.wait(t, "b_tke", 8)
            for g in range(4):
                S_.wait(t, "b_ex", g)
                for st in range(4):
                    ins = t.matmul(pslg[:, 512 * st:512 * st + 512],
                                   qgbdp[:, 128 * g:128 * g + 128],
                                   KkF[:, g, 512 * st:512 * st + 512],
                                   start=True, stop=True)
                S_.bump(ins, "b_lg")
                if g >= 1:
                    for jb in range(4):
                        _a_tp(t, g - 1, jb)
            for jb in range(4):
                _a_tp(t, 3, jb)

        @block.vector
        def _(v):
            for j in range(8):
                S_.wait(v, "b_tpk", j + 1)
                base = 1024 * (j % 2)
                for half in range(2):
                    tt = 2 * j + half
                    dst = KkF[:, 0:4, tt * P:tt * P + P]
                    srcv = tpk.ap()[:, base + half * 512:base + half * 512 + 512] \
                        .rearrange("p (c n) -> p c n", n=P)
                    ins = v.tensor_copy(dst, srcv)
                S_.bump(ins, "b_tke")
            v.memset(qgbdp, 0.0)
            v.memset(alpha_pad, 0.0)
            ins = None
            for g in range(4):
                for j in range(8):
                    h = 8 * g + j
                    par = h % 2
                    srcq = QgF[64 * par:64 * par + 64, h // 2:h // 2 + 1]
                    dst = qgbdp[64 * (j // 4):64 * (j // 4) + 64,
                                128 * g + j:128 * g + j + 1]
                    ins = v.tensor_copy(dst, srcq)
            S_.bump(ins, "b_bd")
            for g in range(4):
                S_.wait(v, "b_lg", g + 1)
                S_.bump(v.reduce_max(smalls[0:8, 8:9], pslg[0:8, :], axis=AX.X,
                                     negate=True), "b_mx")
                S_.wait(v, "b_ex", g + 1)
                S_.bump(v.reciprocal(smalls[0:8, 10:11], smalls[0:8, 9:10]),
                        "v_rc")
                S_.wait(v, "v_rc", S_.c["v_rc"])
                S_.bump(v.tensor_scalar(alpha_fm, alpha_fm, smalls[0:8, 10:11],
                                        float(S), ALU.mult, ALU.mult), "b_al")
                for jb in range(4):
                    m = 4 * g + jb
                    S_.wait(v, "b_atp", m + 1)
                    srca = psal[m % 2].ap() \
                        .rearrange("p (i n) -> p i n", n=P)[:, :, 0:8]
                    dstv = alpha_tm.rearrange("p (t c) -> p t c", c=32)[
                        :, 4 * jb:4 * jb + 4, 8 * g:8 * g + 8]
                    S_.bump(v.tensor_copy(dstv, srca), "b_aev")

        @block.scalar
        def _(s):
            for g in range(4):
                S_.wait(s, "b_mx", g + 1)
                if g >= 1:
                    S_.wait(s, "b_al", g)
                    S_.wait(s, "b_atp", 4 * g)
                s.activation(alpha_fm, pslg[0:8, :], AF.Exp,
                             bias=smalls[0:8, 8:9],
                             accum_out=smalls[0:8, 9:10])
                S_.bump(s.drain(), "b_ex")

        barrier()
        if nph < 5:
            _finish()
            return nc
        psB.pop_all().close()

        psC = contextlib.ExitStack()
        psot = psC.enter_context(nc.psum_tensor("psot", [P, 1024], F32))

        @block.vector
        def _(v):
            v.memset(outer_pad, 0.0)
            for tt in range(NT):
                if tt >= 2:
                    S_.wait(v, "c_mm", tt - 1)
                kk = KkT[:, tt, :].rearrange("p (h d) -> p h d", d=HD) \
                    .unsqueeze(2).broadcast_to([P, 8, 4, HD])
                al = alpha_tm[:, tt * 32:(tt + 1) * 32] \
                    .rearrange("p (h j) -> p h j", j=4) \
                    .unsqueeze(3).broadcast_to([P, 8, 4, HD])
                dst = akbuf[tt % 2].rearrange("p (h j d) -> p h j d", j=4, d=HD)
                S_.bump(v.tensor_tensor(dst, kk, al, ALU.mult), "c_ak")

        @block.tensor
        def _(t):
            for tt in range(NT):
                S_.wait(t, "c_ak", tt + 1)
                for j in range(16):
                    ins = t.matmul(
                        psot[:, 64 * j:64 * j + 64],
                        akbuf[tt % 2][:, 128 * j:128 * j + 128],
                        VT[:, tt, 64 * (j // 2):64 * (j // 2) + 64],
                        start=(tt == 0 and j in (0, 8)),
                        stop=(tt == NT - 1), skip_group_check=True)
                S_.bump(ins, "c_mm")

        @block.scalar
        def _(s):
            S_.wait(s, "c_mm", NT)
            ope = outer_pad[0:64, :].rearrange("p (h e d) -> p h e d",
                                               e=2, d=HD)[:, :, 0, :]
            opo = outer_pad[64:128, :].rearrange("p (h e d) -> p h e d",
                                                 e=2, d=HD)[:, :, 1, :]
            s.copy(ope, psot[0:64, :].rearrange("p (h d) -> p h d", d=HD))
            S_.bump(s.copy(opo, psot[64:128, :]
                           .rearrange("p (h d) -> p h d", d=HD)), "c_oev")

        barrier()
        if nph < 6:
            _finish()
            return nc
        psC.pop_all().close()

        psD = contextlib.ExitStack()
        psph = [psD.enter_context(nc.psum_tensor(f"psph{i}", [P, 1024], F32))
                for i in range(2)]
        psrs = [psD.enter_context(nc.psum_tensor(f"psrs{i}", [P, 1024], F32))
                for i in range(2)]

        def _res_mm(t, m):
            S_.wait(t, "d_ctx", max(0, m - 1))
            S_.wait(t, "d_xev", m + 1)
            ins = None
            for st in range(2):
                ins = t.matmul(psrs[m % 2][:, 512 * st:512 * st + 512],
                               outer_pad[:, 128 * m:128 * m + 128],
                               QkF[:, m, 512 * st:512 * st + 512],
                               start=True, stop=True)
            S_.bump(ins, "d_res")

        @block.tensor
        def _(t):
            for m in range(KC):
                S_.wait(t, "s_w", 112 + 16 * (m // 4))
                S_.wait(t, "d_xev", max(0, m - 1))
                for k in range(KC):
                    for st in range(2):
                        ins = t.matmul(psph[m % 2][:, 512 * st:512 * st + 512],
                                       SS[m // 4][:, k,
                                                  128 * (m % 4):128 * (m % 4) + 128],
                                       xnT_own[:, k, 512 * st:512 * st + 512],
                                       start=(k == 0), stop=(k == KC - 1))
                S_.bump(ins, "d_phi")
                if m >= 1:
                    _res_mm(t, m - 1)
            _res_mm(t, KC - 1)

        @block.scalar
        def _(s):
            S_.wait(s, "c_oev", 1)
            for m in range(KC):
                S_.wait(s, "d_phi", m + 1)
                if m >= 2:
                    S_.wait(s, "d_ctx", m - 1)
                s.copy(xphi[m % 2], psph[m % 2][:])
                S_.bump(s.drain(), "d_xev")

        @block.vector
        def _(v):
            for m in range(KC):
                S_.wait(v, "d_res", m + 1)
                v.tensor_tensor(ctxb[:, m, :], psrs[m % 2][:],
                                xphi[m % 2], ALU.mult)
                S_.bump(v.drain(), "d_ctx")

        barrier()
        if nph < 7:
            _finish()
            return nc
        psD.pop_all().close()

        psbcS = contextlib.ExitStack()
        psbc = [psbcS.enter_context(nc.psum_tensor(f"psbc{i}", [P, 512], F32))
                for i in range(2)]

        @block.sync
        def _(q):
            S_.wait(q, "d_ctx", KC)
            S_.bump(q.dma_start(out=scale_bc[0:1, :], in_=scale_d), "s_x", 16)
        SCROW = S_.c["s_x"]

        @block.vector
        def _(v):
            v.memset(scale_bc, 0.0)
            v.memset(wbuf[:, 0:128], 1.0)
            S_.bump(v.nop(), "f_bc2")
            for st in range(4):
                S_.wait(v, "f_bc", st + 1)
                S_.bump(v.tensor_copy(scale_bc[:, 512 * st:512 * st + 512],
                                      psbc[st % 2][:]), "f_bc2")
            for tt in range(OWN):
                S_.wait(v, "f_yev", OWN + tt + 1)
                S_.bump(v.tensor_tensor(wbuf, ysb[:, tt, :], scale_bc,
                                        ALU.mult), "f_v")
                S_.wait(v, "f_a", 4 * tt + 1)
                S_.bump(v.tensor_scalar(smalls[:, 17:18], smalls[:, 16:17],
                                        1e-14, None, ALU.max), "f_v")
                S_.wait(v, "f_a", 4 * tt + 2)
                v.tensor_scalar(smalls[:, 20:21], smalls[:, 19:20], -1.0, 1.0,
                                ALU.mult, ALU.add)
                v.tensor_scalar(smalls[:, 21:22], smalls[:, 19:20], 1.0, None,
                                ALU.add)
                S_.bump(v.tensor_scalar(smalls[:, 22:23], smalls[:, 18:19],
                                        1e-7, None, ALU.add), "v_rc")
                S_.wait(v, "v_rc", S_.c["v_rc"])
                S_.bump(v.tensor_tensor(smalls[:, 23:24], smalls[:, 21:22],
                                        smalls[:, 22:23], ALU.mult), "v_rc")
                S_.wait(v, "v_rc", S_.c["v_rc"])
                S_.bump(v.reciprocal(smalls[:, 24:25], smalls[:, 23:24]),
                        "v_rc")
                S_.wait(v, "v_rc", S_.c["v_rc"])
                S_.bump(v.tensor_tensor(smalls[:, 25:26], smalls[:, 20:21],
                                        smalls[:, 24:25], ALU.mult), "v_rc")
                S_.wait(v, "v_rc", S_.c["v_rc"])
                S_.bump(v.tensor_scalar(w2buf, wbuf, smalls[:, 25:26], None,
                                        ALU.mult), "f_v")
                S_.wait(v, "f_a", 4 * tt + 3)
                S_.bump(v.tensor_scalar(smalls[:, 27:28], smalls[:, 26:27],
                                        1e-14, None, ALU.max), "f_v")
                S_.wait(v, "f_a", 4 * tt + 4)
                S_.bump(v.reciprocal(smalls[:, 29:30], smalls[:, 28:29]),
                        "v_rc")
                S_.wait(v, "v_rc", S_.c["v_rc"])
                S_.bump(v.tensor_scalar(smalls[:, 30:31], smalls[:, 29:30],
                                        0.99, 1.0, ALU.mult, ALU.min), "v_rc")
                S_.wait(v, "v_rc", S_.c["v_rc"])
                S_.bump(v.tensor_scalar(ysb[:, tt, :], w2buf, smalls[:, 30:31],
                                        None, ALU.mult), "f_v")

        @block.tensor
        def _(t):
            t.wait_ge(S_.h["s_x"], SCROW)
            for st in range(4):
                S_.wait(t, "f_bc2", st + 1)
                S_.bump(t.matmul(psbc[st % 2][:], wbuf[:, 0:128],
                                 scale_bc[:, 512 * st:512 * st + 512],
                                 start=True, stop=True), "f_bc")

        psbcS.pop_all().close()
        psE = contextlib.ExitStack()
        psf = [psE.enter_context(nc.psum_tensor(f"psf{i}", [P, 1024], F32))
               for i in range(4)]

        @block.tensor
        def _(t):
            for half in range(2):
                t.wait_ge(S_.h["s_w"], 192 + 32 * half)
                t.wait_ge(S_.h["f_bc2"], 5)
                for tt in range(OWN):
                    i = half * OWN + tt
                    if i >= 3:
                        S_.wait(t, "f_yev", i - 2)
                    for k in range(KC):
                        for st in range(2):
                            sl = SS[2 * half + st]
                            ins = t.matmul(psf[i % 4][:, 512 * st:512 * st + 512],
                                           xnT_own[:, k, tt * P:(tt + 1) * P],
                                           sl[:, k, :], start=(k == 0),
                                           stop=(k == KC - 1))
                    S_.bump(ins, "f_mm")

        def _epi_act(s, tt):
            S_.wait(s, "f_v", 5 * tt + 1)
            s.activation(dump, wbuf, AF.Square,
                         accum_out=smalls[:, 16:17])
            S_.bump(s.drain(), "f_a")
            S_.wait(s, "f_v", 5 * tt + 2)
            s.activation(smalls[:, 31:32], smalls[:, 17:18], AF.Ln)
            s.activation(smalls[:, 18:19], smalls[:, 31:32], AF.Exp,
                         scale=0.5)
            S_.bump(s.activation(smalls[:, 19:20], smalls[:, 18:19], AF.Exp,
                                 scale=-2.0), "f_a")
            S_.wait(s, "f_v", 5 * tt + 3)
            s.activation(dump, w2buf, AF.Square,
                         accum_out=smalls[:, 26:27])
            S_.bump(s.drain(), "f_a")
            S_.wait(s, "f_v", 5 * tt + 4)
            s.activation(smalls[:, 31:32], smalls[:, 27:28], AF.Ln)
            S_.bump(s.activation(smalls[:, 28:29], smalls[:, 31:32], AF.Exp,
                                 scale=0.5), "f_a")
            S_.wait(s, "f_v", 5 * tt + 5)
            S_.bump(s.activation(ysb[:, tt, :], ysb[:, tt, :], AF.Silu),
                    "g_si")

        @block.scalar
        def _(s):
            for tt in range(OWN):
                S_.wait(s, "f_mm", tt + 1)
                S_.bump(s.copy(ysb[:, tt, 0:1024], psf[tt % 4][:]), "f_yev")
            for tt in range(OWN):
                i = OWN + tt
                S_.wait(s, "f_mm", i + 1)
                cp = s.copy(ysb[:, tt, 1024:2048], psf[i % 4][:])
                if tt == OWN - 1:
                    S_.bump(s.drain(), "f_yev")
                else:
                    S_.bump(cp, "f_yev")
                if tt >= 1:
                    _epi_act(s, tt - 1)
            _epi_act(s, OWN - 1)

        if nph < 8:
            barrier()
            _finish()
            return nc
        psE.pop_all().close()

        psF = contextlib.ExitStack()
        pswo = [psF.enter_context(nc.psum_tensor(f"pswo{i}", [P, 1024], F32))
                for i in range(4)]

        @block.tensor
        def _(t):
            t.wait_ge(S_.h["f_yev"], 2 * OWN)
            t.wait_ge(S_.h["f_bc2"], 5)
            for wh in range(2):
                t.wait_ge(S_.h["s_w"], 256 + 32 * wh)
                for tt in range(OWN):
                    i = wh * OWN + tt
                    if i >= 4:
                        S_.wait(t, "wo_fin", i - 3)
                    for m in range(KC):
                        for st in range(2):
                            sl = SS[2 * wh + st]
                            ins = t.matmul(pswo[i % 4][:, 512 * st:512 * st + 512],
                                           ctxb[:, m, tt * P:(tt + 1) * P],
                                           sl[:, m, :], start=(m == 0),
                                           stop=(m == KC - 1))
                    S_.bump(ins, "wo_mm")

        @block.sync
        def _(q):
            S_.wait(q, "f_mm", 2 * OWN)
            for i in range(2 * OWN):
                wh, tt = divmod(i, OWN)
                if i >= 2:
                    S_.wait(q, "wo_fin", i - 1)
                S_.bump(q.dma_start(
                    out=xrel[i % 2],
                    in_=x_d[tt * P:(tt + 1) * P,
                            wh * 1024:wh * 1024 + 1024]), "s_x", 16)
                if i >= 1:
                    S_.wait(q, "wo_fin", i)
                    wh2, tt2 = divmod(i - 1, OWN)
                    S_.bump(q.dma_start(
                        out=out_d[tt2 * P:(tt2 + 1) * P,
                                  wh2 * 1024:wh2 * 1024 + 1024],
                        in_=outst[(i - 1) % 2]), "s_out", 16)
            S_.wait(q, "wo_fin", 2 * OWN)
            S_.bump(q.dma_start(
                out=out_d[(OWN - 1) * P:OWN * P, 1024:2048],
                in_=outst[(2 * OWN - 1) % 2]), "s_out", 16)
            q.wait_ge(S_.h["s_out"], 16 * 2 * OWN)

        XRBASE = SCROW

        @block.vector
        def _(v):
            for i in range(2 * OWN):
                wh, tt = divmod(i, OWN)
                S_.wait(v, "g_si", tt + 1)
                S_.wait(v, "wo_mm", i + 1)
                S_.wait(v, "s_x", XRBASE + 16 * (i + 1))
                if i >= 2:
                    S_.wait(v, "s_out", 16 * (i - 1))
                v.tensor_tensor(sgbuf, pswo[i % 4][:],
                                ysb[:, tt, wh * 1024:wh * 1024 + 1024],
                                ALU.mult)
                S_.bump(v.tensor_tensor(outst[i % 2], sgbuf, xrel[i % 2],
                                        ALU.add), "wo_fin")

        psF.pop_all().close()

    return nc


def kernel(L0_attn_hidden_states, L0_attn_Wq, L0_attn_Wk, L0_attn_Wv,
           L0_attn_Wphi, L0_attn_Wo, L0_ffn_weight, L0_ffn_scale):
    x = np.ascontiguousarray(np.asarray(L0_attn_hidden_states, dtype=np.float32))
    wq = np.ascontiguousarray(np.asarray(L0_attn_Wq, dtype=np.float32))
    wk = np.ascontiguousarray(np.asarray(L0_attn_Wk, dtype=np.float32))
    wv = np.ascontiguousarray(np.asarray(L0_attn_Wv, dtype=np.float32))
    wphi = np.ascontiguousarray(np.asarray(L0_attn_Wphi, dtype=np.float32))
    wo = np.ascontiguousarray(np.asarray(L0_attn_Wo, dtype=np.float32))
    wffn = np.ascontiguousarray(np.asarray(L0_ffn_weight, dtype=np.float32))
    scale = np.ascontiguousarray(
        np.asarray(L0_ffn_scale, dtype=np.float32)).reshape(1, D)

    identbf = np.eye(P, dtype=ml_dtypes.bfloat16)
    identf = np.eye(P, dtype=np.float32)
    onesbf = np.ones((P, 1), dtype=ml_dtypes.bfloat16)
    onesf = np.ones((1, P), dtype=np.float32)
    consts = np.zeros((P, 8), dtype=np.float32)
    consts[:, 0] = 1e-5

    inv = 1.0 / (10000.0 ** (np.arange(0, HD, 2, dtype=np.float64) / HD))
    tables = []
    for h in (0, 1):
        pos = (np.arange(S) + 1024 * h) % S
        ang = pos[:, None].astype(np.float64) * inv[None, :]
        cos = np.concatenate([np.cos(ang), np.cos(ang)], axis=1)
        sinp = np.concatenate([-np.sin(ang), np.sin(ang)], axis=1)
        cosr = cos.reshape(NT, P, HD).transpose(1, 0, 2).reshape(P, NT * HD)
        sinr = sinp.reshape(NT, P, HD).transpose(1, 0, 2).reshape(P, NT * HD)
        tables.append((np.ascontiguousarray(cosr.astype(np.float32)),
                       np.ascontiguousarray(sinr.astype(np.float32))))

    nc = _build()
    in_maps = []
    for c in range(8):
        b, h = divmod(c, 2)
        xc = np.ascontiguousarray(np.roll(x[b], -1024 * h, axis=0))
        in_maps.append({
            "x": xc, "wq": wq, "wk": wk, "wv": wv, "wphi": wphi,
            "wffn": wffn, "wo": wo, "scale": scale,
            "cosr": tables[h][0], "sinr": tables[h][1],
            "identbf": identbf, "identf": identf, "onesbf": onesbf,
            "onesf": onesf, "consts": consts,
        })
    res = run_bass_kernel_spmd(nc, in_maps, core_ids=list(range(8)))
    out = np.empty((4, S, D), dtype=np.float32)
    for c in range(8):
        b, h = divmod(c, 2)
        out[b, 1024 * h:1024 * h + 1024, :] = res.results[c]["out"]
    return out


# version 11
# speedup vs baseline: 1.0529x; 1.0023x over previous
import contextlib

import numpy as np
import ml_dtypes

import concourse.bass as bass
import concourse.mybir as mybir
from concourse.bass_utils import run_bass_kernel_spmd

F32 = mybir.dt.float32
BF16 = mybir.dt.bfloat16
AF = mybir.ActivationFunctionType
ALU = mybir.AluOpType
AX = mybir.AxisListType

P = 128
S = 2048
D = 2048
NT = 16
OWN = 8
HD = 64
KC = 16

O_IDENTBF = 0
O_IDENTF = 256
O_ONESBF = 768
O_ONESF = 800
O_CONSTS = 1312
O_SMALLS = 1376
O_LNST = 1632
O_QGF = 1760
O_QGBD = 1888
O_ALPHT = 2912
O_ALPHF = 4960

O_A = 16128
O_B = 48896
O_C = 81664
O_D = 114432
O_E = 147200
O_F = 179968
O_G = 196352
ARENA = 212736

OC_XT0, OC_XT1 = O_C, O_C + 8192
OC_XNB0, OC_XNB1 = O_C + 16384, O_C + 20480
OC_SC1, OC_SC2, OC_SC3 = O_C, O_C + 4096, O_C + 8192
OC_QB0, OC_QB1 = O_C + 12288, O_C + 13312
OC_WS = O_C + 14848
OD_W, OD_W2, OD_DUMP, OD_SCB = O_D, O_D + 8192, O_D + 16384, O_D + 20480
OE_XR0, OE_XR1 = O_A, O_A + 4096
OE_OS0, OE_OS1 = O_A + 8192, O_A + 12288
OF_COS, OF_SIN = O_F, O_F + 4096
OF_KKF = O_F
OF_AK0, OF_AK1 = O_F, O_F + 4096
OF_XP0, OF_XP1 = O_F, O_F + 2048
O_YSB = O_F


class SemL:
    def __init__(self, ctx, nc, names):
        self.h = {n: ctx.enter_context(nc.semaphore(n)) for n in names}
        self.c = {n: 0 for n in names}

    def bump(self, instr, name, amt=1):
        instr.then_inc(self.h[name], amt)
        self.c[name] += amt
        return self.c[name]

    def wait(self, eng, name, v):
        if v > 0:
            eng.wait_ge(self.h[name], v)


SEMS = [
    "s_x", "s_w", "s_out", "s_bar",
    "p1_st", "p1_mv", "p1_rs", "p1_ln", "p1_tp", "p1_ev",
    "kk_mm", "kk_pf", "kk_m0", "kk_e", "kk_kk",
    "kv_mm", "kv_vev",
    "q_mm", "q_pf", "q_m0", "q_e", "q_qk", "q_qg", "q_tp", "q_ev",
    "b_tpk", "b_tke", "b_bd", "b_lg", "b_mx", "b_ex", "b_al", "b_atp", "b_aev",
    "c_ak", "c_mm", "c_oev",
    "d_phi", "d_xev", "d_res", "d_ctx",
    "f_mm", "f_yev", "f_v", "f_a", "f_bc", "f_bc2",
    "g_si", "wo_mm", "wo_fin", "v_rc",
]


def _build(nph=99):
    nc = bass.Bass("TRN2", target_bir_lowering=False)

    x_d = nc.dram_tensor("x", [S, D], F32, kind="ExternalInput").ap()
    wq_d = nc.dram_tensor("wq", [D, D], F32, kind="ExternalInput").ap()
    wk_d = nc.dram_tensor("wk", [D, 512], F32, kind="ExternalInput").ap()
    wv_d = nc.dram_tensor("wv", [D, 512], F32, kind="ExternalInput").ap()
    wphi_d = nc.dram_tensor("wphi", [D, D], F32, kind="ExternalInput").ap()
    wffn_d = nc.dram_tensor("wffn", [D, D], F32, kind="ExternalInput").ap()
    wo_d = nc.dram_tensor("wo", [D, D], F32, kind="ExternalInput").ap()
    scale_d = nc.dram_tensor("scale", [1, D], F32, kind="ExternalInput").ap()
    cosr_d = nc.dram_tensor("cosr", [P, NT * HD], F32, kind="ExternalInput").ap()
    sinr_d = nc.dram_tensor("sinr", [P, NT * HD], F32, kind="ExternalInput").ap()
    identbf_d = nc.dram_tensor("identbf", [P, P], BF16, kind="ExternalInput").ap()
    identf_d = nc.dram_tensor("identf", [P, P], F32, kind="ExternalInput").ap()
    onesbf_d = nc.dram_tensor("onesbf", [P, 1], BF16, kind="ExternalInput").ap()
    onesf_d = nc.dram_tensor("onesf", [1, P], F32, kind="ExternalInput").ap()
    consts_d = nc.dram_tensor("consts", [P, 8], F32, kind="ExternalInput").ap()
    out_d = nc.dram_tensor("out", [S // 2, D], F32, kind="ExternalOutput").ap()

    wq_v = wq_d.rearrange("(k p) n -> p k n", p=P)
    wk_v = wk_d.rearrange("(k p) n -> p k n", p=P)
    wv_v = wv_d.rearrange("(k p) n -> p k n", p=P)
    wphi_v = wphi_d.rearrange("(k p) n -> p k n", p=P)
    wffn_v = wffn_d.rearrange("(k p) n -> p k n", p=P)
    wo_v = wo_d.rearrange("(k p) n -> p k n", p=P)

    ctx = contextlib.ExitStack()
    with ctx:
        ar = ctx.enter_context(nc.sbuf_tensor("arena", [P, ARENA], mybir.dt.uint8))

        def vw(off, dt, nelem, rows=None):
            nb = nelem * mybir.dt.size(dt)
            a = ar[:, off:off + nb] if rows is None else ar[0:rows, off:off + nb]
            return a.bitcast(dt)

        identbf = vw(O_IDENTBF, BF16, 128)
        identf = vw(O_IDENTF, F32, 128)
        onesbf = vw(O_ONESBF, BF16, 1)
        onesf = vw(O_ONESF, F32, 128, rows=1)
        constsv = vw(O_CONSTS, F32, 8)
        smalls = vw(O_SMALLS, F32, 64)
        lnst = vw(O_LNST, F32, 24)
        QgF = vw(O_QGF, F32, 16)
        qgbdp = vw(O_QGBD, BF16, 512)
        alpha_tm = vw(O_ALPHT, F32, 512)
        alpha_pad = vw(O_ALPHF, F32, 2048)
        alpha_fm = vw(O_ALPHF, F32, 2048, rows=8)
        outer_pad = vw(O_ALPHF, BF16, 2048)

        xnT_own = vw(O_A, BF16, KC * 1024).rearrange("p (k n) -> p k n", n=1024)
        xnT_oth = vw(O_B, BF16, KC * 1024).rearrange("p (k n) -> p k n", n=1024)
        QkF = vw(O_D, BF16, KC * 1024).rearrange("p (c n) -> p c n", n=1024)
        KkT = vw(O_E, BF16, NT * 512).rearrange("p (t n) -> p t n", n=512)
        VT = vw(O_E + 16384, BF16, NT * 512).rearrange("p (t n) -> p t n", n=512)
        KkF = vw(OF_KKF, BF16, 4 * 2048).rearrange("p (c n) -> p c n", n=2048)
        ctxb = vw(O_E, BF16, KC * 1024).rearrange("p (c n) -> p c n", n=1024)
        ysb = vw(O_YSB, BF16, OWN * D).rearrange("p (t n) -> p t n", n=D)

        xt = [vw(OC_XT0, F32, D), vw(OC_XT1, F32, D)]
        xnb = [vw(OC_XNB0, BF16, D), vw(OC_XNB1, BF16, D)]
        sc1 = vw(OC_SC1, F32, 1024)
        sc2 = vw(OC_SC2, F32, 1024)
        sc3 = vw(OC_SC3, F32, 1024)
        qbuf = [vw(OC_QB0, BF16, 512), vw(OC_QB1, BF16, 512)]
        slotG = vw(O_G, BF16, KC * 512).rearrange("p (k n) -> p k n", n=512)
        slotC = vw(OC_WS, BF16, KC * 512).rearrange("p (k n) -> p k n", n=512)
        S0 = vw(O_B, BF16, KC * 512).rearrange("p (k n) -> p k n", n=512)
        S1 = vw(O_B + 16384, BF16, KC * 512).rearrange("p (k n) -> p k n", n=512)
        S2 = vw(O_C, BF16, KC * 512).rearrange("p (k n) -> p k n", n=512)
        S3 = vw(O_C + 16384, BF16, KC * 512).rearrange("p (k n) -> p k n", n=512)
        SS = [S0, S1, S2, S3]
        cosr = vw(OF_COS, F32, NT * HD)
        sinr = vw(OF_SIN, F32, NT * HD)
        akbuf = [vw(OF_AK0, BF16, 2048), vw(OF_AK1, BF16, 2048)]
        xphi = [vw(OF_XP0, BF16, 1024), vw(OF_XP1, BF16, 1024)]
        wbuf = vw(OD_W, F32, D)
        w2buf = vw(OD_W2, F32, D)
        dump = vw(OD_DUMP, BF16, D)
        scale_bc = vw(OD_SCB, F32, D)
        sgbuf = vw(OD_W, F32, 1024)
        xrel = [vw(OE_XR0, F32, 1024), vw(OE_XR1, F32, 1024)]
        outst = [vw(OE_OS0, F32, 1024), vw(OE_OS1, F32, 1024)]

        S_ = SemL(ctx, nc, SEMS)
        block = ctx.enter_context(nc.Block())
        nbar = [0]

        def barrier():
            nbar[0] += 3

            @block.tensor
            def _(t):
                t.drain().then_inc(S_.h["s_bar"], 1)
                t.wait_ge(S_.h["s_bar"], nbar[0])

            @block.vector
            def _(v):
                v.drain().then_inc(S_.h["s_bar"], 1)
                v.wait_ge(S_.h["s_bar"], nbar[0])

            @block.scalar
            def _(s):
                s.drain().then_inc(S_.h["s_bar"], 1)
                s.wait_ge(S_.h["s_bar"], nbar[0])

        def _finish():
            @block.sync
            def _(q):
                S_.wait(q, "s_bar", nbar[0])
                for r in range(8):
                    S_.bump(q.dma_start(
                        out=out_d[r * P:(r + 1) * P, :],
                        in_=ar[:, O_A:O_A + D * 4].bitcast(F32)), "s_out", 16)
                q.wait_ge(S_.h["s_out"], S_.c["s_out"])

        def xnT_k_t(k, t):
            if t < OWN:
                return xnT_own[:, k, t * P:(t + 1) * P]
            return xnT_oth[:, k, (t - OWN) * P:(t - OWN) * P + P]

        def rope_kappa_front(v, ps, t, pf_sem):
            ps3 = ps.ap()[:, 0:512].rearrange("p (h d) -> p h d", d=HD)
            cosb = cosr[:, t * HD:(t + 1) * HD].unsqueeze(1) \
                .broadcast_to([P, 8, HD])
            sin_a = sinr[:, t * HD:t * HD + 32].unsqueeze(1) \
                .broadcast_to([P, 8, 32])
            sin_b = sinr[:, t * HD + 32:t * HD + 64].unsqueeze(1) \
                .broadcast_to([P, 8, 32])
            s1 = sc1[:, 0:512].rearrange("p (h d) -> p h d", d=HD)
            s2 = sc2[:, 0:512].rearrange("p (h d) -> p h d", d=HD)
            v.tensor_tensor(s1, ps3, cosb, ALU.mult)
            v.tensor_tensor(s2[:, :, 0:32], ps3[:, :, 32:64], sin_a, ALU.mult)
            S_.bump(v.tensor_tensor(s2[:, :, 32:64], ps3[:, :, 0:32], sin_b,
                                    ALU.mult), pf_sem)
            v.tensor_tensor(sc1[:, 0:512], sc1[:, 0:512], sc2[:, 0:512], ALU.add)

        psA = contextlib.ExitStack()
        tpb = [psA.enter_context(nc.psum_tensor(f"tpb{i}", [P, 1024], BF16))
               for i in range(4)]
        psqg = psA.enter_context(nc.psum_tensor("psqg", [P, 16], F32))
        pp = [psA.enter_context(nc.psum_tensor(f"pp{i}", [P, 512], F32))
              for i in range(3)]

        @block.sync
        def _(q):
            S_.bump(q.dma_start(out=identbf, in_=identbf_d), "s_x", 16)
            S_.bump(q.dma_start(out=identf, in_=identf_d), "s_x", 16)
            S_.bump(q.dma_start(out=onesbf, in_=onesbf_d), "s_x", 16)
            S_.bump(q.dma_start(out=onesf, in_=onesf_d), "s_x", 16)
            S_.bump(q.dma_start(out=constsv, in_=consts_d), "s_x", 16)
            S_.bump(q.dma_start(out=cosr, in_=cosr_d), "s_x", 16)
            S_.bump(q.dma_start(out=sinr, in_=sinr_d), "s_x", 16)
            for t in range(NT):
                if t >= 2:
                    S_.wait(q, "p1_ln", t - 1)
                S_.bump(q.dma_start(out=xt[t % 2],
                                    in_=x_d[t * P:(t + 1) * P, :]), "s_x", 16)
        XBASE = 7 * 16

        @block.gpsimd
        def _(g):
            if nph >= 2:
                S_.bump(g.dma_start(out=slotG, in_=wk_v), "s_w", 16)
                S_.wait(g, "p1_tp", 2 * NT)
                S_.bump(g.dma_start(out=slotC, in_=wv_v), "s_w", 16)
            if nph >= 3:
                S_.wait(g, "kk_mm", NT)
                S_.bump(g.dma_start(out=slotG, in_=wq_v[:, :, 0:512]), "s_w", 16)
                S_.wait(g, "kv_mm", NT)
                S_.bump(g.dma_start(out=slotC, in_=wq_v[:, :, 512:1024]),
                        "s_w", 16)
                S_.wait(g, "q_mm", NT)
                S_.bump(g.dma_start(out=slotG, in_=wq_v[:, :, 1024:1536]),
                        "s_w", 16)
                S_.wait(g, "q_mm", 2 * NT)
                S_.bump(g.dma_start(out=slotC, in_=wq_v[:, :, 1536:2048]),
                        "s_w", 16)
            if nph >= 6:
                S_.wait(g, "q_mm", 4 * NT)
                S_.wait(g, "q_qg", 4 * NT)
                S_.wait(g, "q_qk", 4 * NT)
                for s in range(4):
                    S_.bump(g.dma_start(out=SS[s],
                                        in_=wphi_v[:, :, 512 * s:512 * s + 512]),
                            "s_w", 16)
            if nph >= 7:
                for s in range(4):
                    S_.wait(g, "d_phi", 4 * (s + 1))
                    S_.bump(g.dma_start(out=SS[s],
                                        in_=wffn_v[:, :, 512 * s:512 * s + 512]),
                            "s_w", 16)
            if nph >= 8:
                for s in range(4):
                    S_.wait(g, "f_mm", 8 * (s // 2 + 1))
                    S_.bump(g.dma_start(out=SS[s],
                                        in_=wo_v[:, :, 512 * s:512 * s + 512]),
                            "s_w", 16)

        def _ln_apply(v, t):
            par = t % 2
            S_.wait(v, "p1_rs", t + 1)
            S_.bump(v.tensor_scalar(xnb[t % 2], xt[t % 2],
                                    smalls[:, 4 * par:4 * par + 1],
                                    smalls[:, 4 * par + 2:4 * par + 3],
                                    ALU.subtract, ALU.mult), "p1_ln")
            for gidx in range(2):
                S_.wait(v, "p1_tp", 2 * t + gidx + 1)
                dst3 = xnT_own if t < OWN else xnT_oth
                tl = t if t < OWN else t - OWN
                dst = dst3[:, 8 * gidx:8 * gidx + 8, tl * P:tl * P + P]
                srcp = tpb[(t % 2) * 2 + gidx].ap() \
                    .rearrange("p (c n) -> p c n", n=P)
                S_.bump(v.tensor_copy(dst, srcp), "p1_ev")

        @block.vector
        def _(v):
            for t in range(NT):
                S_.wait(v, "s_x", XBASE + 16 * (t + 1))
                if t >= 2:
                    S_.wait(v, "p1_tp", 2 * (t - 2) + 2)
                par = t % 2
                lslot = lnst[:, 0:24] if par == 0 else smalls[:, 32:56]
                xb = xt[t % 2]
                for j in range(4):
                    ins = v.bn_stats(lslot[:, 6 * j:6 * j + 6],
                                     xb[:, 512 * j:512 * j + 512])
                S_.bump(ins, "p1_st")
                S_.wait(v, "p1_st", t + 1)
                S_.bump(v.bn_aggr(smalls[:, 4 * par:4 * par + 2], lslot),
                        "p1_mv")
                if t >= 1:
                    _ln_apply(v, t - 1)
            _ln_apply(v, NT - 1)

        @block.scalar
        def _(s):
            for t in range(NT):
                par = t % 2
                S_.wait(s, "p1_mv", t + 1)
                s.activation(smalls[:, 3:4], smalls[:, 4 * par + 1:4 * par + 2],
                             AF.Ln, bias=constsv[:, 0:1])
                S_.bump(s.activation(smalls[:, 4 * par + 2:4 * par + 3],
                                     smalls[:, 3:4], AF.Exp, scale=-0.5),
                        "p1_rs")

        @block.tensor
        def _(t):
            for tt in range(NT):
                S_.wait(t, "p1_ln", tt + 1)
                for gidx in range(2):
                    S_.wait(t, "p1_ev", max(0, 2 * (tt - 2) + gidx + 2))
                    for c in range(8):
                        cc = 8 * gidx + c
                        t.transpose(
                            tpb[(tt % 2) * 2 + gidx][:, c * P:(c + 1) * P],
                            xnb[tt % 2][:, cc * P:(cc + 1) * P], identbf)
                    S_.bump(t.drain(), "p1_tp")

        barrier()
        if nph < 2:
            _finish()
            return nc

        @block.tensor
        def _(t):
            t.wait_ge(S_.h["s_w"], 16)
            for tt in range(NT):
                if tt >= 2:
                    S_.wait(t, "kk_pf", tt - 1)
                for k in range(KC):
                    ins = t.matmul(pp[tt % 3][:], xnT_k_t(k, tt),
                                   slotG[:, k, :], start=(k == 0),
                                   stop=(k == KC - 1))
                S_.bump(ins, "kk_mm")

        @block.vector
        def _(v):
            for tt in range(NT):
                S_.wait(v, "kk_mm", tt + 1)
                rope_kappa_front(v, pp[tt % 3], tt, "kk_pf")
                S_.bump(v.tensor_scalar(sc2[:, 0:512], sc1[:, 0:512], 0.0, None,
                                        ALU.min), "kk_m0")
                S_.wait(v, "kk_e", tt + 1)
                S_.bump(v.scalar_tensor_tensor(KkT[:, tt, :], sc1[:, 0:512], 0.0,
                                               sc3[:, 0:512], ALU.max, ALU.add),
                        "kk_kk")

        @block.scalar
        def _(s):
            for tt in range(NT):
                S_.wait(s, "kk_m0", tt + 1)
                if tt >= 1:
                    S_.wait(s, "kk_kk", tt)
                S_.bump(s.activation(sc3[:, 0:512], sc2[:, 0:512], AF.Exp),
                        "kk_e")

        @block.tensor
        def _(t):
            t.wait_ge(S_.h["s_w"], 32)
            S_.wait(t, "kk_pf", NT)
            S_.wait(t, "kk_kk", NT)
            for tt in range(NT):
                if tt >= 2:
                    S_.wait(t, "kv_vev", tt - 1)
                for k in range(KC):
                    ins = t.matmul(pp[tt % 3][:], xnT_k_t(k, tt),
                                   slotC[:, k, :], start=(k == 0),
                                   stop=(k == KC - 1))
                S_.bump(ins, "kv_mm")

        @block.scalar
        def _(s):
            for tt in range(NT):
                S_.wait(s, "kv_mm", tt + 1)
                S_.bump(s.copy(VT[:, tt, :], pp[tt % 3][:]), "kv_vev")

        barrier()
        if nph < 3:
            _finish()
            return nc

        QTH = [48, 64, 80, 96]
        QSL = [slotG, slotC, slotG, slotC]

        def _q_back(t, i, nown):
            Q, tt = divmod(i, NT)
            S_.wait(t, "q_qk", i + 1)
            ins = None
            for c in range(4):
                col = 4 * Q + c
                ins = t.matmul(psqg[:, col:col + 1],
                               qbuf[i % 2][:, c * P:(c + 1) * P], onesbf,
                               start=(tt == 0 and c == 0),
                               stop=(tt == NT - 1 and c == 3),
                               skip_group_check=True)
            if tt < OWN:
                S_.wait(t, "q_ev", max(0, nown - 1))
                for c in range(4):
                    t.transpose(tpb[nown % 4][:, c * P:(c + 1) * P],
                                qbuf[i % 2][:, c * P:(c + 1) * P],
                                identbf)
                S_.bump(t.drain(), "q_tp")
                nown += 1
                S_.bump(t.nop(), "q_qg")
            else:
                S_.bump(ins, "q_qg")
            return nown

        @block.tensor
        def _(t):
            nown = 0
            for Q in range(4):
                t.wait_ge(S_.h["s_w"], QTH[Q])
                for tt in range(NT):
                    i = Q * NT + tt
                    if i < 2:
                        S_.wait(t, "kv_vev", 14 + i)
                    else:
                        S_.wait(t, "q_pf", i - 1)
                    for k in range(KC):
                        ins = t.matmul(pp[(i + 1) % 3][:], xnT_k_t(k, tt),
                                       QSL[Q][:, k, :], start=(k == 0),
                                       stop=(k == KC - 1))
                    S_.bump(ins, "q_mm")
                    if i >= 1:
                        nown = _q_back(t, i - 1, nown)
            _q_back(t, 4 * NT - 1, nown)

        @block.vector
        def _(v):
            nown = 0
            for Q in range(4):
                for tt in range(NT):
                    i = Q * NT + tt
                    S_.wait(v, "q_mm", i + 1)
                    rope_kappa_front(v, pp[(i + 1) % 3], tt, "q_pf")
                    S_.bump(v.tensor_scalar(sc2[:, 0:512], sc1[:, 0:512], 0.0,
                                            None, ALU.min), "q_m0")
                    S_.wait(v, "q_e", i + 1)
                    S_.wait(v, "q_qg", max(0, i - 1))
                    S_.bump(v.scalar_tensor_tensor(qbuf[i % 2], sc1[:, 0:512],
                                                   0.0, sc3[:, 0:512], ALU.max,
                                                   ALU.add), "q_qk")
                    if tt < OWN:
                        S_.wait(v, "q_tp", nown + 1)
                        dst = QkF[:, 4 * Q:4 * Q + 4, tt * P:tt * P + P]
                        src = tpb[nown % 4].ap()[:, 0:512] \
                            .rearrange("p (c n) -> p c n", n=P)
                        S_.bump(v.tensor_copy(dst, src), "q_ev")
                        nown += 1

        @block.scalar
        def _(s):
            for i in range(4 * NT):
                S_.wait(s, "q_m0", i + 1)
                S_.wait(s, "q_qk", i)
                S_.bump(s.activation(sc3[:, 0:512], sc2[:, 0:512], AF.Exp),
                        "q_e")
            s.wait_ge(S_.h["q_qg"], 4 * NT)
            s.copy(QgF, psqg[:])
            s.mul(QgF, QgF, 1.0 / float(S))

        barrier()
        if nph < 4:
            _finish()
            return nc
        psA.pop_all().close()

        psB = contextlib.ExitStack()
        pslg = psB.enter_context(nc.psum_tensor("pslg", [P, 2048], F32))
        psal = [psB.enter_context(nc.psum_tensor(f"psal{i}", [P, 512], F32))
                for i in range(2)]
        tpk = psB.enter_context(nc.psum_tensor("tpk", [P, 2048], BF16))

        def _a_tp(t, g, jb):
            m = 4 * g + jb
            S_.wait(t, "b_al", g + 1)
            S_.wait(t, "b_aev", m)
            for i in range(4):
                tt = 4 * jb + i
                t.transpose(psal[m % 2][:, i * P:(i + 1) * P],
                            alpha_pad[:, tt * P:(tt + 1) * P], identf)
            S_.bump(t.drain(), "b_atp")

        @block.tensor
        def _(t):
            for j in range(8):
                S_.wait(t, "b_tke", j)
                base = 1024 * (j % 2)
                for half in range(2):
                    tt = 2 * j + half
                    for c in range(4):
                        off = base + half * 512 + c * P
                        t.transpose(tpk[:, off:off + P],
                                    KkT[:, tt, c * P:(c + 1) * P],
                                    identbf)
                S_.bump(t.drain(), "b_tpk")
            S_.wait(t, "b_bd", 1)
            S_.wait(t, "b_tke", 8)
            for g in range(4):
                S_.wait(t, "b_ex", g)
                for st in range(4):
                    ins = t.matmul(pslg[:, 512 * st:512 * st + 512],
                                   qgbdp[:, 128 * g:128 * g + 128],
                                   KkF[:, g, 512 * st:512 * st + 512],
                                   start=True, stop=True)
                S_.bump(ins, "b_lg")
                if g >= 1:
                    for jb in range(4):
                        _a_tp(t, g - 1, jb)
            for jb in range(4):
                _a_tp(t, 3, jb)

        @block.vector
        def _(v):
            for j in range(8):
                S_.wait(v, "b_tpk", j + 1)
                base = 1024 * (j % 2)
                for half in range(2):
                    tt = 2 * j + half
                    dst = KkF[:, 0:4, tt * P:tt * P + P]
                    srcv = tpk.ap()[:, base + half * 512:base + half * 512 + 512] \
                        .rearrange("p (c n) -> p c n", n=P)
                    ins = v.tensor_copy(dst, srcv)
                S_.bump(ins, "b_tke")
            v.memset(qgbdp, 0.0)
            v.memset(alpha_pad, 0.0)
            ins = None
            for g in range(4):
                for j in range(8):
                    h = 8 * g + j
                    par = h % 2
                    srcq = QgF[64 * par:64 * par + 64, h // 2:h // 2 + 1]
                    dst = qgbdp[64 * (j // 4):64 * (j // 4) + 64,
                                128 * g + j:128 * g + j + 1]
                    ins = v.tensor_copy(dst, srcq)
            S_.bump(ins, "b_bd")
            for g in range(4):
                S_.wait(v, "b_lg", g + 1)
                S_.bump(v.reduce_max(smalls[0:8, 8:9], pslg[0:8, :], axis=AX.X,
                                     negate=True), "b_mx")
                S_.wait(v, "b_ex", g + 1)
                S_.bump(v.reciprocal(smalls[0:8, 10:11], smalls[0:8, 9:10]),
                        "v_rc")
                S_.wait(v, "v_rc", S_.c["v_rc"])
                S_.bump(v.tensor_scalar(alpha_fm, alpha_fm, smalls[0:8, 10:11],
                                        float(S), ALU.mult, ALU.mult), "b_al")
                for jb in range(4):
                    m = 4 * g + jb
                    S_.wait(v, "b_atp", m + 1)
                    srca = psal[m % 2].ap() \
                        .rearrange("p (i n) -> p i n", n=P)[:, :, 0:8]
                    dstv = alpha_tm.rearrange("p (t c) -> p t c", c=32)[
                        :, 4 * jb:4 * jb + 4, 8 * g:8 * g + 8]
                    S_.bump(v.tensor_copy(dstv, srca), "b_aev")

        @block.scalar
        def _(s):
            for g in range(4):
                S_.wait(s, "b_mx", g + 1)
                if g >= 1:
                    S_.wait(s, "b_al", g)
                    S_.wait(s, "b_atp", 4 * g)
                s.activation(alpha_fm, pslg[0:8, :], AF.Exp,
                             bias=smalls[0:8, 8:9],
                             accum_out=smalls[0:8, 9:10])
                S_.bump(s.drain(), "b_ex")

        barrier()
        if nph < 5:
            _finish()
            return nc
        psB.pop_all().close()

        psC = contextlib.ExitStack()
        psot = psC.enter_context(nc.psum_tensor("psot", [P, 1024], F32))

        @block.vector
        def _(v):
            v.memset(outer_pad, 0.0)
            for tt in range(NT):
                if tt >= 2:
                    S_.wait(v, "c_mm", tt - 1)
                kk = KkT[:, tt, :].rearrange("p (h d) -> p h d", d=HD) \
                    .unsqueeze(2).broadcast_to([P, 8, 4, HD])
                al = alpha_tm[:, tt * 32:(tt + 1) * 32] \
                    .rearrange("p (h j) -> p h j", j=4) \
                    .unsqueeze(3).broadcast_to([P, 8, 4, HD])
                dst = akbuf[tt % 2].rearrange("p (h j d) -> p h j d", j=4, d=HD)
                S_.bump(v.tensor_tensor(dst, kk, al, ALU.mult), "c_ak")

        @block.tensor
        def _(t):
            for tt in range(NT):
                S_.wait(t, "c_ak", tt + 1)
                for j in range(16):
                    ins = t.matmul(
                        psot[:, 64 * j:64 * j + 64],
                        akbuf[tt % 2][:, 128 * j:128 * j + 128],
                        VT[:, tt, 64 * (j // 2):64 * (j // 2) + 64],
                        start=(tt == 0 and j in (0, 8)),
                        stop=(tt == NT - 1), skip_group_check=True)
                S_.bump(ins, "c_mm")

        @block.scalar
        def _(s):
            S_.wait(s, "c_mm", NT)
            ope = outer_pad[0:64, :].rearrange("p (h e d) -> p h e d",
                                               e=2, d=HD)[:, :, 0, :]
            opo = outer_pad[64:128, :].rearrange("p (h e d) -> p h e d",
                                                 e=2, d=HD)[:, :, 1, :]
            s.copy(ope, psot[0:64, :].rearrange("p (h d) -> p h d", d=HD))
            S_.bump(s.copy(opo, psot[64:128, :]
                           .rearrange("p (h d) -> p h d", d=HD)), "c_oev")

        barrier()
        if nph < 6:
            _finish()
            return nc
        psC.pop_all().close()

        psD = contextlib.ExitStack()
        psph = [psD.enter_context(nc.psum_tensor(f"psph{i}", [P, 1024], F32))
                for i in range(2)]
        psrs = [psD.enter_context(nc.psum_tensor(f"psrs{i}", [P, 1024], F32))
                for i in range(2)]

        def _res_mm(t, m):
            S_.wait(t, "d_ctx", max(0, m - 1))
            S_.wait(t, "d_xev", m + 1)
            ins = None
            for st in range(2):
                ins = t.matmul(psrs[m % 2][:, 512 * st:512 * st + 512],
                               outer_pad[:, 128 * m:128 * m + 128],
                               QkF[:, m, 512 * st:512 * st + 512],
                               start=True, stop=True)
            S_.bump(ins, "d_res")

        @block.tensor
        def _(t):
            for m in range(KC):
                S_.wait(t, "s_w", 112 + 16 * (m // 4))
                S_.wait(t, "d_xev", max(0, m - 1))
                for k in range(KC):
                    for st in range(2):
                        ins = t.matmul(psph[m % 2][:, 512 * st:512 * st + 512],
                                       SS[m // 4][:, k,
                                                  128 * (m % 4):128 * (m % 4) + 128],
                                       xnT_own[:, k, 512 * st:512 * st + 512],
                                       start=(k == 0), stop=(k == KC - 1))
                S_.bump(ins, "d_phi")
                if m >= 1:
                    _res_mm(t, m - 1)
            _res_mm(t, KC - 1)

        @block.scalar
        def _(s):
            S_.wait(s, "c_oev", 1)
            for m in range(KC):
                S_.wait(s, "d_phi", m + 1)
                if m >= 2:
                    S_.wait(s, "d_ctx", m - 1)
                s.copy(xphi[m % 2], psph[m % 2][:])
                S_.bump(s.drain(), "d_xev")

        @block.vector
        def _(v):
            for m in range(KC):
                S_.wait(v, "d_res", m + 1)
                v.tensor_tensor(ctxb[:, m, :], psrs[m % 2][:],
                                xphi[m % 2], ALU.mult)
                S_.bump(v.drain(), "d_ctx")

        barrier()
        if nph < 7:
            _finish()
            return nc
        psD.pop_all().close()

        psbcS = contextlib.ExitStack()
        psbc = [psbcS.enter_context(nc.psum_tensor(f"psbc{i}", [P, 512], F32))
                for i in range(2)]

        @block.sync
        def _(q):
            S_.wait(q, "d_ctx", KC)
            S_.bump(q.dma_start(out=scale_bc[0:1, :], in_=scale_d), "s_x", 16)
        SCROW = S_.c["s_x"]

        @block.vector
        def _(v):
            v.memset(scale_bc, 0.0)
            v.memset(wbuf[:, 0:128], 1.0)
            S_.bump(v.nop(), "f_bc2")
            for st in range(4):
                S_.wait(v, "f_bc", st + 1)
                S_.bump(v.tensor_copy(scale_bc[:, 512 * st:512 * st + 512],
                                      psbc[st % 2][:]), "f_bc2")
            for tt in range(OWN):
                S_.wait(v, "f_yev", OWN + tt + 1)
                S_.bump(v.tensor_tensor(wbuf, ysb[:, tt, :], scale_bc,
                                        ALU.mult), "f_v")
                S_.wait(v, "f_a", 2 * tt + 1)
                S_.bump(v.tensor_scalar(smalls[:, 17:18], smalls[:, 16:17],
                                        1e-14, None, ALU.max), "f_v")
                S_.wait(v, "f_a", 2 * tt + 2)
                v.tensor_scalar(smalls[:, 22:23], smalls[:, 18:19], 1e-7,
                                None, ALU.add)
                v.reciprocal(smalls[:, 26:27], smalls[:, 18:19])
                v.reciprocal(smalls[:, 24:25], smalls[:, 22:23])
                v.tensor_scalar(smalls[:, 28:29], smalls[:, 26:27], 0.99,
                                None, ALU.mult)
                S_.bump(v.tensor_tensor(smalls[:, 25:26], smalls[:, 19:20],
                                        smalls[:, 24:25], ALU.mult),
                        "v_rc")
                S_.wait(v, "v_rc", S_.c["v_rc"])
                S_.bump(v.tensor_tensor(smalls[:, 29:30], smalls[:, 25:26],
                                        smalls[:, 28:29], ALU.min),
                        "v_rc")
                S_.wait(v, "v_rc", S_.c["v_rc"])
                S_.bump(v.tensor_scalar(ysb[:, tt, :], wbuf, smalls[:, 29:30],
                                        None, ALU.mult), "f_v")

        @block.tensor
        def _(t):
            t.wait_ge(S_.h["s_x"], SCROW)
            for st in range(4):
                S_.wait(t, "f_bc2", st + 1)
                S_.bump(t.matmul(psbc[st % 2][:], wbuf[:, 0:128],
                                 scale_bc[:, 512 * st:512 * st + 512],
                                 start=True, stop=True), "f_bc")

        psbcS.pop_all().close()
        psE = contextlib.ExitStack()
        psf = [psE.enter_context(nc.psum_tensor(f"psf{i}", [P, 1024], F32))
               for i in range(4)]

        @block.tensor
        def _(t):
            for half in range(2):
                t.wait_ge(S_.h["s_w"], 192 + 32 * half)
                t.wait_ge(S_.h["f_bc2"], 5)
                for tt in range(OWN):
                    i = half * OWN + tt
                    if i >= 3:
                        S_.wait(t, "f_yev", i - 2)
                    for k in range(KC):
                        for st in range(2):
                            sl = SS[2 * half + st]
                            ins = t.matmul(psf[i % 4][:, 512 * st:512 * st + 512],
                                           xnT_own[:, k, tt * P:(tt + 1) * P],
                                           sl[:, k, :], start=(k == 0),
                                           stop=(k == KC - 1))
                    S_.bump(ins, "f_mm")

        def _epi_act(s, tt):
            S_.wait(s, "f_v", 3 * tt + 1)
            s.activation(dump, wbuf, AF.Square,
                         accum_out=smalls[:, 16:17])
            S_.bump(s.drain(), "f_a")
            S_.wait(s, "f_v", 3 * tt + 2)
            s.activation(smalls[:, 31:32], smalls[:, 17:18], AF.Ln)
            s.activation(smalls[:, 18:19], smalls[:, 31:32], AF.Exp,
                         scale=0.5)
            S_.bump(s.activation(smalls[:, 19:20], smalls[:, 18:19], AF.Tanh),
                    "f_a")
            S_.wait(s, "f_v", 3 * tt + 3)
            S_.bump(s.activation(ysb[:, tt, :], ysb[:, tt, :], AF.Silu),
                    "g_si")

        @block.scalar
        def _(s):
            for tt in range(OWN):
                S_.wait(s, "f_mm", tt + 1)
                S_.bump(s.copy(ysb[:, tt, 0:1024], psf[tt % 4][:]), "f_yev")
            for tt in range(OWN):
                i = OWN + tt
                S_.wait(s, "f_mm", i + 1)
                cp = s.copy(ysb[:, tt, 1024:2048], psf[i % 4][:])
                if tt == OWN - 1:
                    S_.bump(s.drain(), "f_yev")
                else:
                    S_.bump(cp, "f_yev")
            for tt in range(OWN):
                _epi_act(s, tt)

        if nph < 8:
            barrier()
            _finish()
            return nc
        psE.pop_all().close()

        psF = contextlib.ExitStack()
        pswo = [psF.enter_context(nc.psum_tensor(f"pswo{i}", [P, 1024], F32))
                for i in range(4)]

        @block.tensor
        def _(t):
            t.wait_ge(S_.h["f_yev"], 2 * OWN)
            t.wait_ge(S_.h["f_bc2"], 5)
            for wh in range(2):
                t.wait_ge(S_.h["s_w"], 256 + 32 * wh)
                for tt in range(OWN):
                    i = wh * OWN + tt
                    if i >= 4:
                        S_.wait(t, "wo_fin", i - 3)
                    for m in range(KC):
                        for st in range(2):
                            sl = SS[2 * wh + st]
                            ins = t.matmul(pswo[i % 4][:, 512 * st:512 * st + 512],
                                           ctxb[:, m, tt * P:(tt + 1) * P],
                                           sl[:, m, :], start=(m == 0),
                                           stop=(m == KC - 1))
                    S_.bump(ins, "wo_mm")

        @block.sync
        def _(q):
            S_.wait(q, "f_mm", 2 * OWN)
            for i in range(2 * OWN):
                wh, tt = divmod(i, OWN)
                if i >= 2:
                    S_.wait(q, "wo_fin", i - 1)
                S_.bump(q.dma_start(
                    out=xrel[i % 2],
                    in_=x_d[tt * P:(tt + 1) * P,
                            wh * 1024:wh * 1024 + 1024]), "s_x", 16)
                if i >= 1:
                    S_.wait(q, "wo_fin", i)
                    wh2, tt2 = divmod(i - 1, OWN)
                    S_.bump(q.dma_start(
                        out=out_d[tt2 * P:(tt2 + 1) * P,
                                  wh2 * 1024:wh2 * 1024 + 1024],
                        in_=outst[(i - 1) % 2]), "s_out", 16)
            S_.wait(q, "wo_fin", 2 * OWN)
            S_.bump(q.dma_start(
                out=out_d[(OWN - 1) * P:OWN * P, 1024:2048],
                in_=outst[(2 * OWN - 1) % 2]), "s_out", 16)
            q.wait_ge(S_.h["s_out"], 16 * 2 * OWN)

        XRBASE = SCROW

        @block.vector
        def _(v):
            for i in range(2 * OWN):
                wh, tt = divmod(i, OWN)
                S_.wait(v, "g_si", tt + 1)
                S_.wait(v, "wo_mm", i + 1)
                S_.wait(v, "s_x", XRBASE + 16 * (i + 1))
                if i >= 2:
                    S_.wait(v, "s_out", 16 * (i - 1))
                v.tensor_tensor(sgbuf, pswo[i % 4][:],
                                ysb[:, tt, wh * 1024:wh * 1024 + 1024],
                                ALU.mult)
                S_.bump(v.tensor_tensor(outst[i % 2], sgbuf, xrel[i % 2],
                                        ALU.add), "wo_fin")

        psF.pop_all().close()

    return nc


def kernel(L0_attn_hidden_states, L0_attn_Wq, L0_attn_Wk, L0_attn_Wv,
           L0_attn_Wphi, L0_attn_Wo, L0_ffn_weight, L0_ffn_scale):
    x = np.ascontiguousarray(np.asarray(L0_attn_hidden_states, dtype=np.float32))
    wq = np.ascontiguousarray(np.asarray(L0_attn_Wq, dtype=np.float32))
    wk = np.ascontiguousarray(np.asarray(L0_attn_Wk, dtype=np.float32))
    wv = np.ascontiguousarray(np.asarray(L0_attn_Wv, dtype=np.float32))
    wphi = np.ascontiguousarray(np.asarray(L0_attn_Wphi, dtype=np.float32))
    wo = np.ascontiguousarray(np.asarray(L0_attn_Wo, dtype=np.float32))
    wffn = np.ascontiguousarray(np.asarray(L0_ffn_weight, dtype=np.float32))
    scale = np.ascontiguousarray(
        np.asarray(L0_ffn_scale, dtype=np.float32)).reshape(1, D)

    identbf = np.eye(P, dtype=ml_dtypes.bfloat16)
    identf = np.eye(P, dtype=np.float32)
    onesbf = np.ones((P, 1), dtype=ml_dtypes.bfloat16)
    onesf = np.ones((1, P), dtype=np.float32)
    consts = np.zeros((P, 8), dtype=np.float32)
    consts[:, 0] = 1e-5

    inv = 1.0 / (10000.0 ** (np.arange(0, HD, 2, dtype=np.float64) / HD))
    tables = []
    for h in (0, 1):
        pos = (np.arange(S) + 1024 * h) % S
        ang = pos[:, None].astype(np.float64) * inv[None, :]
        cos = np.concatenate([np.cos(ang), np.cos(ang)], axis=1)
        sinp = np.concatenate([-np.sin(ang), np.sin(ang)], axis=1)
        cosr = cos.reshape(NT, P, HD).transpose(1, 0, 2).reshape(P, NT * HD)
        sinr = sinp.reshape(NT, P, HD).transpose(1, 0, 2).reshape(P, NT * HD)
        tables.append((np.ascontiguousarray(cosr.astype(np.float32)),
                       np.ascontiguousarray(sinr.astype(np.float32))))

    nc = _build()
    in_maps = []
    for c in range(8):
        b, h = divmod(c, 2)
        xc = np.ascontiguousarray(np.roll(x[b], -1024 * h, axis=0))
        in_maps.append({
            "x": xc, "wq": wq, "wk": wk, "wv": wv, "wphi": wphi,
            "wffn": wffn, "wo": wo, "scale": scale,
            "cosr": tables[h][0], "sinr": tables[h][1],
            "identbf": identbf, "identf": identf, "onesbf": onesbf,
            "onesf": onesf, "consts": consts,
        })
    res = run_bass_kernel_spmd(nc, in_maps, core_ids=list(range(8)))
    out = np.empty((4, S, D), dtype=np.float32)
    for c in range(8):
        b, h = divmod(c, 2)
        out[b, 1024 * h:1024 * h + 1024, :] = res.results[c]["out"]
    return out


# version 12
# speedup vs baseline: 1.0761x; 1.0220x over previous
import contextlib

import numpy as np
import ml_dtypes

import concourse.bass as bass
import concourse.mybir as mybir
from concourse.bass_utils import run_bass_kernel_spmd

F32 = mybir.dt.float32
BF16 = mybir.dt.bfloat16
AF = mybir.ActivationFunctionType
ALU = mybir.AluOpType
AX = mybir.AxisListType

P = 128
S = 2048
D = 2048
NT = 16
OWN = 8
HD = 64
KC = 16

O_IDENTBF = 0
O_IDENTF = 256
O_ONESBF = 768
O_ONESF = 800
O_CONSTS = 1312
O_SMALLS = 1376
O_LNST = 1632
O_QGF = 1760
O_QGBD = 1888
O_ALPHT = 2912
O_ALPHF = 4960

O_A = 16128
O_B = 48896
O_C = 81664
O_D = 114432
O_E = 147200
O_F = 179968
O_G = 196352
ARENA = 212736

OC_XT0, OC_XT1 = O_C, O_C + 8192
OC_XNB0, OC_XNB1 = O_C + 16384, O_C + 20480
OC_SC1, OC_SC2, OC_SC3 = O_C, O_C + 4096, O_C + 8192
OC_QB0, OC_QB1 = O_C + 12288, O_C + 13312
OC_WS = O_C + 14848
OD_W, OD_W2, OD_DUMP, OD_SCB = O_D, O_D + 8192, O_D + 16384, O_D + 20480
OE_XR0, OE_XR1 = O_A, O_A + 4096
OE_OS0, OE_OS1 = O_A + 8192, O_A + 12288
OF_COS, OF_SIN = O_F, O_F + 4096
OF_KKF = O_F
OF_AK0, OF_AK1 = O_F, O_F + 4096
OF_XP0, OF_XP1 = O_F, O_F + 2048
O_YSB = O_F


class SemL:
    def __init__(self, ctx, nc, names):
        self.h = {n: ctx.enter_context(nc.semaphore(n)) for n in names}
        self.c = {n: 0 for n in names}

    def bump(self, instr, name, amt=1):
        instr.then_inc(self.h[name], amt)
        self.c[name] += amt
        return self.c[name]

    def wait(self, eng, name, v):
        if v > 0:
            eng.wait_ge(self.h[name], v)


SEMS = [
    "s_x", "s_w", "s_out", "s_bar",
    "p1_st", "p1_mv", "p1_rs", "p1_ln", "p1_tp", "p1_ev",
    "kk_mm", "kk_pf", "kk_m0", "kk_e", "kk_kk",
    "kv_mm", "kv_vev",
    "q_mm", "q_pf", "q_m0", "q_e", "q_qk", "q_qg", "q_tp", "q_ev",
    "b_tpk", "b_tke", "b_bd", "b_lg", "b_mx", "b_ex", "b_al", "b_atp", "b_aev",
    "c_ak", "c_mm", "c_oev",
    "d_phi", "d_xev", "d_res", "d_ctx",
    "f_mm", "f_yev", "f_v", "f_a", "f_bc", "f_bc2",
    "g_si", "wo_mm", "wo_fin", "v_rc",
]


def _build(nph=99):
    nc = bass.Bass("TRN2", target_bir_lowering=False)

    x_d = nc.dram_tensor("x", [S, D], F32, kind="ExternalInput").ap()
    wq_d = nc.dram_tensor("wq", [D, D], F32, kind="ExternalInput").ap()
    wk_d = nc.dram_tensor("wk", [D, 512], F32, kind="ExternalInput").ap()
    wv_d = nc.dram_tensor("wv", [D, 512], F32, kind="ExternalInput").ap()
    wphi_d = nc.dram_tensor("wphi", [D, D], F32, kind="ExternalInput").ap()
    wffn_d = nc.dram_tensor("wffn", [D, D], F32, kind="ExternalInput").ap()
    wo_d = nc.dram_tensor("wo", [D, D], F32, kind="ExternalInput").ap()
    scale_d = nc.dram_tensor("scale", [1, D], F32, kind="ExternalInput").ap()
    cosr_d = nc.dram_tensor("cosr", [P, NT * HD], F32, kind="ExternalInput").ap()
    sinr_d = nc.dram_tensor("sinr", [P, NT * HD], F32, kind="ExternalInput").ap()
    identbf_d = nc.dram_tensor("identbf", [P, P], BF16, kind="ExternalInput").ap()
    identf_d = nc.dram_tensor("identf", [P, P], F32, kind="ExternalInput").ap()
    onesbf_d = nc.dram_tensor("onesbf", [P, 1], BF16, kind="ExternalInput").ap()
    onesf_d = nc.dram_tensor("onesf", [1, P], F32, kind="ExternalInput").ap()
    consts_d = nc.dram_tensor("consts", [P, 8], F32, kind="ExternalInput").ap()
    out_d = nc.dram_tensor("out", [S // 2, D], F32, kind="ExternalOutput").ap()

    wq_v = wq_d.rearrange("(k p) n -> p k n", p=P)
    wk_v = wk_d.rearrange("(k p) n -> p k n", p=P)
    wv_v = wv_d.rearrange("(k p) n -> p k n", p=P)
    wphi_v = wphi_d.rearrange("(k p) n -> p k n", p=P)
    wffn_v = wffn_d.rearrange("(k p) n -> p k n", p=P)
    wo_v = wo_d.rearrange("(k p) n -> p k n", p=P)

    ctx = contextlib.ExitStack()
    with ctx:
        ar = ctx.enter_context(nc.sbuf_tensor("arena", [P, ARENA], mybir.dt.uint8))

        def vw(off, dt, nelem, rows=None):
            nb = nelem * mybir.dt.size(dt)
            a = ar[:, off:off + nb] if rows is None else ar[0:rows, off:off + nb]
            return a.bitcast(dt)

        identbf = vw(O_IDENTBF, BF16, 128)
        identf = vw(O_IDENTF, F32, 128)
        onesbf = vw(O_ONESBF, BF16, 1)
        onesf = vw(O_ONESF, F32, 128, rows=1)
        constsv = vw(O_CONSTS, F32, 8)
        smalls = vw(O_SMALLS, F32, 64)
        lnst = vw(O_LNST, F32, 24)
        QgF = vw(O_QGF, F32, 16)
        qgbdp = vw(O_QGBD, BF16, 512)
        alpha_tm = vw(O_ALPHT, F32, 512)
        alpha_pad = vw(O_ALPHF, F32, 2048)
        alpha_fm = vw(O_ALPHF, F32, 2048, rows=8)
        outer_pad = vw(O_ALPHF, BF16, 2048)

        xnT_own = vw(O_A, BF16, KC * 1024).rearrange("p (k n) -> p k n", n=1024)
        xnT_oth = vw(O_B, BF16, KC * 1024).rearrange("p (k n) -> p k n", n=1024)
        QkF = vw(O_D, BF16, KC * 1024).rearrange("p (c n) -> p c n", n=1024)
        KkT = vw(O_E, BF16, NT * 512).rearrange("p (t n) -> p t n", n=512)
        VT = vw(O_E + 16384, BF16, NT * 512).rearrange("p (t n) -> p t n", n=512)
        KkF = vw(OF_KKF, BF16, 4 * 2048).rearrange("p (c n) -> p c n", n=2048)
        ctxb = vw(O_E, BF16, KC * 1024).rearrange("p (c n) -> p c n", n=1024)
        ysb = vw(O_YSB, BF16, OWN * D).rearrange("p (t n) -> p t n", n=D)

        xt = [vw(OC_XT0, F32, D), vw(OC_XT1, F32, D)]
        xnb = [vw(OC_XNB0, BF16, D), vw(OC_XNB1, BF16, D)]
        sc1 = vw(OC_SC1, F32, 1024)
        sc2 = vw(OC_SC2, F32, 1024)
        sc3 = vw(OC_SC3, F32, 1024)
        qbuf = [vw(OC_QB0, BF16, 512), vw(OC_QB1, BF16, 512)]
        slotG = vw(O_G, BF16, KC * 512).rearrange("p (k n) -> p k n", n=512)
        slotC = vw(OC_WS, BF16, KC * 512).rearrange("p (k n) -> p k n", n=512)
        S0 = vw(O_B, BF16, KC * 512).rearrange("p (k n) -> p k n", n=512)
        S1 = vw(O_B + 16384, BF16, KC * 512).rearrange("p (k n) -> p k n", n=512)
        S2 = vw(O_C, BF16, KC * 512).rearrange("p (k n) -> p k n", n=512)
        S3 = vw(O_C + 16384, BF16, KC * 512).rearrange("p (k n) -> p k n", n=512)
        SS = [S0, S1, S2, S3]
        cosr = vw(OF_COS, F32, NT * HD)
        sinr = vw(OF_SIN, F32, NT * HD)
        akbuf = [vw(OF_AK0, BF16, 2048), vw(OF_AK1, BF16, 2048)]
        xphi = [vw(OF_XP0, BF16, 1024), vw(OF_XP1, BF16, 1024)]
        wbuf = vw(OD_W, F32, D)
        w2buf = vw(OD_W2, F32, D)
        dump = vw(OD_DUMP, BF16, D)
        scale_bc = vw(OD_SCB, F32, D)
        sgbuf = vw(OD_W, F32, 1024)
        xrel = [vw(OE_XR0, F32, 1024), vw(OE_XR1, F32, 1024)]
        outst = [vw(OE_OS0, F32, 1024), vw(OE_OS1, F32, 1024)]

        S_ = SemL(ctx, nc, SEMS)
        block = ctx.enter_context(nc.Block())
        nbar = [0]

        def barrier():
            nbar[0] += 3

            @block.tensor
            def _(t):
                t.drain().then_inc(S_.h["s_bar"], 1)
                t.wait_ge(S_.h["s_bar"], nbar[0])

            @block.vector
            def _(v):
                v.drain().then_inc(S_.h["s_bar"], 1)
                v.wait_ge(S_.h["s_bar"], nbar[0])

            @block.scalar
            def _(s):
                s.drain().then_inc(S_.h["s_bar"], 1)
                s.wait_ge(S_.h["s_bar"], nbar[0])

        def _finish():
            @block.sync
            def _(q):
                S_.wait(q, "s_bar", nbar[0])
                for r in range(8):
                    S_.bump(q.dma_start(
                        out=out_d[r * P:(r + 1) * P, :],
                        in_=ar[:, O_A:O_A + D * 4].bitcast(F32)), "s_out", 16)
                q.wait_ge(S_.h["s_out"], S_.c["s_out"])

        def xnT_k_t(k, t):
            if t < OWN:
                return xnT_own[:, k, t * P:(t + 1) * P]
            return xnT_oth[:, k, (t - OWN) * P:(t - OWN) * P + P]

        def rope_kappa_front(v, ps, t, pf_sem):
            ps3 = ps.ap()[:, 0:512].rearrange("p (h d) -> p h d", d=HD)
            cosb = cosr[:, t * HD:(t + 1) * HD].unsqueeze(1) \
                .broadcast_to([P, 8, HD])
            sin_a = sinr[:, t * HD:t * HD + 32].unsqueeze(1) \
                .broadcast_to([P, 8, 32])
            sin_b = sinr[:, t * HD + 32:t * HD + 64].unsqueeze(1) \
                .broadcast_to([P, 8, 32])
            s1 = sc1[:, 0:512].rearrange("p (h d) -> p h d", d=HD)
            s2 = sc2[:, 0:512].rearrange("p (h d) -> p h d", d=HD)
            v.tensor_tensor(s1, ps3, cosb, ALU.mult)
            v.tensor_tensor(s2[:, :, 0:32], ps3[:, :, 32:64], sin_a, ALU.mult)
            S_.bump(v.tensor_tensor(s2[:, :, 32:64], ps3[:, :, 0:32], sin_b,
                                    ALU.mult), pf_sem)
            v.tensor_tensor(sc1[:, 0:512], sc1[:, 0:512], sc2[:, 0:512], ALU.add)

        psA = contextlib.ExitStack()
        tpb = [psA.enter_context(nc.psum_tensor(f"tpb{i}", [P, 1024], BF16))
               for i in range(4)]
        psqg = psA.enter_context(nc.psum_tensor("psqg", [P, 16], F32))
        pp = [psA.enter_context(nc.psum_tensor(f"pp{i}", [P, 512], F32))
              for i in range(3)]

        @block.sync
        def _(q):
            S_.bump(q.dma_start(out=identbf, in_=identbf_d), "s_x", 16)
            S_.bump(q.dma_start(out=identf, in_=identf_d), "s_x", 16)
            S_.bump(q.dma_start(out=onesbf, in_=onesbf_d), "s_x", 16)
            S_.bump(q.dma_start(out=onesf, in_=onesf_d), "s_x", 16)
            S_.bump(q.dma_start(out=constsv, in_=consts_d), "s_x", 16)
            S_.bump(q.dma_start(out=cosr, in_=cosr_d), "s_x", 16)
            S_.bump(q.dma_start(out=sinr, in_=sinr_d), "s_x", 16)
            for t in range(NT):
                if t >= 2:
                    S_.wait(q, "p1_ln", t - 1)
                S_.bump(q.dma_start(out=xt[t % 2],
                                    in_=x_d[t * P:(t + 1) * P, :]), "s_x", 16)
        XBASE = 7 * 16

        @block.gpsimd
        def _(g):
            if nph >= 2:
                S_.bump(g.dma_start(out=slotG, in_=wk_v), "s_w", 16)
                S_.wait(g, "p1_tp", 2 * NT)
                S_.bump(g.dma_start(out=slotC, in_=wv_v), "s_w", 16)
            if nph >= 3:
                S_.wait(g, "kk_mm", NT)
                S_.bump(g.dma_start(out=slotG, in_=wq_v[:, :, 0:512]), "s_w", 16)
                S_.wait(g, "kv_mm", NT)
                S_.bump(g.dma_start(out=slotC, in_=wq_v[:, :, 512:1024]),
                        "s_w", 16)
                S_.wait(g, "q_mm", NT)
                S_.bump(g.dma_start(out=slotG, in_=wq_v[:, :, 1024:1536]),
                        "s_w", 16)
                S_.wait(g, "q_mm", 2 * NT)
                S_.bump(g.dma_start(out=slotC, in_=wq_v[:, :, 1536:2048]),
                        "s_w", 16)
            if nph >= 6:
                S_.wait(g, "q_mm", 4 * NT)
                S_.wait(g, "q_qg", 4 * NT)
                S_.wait(g, "q_qk", 4 * NT)
                for s in range(4):
                    S_.bump(g.dma_start(out=SS[s],
                                        in_=wphi_v[:, :, 512 * s:512 * s + 512]),
                            "s_w", 16)
            if nph >= 7:
                for s in range(4):
                    S_.wait(g, "d_phi", 4 * (s + 1))
                    S_.bump(g.dma_start(out=SS[s],
                                        in_=wffn_v[:, :, 512 * s:512 * s + 512]),
                            "s_w", 16)
            if nph >= 8:
                for s in range(4):
                    S_.wait(g, "f_mm", 8 * (s // 2 + 1))
                    S_.bump(g.dma_start(out=SS[s],
                                        in_=wo_v[:, :, 512 * s:512 * s + 512]),
                            "s_w", 16)

        def _ln_apply(v, t):
            par = t % 2
            S_.wait(v, "p1_rs", t + 1)
            S_.bump(v.tensor_scalar(xnb[t % 2], xt[t % 2],
                                    smalls[:, 4 * par:4 * par + 1],
                                    smalls[:, 4 * par + 2:4 * par + 3],
                                    ALU.subtract, ALU.mult), "p1_ln")
            for gidx in range(2):
                S_.wait(v, "p1_tp", 2 * t + gidx + 1)
                dst3 = xnT_own if t < OWN else xnT_oth
                tl = t if t < OWN else t - OWN
                dst = dst3[:, 8 * gidx:8 * gidx + 8, tl * P:tl * P + P]
                srcp = tpb[(t % 2) * 2 + gidx].ap() \
                    .rearrange("p (c n) -> p c n", n=P)
                S_.bump(v.tensor_copy(dst, srcp), "p1_ev")

        @block.vector
        def _(v):
            for t in range(NT):
                S_.wait(v, "s_x", XBASE + 16 * (t + 1))
                if t >= 2:
                    S_.wait(v, "p1_tp", 2 * (t - 2) + 2)
                par = t % 2
                lslot = lnst[:, 0:24] if par == 0 else smalls[:, 32:56]
                xb = xt[t % 2]
                for j in range(4):
                    ins = v.bn_stats(lslot[:, 6 * j:6 * j + 6],
                                     xb[:, 512 * j:512 * j + 512])
                S_.bump(ins, "p1_st")
                S_.wait(v, "p1_st", t + 1)
                S_.bump(v.bn_aggr(smalls[:, 4 * par:4 * par + 2], lslot),
                        "p1_mv")
                if t >= 1:
                    _ln_apply(v, t - 1)
            _ln_apply(v, NT - 1)

        @block.scalar
        def _(s):
            for t in range(NT):
                par = t % 2
                S_.wait(s, "p1_mv", t + 1)
                s.activation(smalls[:, 3:4], smalls[:, 4 * par + 1:4 * par + 2],
                             AF.Ln, bias=constsv[:, 0:1])
                S_.bump(s.activation(smalls[:, 4 * par + 2:4 * par + 3],
                                     smalls[:, 3:4], AF.Exp, scale=-0.5),
                        "p1_rs")

        @block.tensor
        def _(t):
            for tt in range(NT):
                S_.wait(t, "p1_ln", tt + 1)
                for gidx in range(2):
                    S_.wait(t, "p1_ev", max(0, 2 * (tt - 2) + gidx + 2))
                    for c in range(8):
                        cc = 8 * gidx + c
                        t.transpose(
                            tpb[(tt % 2) * 2 + gidx][:, c * P:(c + 1) * P],
                            xnb[tt % 2][:, cc * P:(cc + 1) * P], identbf)
                    S_.bump(t.drain(), "p1_tp")

        barrier()
        if nph < 2:
            _finish()
            return nc

        @block.tensor
        def _(t):
            t.wait_ge(S_.h["s_w"], 16)
            for tt in range(NT):
                if tt >= 2:
                    S_.wait(t, "kk_pf", tt - 1)
                for k in range(KC):
                    ins = t.matmul(pp[tt % 3][:], xnT_k_t(k, tt),
                                   slotG[:, k, :], start=(k == 0),
                                   stop=(k == KC - 1))
                S_.bump(ins, "kk_mm")

        @block.vector
        def _(v):
            for tt in range(NT):
                S_.wait(v, "kk_mm", tt + 1)
                rope_kappa_front(v, pp[tt % 3], tt, "kk_pf")
                S_.bump(v.tensor_scalar(sc2[:, 0:512], sc1[:, 0:512], 0.0, None,
                                        ALU.min), "kk_m0")
                S_.wait(v, "kk_e", tt + 1)
                S_.bump(v.scalar_tensor_tensor(KkT[:, tt, :], sc1[:, 0:512], 0.0,
                                               sc3[:, 0:512], ALU.max, ALU.add),
                        "kk_kk")

        @block.scalar
        def _(s):
            for tt in range(NT):
                S_.wait(s, "kk_m0", tt + 1)
                if tt >= 1:
                    S_.wait(s, "kk_kk", tt)
                S_.bump(s.activation(sc3[:, 0:512], sc2[:, 0:512], AF.Exp),
                        "kk_e")

        @block.tensor
        def _(t):
            t.wait_ge(S_.h["s_w"], 32)
            S_.wait(t, "kk_pf", NT)
            S_.wait(t, "kk_kk", NT)
            for tt in range(NT):
                if tt >= 2:
                    S_.wait(t, "kv_vev", tt - 1)
                for k in range(KC):
                    ins = t.matmul(pp[tt % 3][:], xnT_k_t(k, tt),
                                   slotC[:, k, :], start=(k == 0),
                                   stop=(k == KC - 1))
                S_.bump(ins, "kv_mm")

        @block.scalar
        def _(s):
            for tt in range(NT):
                S_.wait(s, "kv_mm", tt + 1)
                S_.bump(s.copy(VT[:, tt, :], pp[tt % 3][:]), "kv_vev")

        barrier()
        if nph < 3:
            _finish()
            return nc

        QTH = [48, 64, 80, 96]
        QSL = [slotG, slotC, slotG, slotC]

        def _q_back(t, i, nown):
            Q, tt = divmod(i, NT)
            S_.wait(t, "q_qk", i + 1)
            ins = None
            for c in range(4):
                col = 4 * Q + c
                ins = t.matmul(psqg[:, col:col + 1],
                               qbuf[i % 2][:, c * P:(c + 1) * P], onesbf,
                               start=(tt == 0 and c == 0),
                               stop=(tt == NT - 1 and c == 3),
                               skip_group_check=True)
            if tt < OWN:
                S_.wait(t, "q_ev", max(0, nown - 1))
                for c in range(4):
                    t.transpose(tpb[nown % 4][:, c * P:(c + 1) * P],
                                qbuf[i % 2][:, c * P:(c + 1) * P],
                                identbf)
                S_.bump(t.drain(), "q_tp")
                nown += 1
                S_.bump(t.nop(), "q_qg")
            else:
                S_.bump(ins, "q_qg")
            return nown

        @block.tensor
        def _(t):
            nown = 0
            for Q in range(4):
                t.wait_ge(S_.h["s_w"], QTH[Q])
                for tt in range(NT):
                    i = Q * NT + tt
                    if i < 2:
                        S_.wait(t, "kv_vev", 14 + i)
                    else:
                        S_.wait(t, "q_pf", i - 1)
                    for k in range(KC):
                        ins = t.matmul(pp[(i + 1) % 3][:], xnT_k_t(k, tt),
                                       QSL[Q][:, k, :], start=(k == 0),
                                       stop=(k == KC - 1))
                    S_.bump(ins, "q_mm")
                    if i >= 1:
                        nown = _q_back(t, i - 1, nown)
            _q_back(t, 4 * NT - 1, nown)

        @block.vector
        def _(v):
            nown = 0
            for Q in range(4):
                for tt in range(NT):
                    i = Q * NT + tt
                    S_.wait(v, "q_mm", i + 1)
                    rope_kappa_front(v, pp[(i + 1) % 3], tt, "q_pf")
                    S_.bump(v.tensor_scalar(sc2[:, 0:512], sc1[:, 0:512], 0.0,
                                            None, ALU.min), "q_m0")
                    S_.wait(v, "q_e", i + 1)
                    S_.wait(v, "q_qg", max(0, i - 1))
                    S_.bump(v.scalar_tensor_tensor(qbuf[i % 2], sc1[:, 0:512],
                                                   0.0, sc3[:, 0:512], ALU.max,
                                                   ALU.add), "q_qk")
                    if tt < OWN:
                        S_.wait(v, "q_tp", nown + 1)
                        dst = QkF[:, 4 * Q:4 * Q + 4, tt * P:tt * P + P]
                        src = tpb[nown % 4].ap()[:, 0:512] \
                            .rearrange("p (c n) -> p c n", n=P)
                        S_.bump(v.tensor_copy(dst, src), "q_ev")
                        nown += 1

        @block.scalar
        def _(s):
            for i in range(4 * NT):
                S_.wait(s, "q_m0", i + 1)
                S_.wait(s, "q_qk", i)
                S_.bump(s.activation(sc3[:, 0:512], sc2[:, 0:512], AF.Exp),
                        "q_e")
            s.wait_ge(S_.h["q_qg"], 4 * NT)
            s.copy(QgF, psqg[:])
            s.mul(QgF, QgF, 1.0 / float(S))

        barrier()
        if nph < 4:
            _finish()
            return nc
        psA.pop_all().close()

        psB = contextlib.ExitStack()
        pslg = psB.enter_context(nc.psum_tensor("pslg", [P, 2048], F32))
        psal = [psB.enter_context(nc.psum_tensor(f"psal{i}", [P, 512], F32))
                for i in range(2)]
        tpk = psB.enter_context(nc.psum_tensor("tpk", [P, 2048], BF16))

        def _a_tp(t, g, jb):
            m = 4 * g + jb
            S_.wait(t, "b_al", g + 1)
            S_.wait(t, "b_aev", m)
            for i in range(4):
                tt = 4 * jb + i
                t.transpose(psal[m % 2][:, i * P:(i + 1) * P],
                            alpha_pad[:, tt * P:(tt + 1) * P], identf)
            S_.bump(t.drain(), "b_atp")

        @block.tensor
        def _(t):
            for j in range(8):
                S_.wait(t, "b_tke", j)
                base = 1024 * (j % 2)
                for half in range(2):
                    tt = 2 * j + half
                    for c in range(4):
                        off = base + half * 512 + c * P
                        t.transpose(tpk[:, off:off + P],
                                    KkT[:, tt, c * P:(c + 1) * P],
                                    identbf)
                S_.bump(t.drain(), "b_tpk")
            S_.wait(t, "b_bd", 1)
            S_.wait(t, "b_tke", 8)
            for g in range(4):
                S_.wait(t, "b_ex", g)
                for st in range(4):
                    ins = t.matmul(pslg[:, 512 * st:512 * st + 512],
                                   qgbdp[:, 128 * g:128 * g + 128],
                                   KkF[:, g, 512 * st:512 * st + 512],
                                   start=True, stop=True)
                S_.bump(ins, "b_lg")
                if g >= 1:
                    for jb in range(4):
                        _a_tp(t, g - 1, jb)
            for jb in range(4):
                _a_tp(t, 3, jb)

        @block.vector
        def _(v):
            for j in range(8):
                S_.wait(v, "b_tpk", j + 1)
                base = 1024 * (j % 2)
                for half in range(2):
                    tt = 2 * j + half
                    dst = KkF[:, 0:4, tt * P:tt * P + P]
                    srcv = tpk.ap()[:, base + half * 512:base + half * 512 + 512] \
                        .rearrange("p (c n) -> p c n", n=P)
                    ins = v.tensor_copy(dst, srcv)
                S_.bump(ins, "b_tke")
            v.memset(qgbdp, 0.0)
            v.memset(alpha_pad, 0.0)
            ins = None
            for g in range(4):
                for j in range(8):
                    h = 8 * g + j
                    par = h % 2
                    srcq = QgF[64 * par:64 * par + 64, h // 2:h // 2 + 1]
                    dst = qgbdp[64 * (j // 4):64 * (j // 4) + 64,
                                128 * g + j:128 * g + j + 1]
                    ins = v.tensor_copy(dst, srcq)
            S_.bump(ins, "b_bd")
            for g in range(4):
                S_.wait(v, "b_lg", g + 1)
                S_.bump(v.reduce_max(smalls[0:8, 8:9], pslg[0:8, :], axis=AX.X,
                                     negate=True), "b_mx")
                S_.wait(v, "b_ex", g + 1)
                S_.bump(v.reciprocal(smalls[0:8, 10:11], smalls[0:8, 9:10]),
                        "v_rc")
                S_.wait(v, "v_rc", S_.c["v_rc"])
                S_.bump(v.tensor_scalar(alpha_fm, alpha_fm, smalls[0:8, 10:11],
                                        float(S), ALU.mult, ALU.mult), "b_al")
                for jb in range(4):
                    m = 4 * g + jb
                    S_.wait(v, "b_atp", m + 1)
                    srca = psal[m % 2].ap() \
                        .rearrange("p (i n) -> p i n", n=P)[:, :, 0:8]
                    dstv = alpha_tm.rearrange("p (t c) -> p t c", c=32)[
                        :, 4 * jb:4 * jb + 4, 8 * g:8 * g + 8]
                    S_.bump(v.tensor_copy(dstv, srca), "b_aev")

        @block.scalar
        def _(s):
            for g in range(4):
                S_.wait(s, "b_mx", g + 1)
                if g >= 1:
                    S_.wait(s, "b_al", g)
                    S_.wait(s, "b_atp", 4 * g)
                s.activation(alpha_fm, pslg[0:8, :], AF.Exp,
                             bias=smalls[0:8, 8:9],
                             accum_out=smalls[0:8, 9:10])
                S_.bump(s.drain(), "b_ex")

        barrier()
        if nph < 5:
            _finish()
            return nc
        psB.pop_all().close()

        psC = contextlib.ExitStack()
        psot = psC.enter_context(nc.psum_tensor("psot", [P, 1024], F32))

        @block.vector
        def _(v):
            v.memset(outer_pad, 0.0)
            for tt in range(NT):
                if tt >= 2:
                    S_.wait(v, "c_mm", tt - 1)
                kk = KkT[:, tt, :].rearrange("p (h d) -> p h d", d=HD) \
                    .unsqueeze(2).broadcast_to([P, 8, 4, HD])
                al = alpha_tm[:, tt * 32:(tt + 1) * 32] \
                    .rearrange("p (h j) -> p h j", j=4) \
                    .unsqueeze(3).broadcast_to([P, 8, 4, HD])
                dst = akbuf[tt % 2].rearrange("p (h j d) -> p h j d", j=4, d=HD)
                S_.bump(v.tensor_tensor(dst, kk, al, ALU.mult), "c_ak")

        @block.tensor
        def _(t):
            for tt in range(NT):
                S_.wait(t, "c_ak", tt + 1)
                for j in range(16):
                    ins = t.matmul(
                        psot[:, 64 * j:64 * j + 64],
                        akbuf[tt % 2][:, 128 * j:128 * j + 128],
                        VT[:, tt, 64 * (j // 2):64 * (j // 2) + 64],
                        start=(tt == 0 and j in (0, 8)),
                        stop=(tt == NT - 1), skip_group_check=True)
                S_.bump(ins, "c_mm")

        @block.scalar
        def _(s):
            S_.wait(s, "c_mm", NT)
            ope = outer_pad[0:64, :].rearrange("p (h e d) -> p h e d",
                                               e=2, d=HD)[:, :, 0, :]
            opo = outer_pad[64:128, :].rearrange("p (h e d) -> p h e d",
                                                 e=2, d=HD)[:, :, 1, :]
            s.copy(ope, psot[0:64, :].rearrange("p (h d) -> p h d", d=HD))
            S_.bump(s.copy(opo, psot[64:128, :]
                           .rearrange("p (h d) -> p h d", d=HD)), "c_oev")

        barrier()
        if nph < 6:
            _finish()
            return nc
        psC.pop_all().close()

        psD = contextlib.ExitStack()
        psph = [psD.enter_context(nc.psum_tensor(f"psph{i}", [P, 1024], F32))
                for i in range(2)]
        psrs = [psD.enter_context(nc.psum_tensor(f"psrs{i}", [P, 1024], F32))
                for i in range(2)]

        def _res_mm(t, m):
            S_.wait(t, "d_ctx", max(0, m - 1))
            S_.wait(t, "d_xev", m + 1)
            ins = None
            for st in range(2):
                ins = t.matmul(psrs[m % 2][:, 512 * st:512 * st + 512],
                               outer_pad[:, 128 * m:128 * m + 128],
                               QkF[:, m, 512 * st:512 * st + 512],
                               start=True, stop=True)
            S_.bump(ins, "d_res")

        @block.tensor
        def _(t):
            for m in range(KC):
                S_.wait(t, "s_w", 112 + 16 * (m // 4))
                S_.wait(t, "d_xev", max(0, m - 1))
                for k in range(KC):
                    for st in range(2):
                        ins = t.matmul(psph[m % 2][:, 512 * st:512 * st + 512],
                                       SS[m // 4][:, k,
                                                  128 * (m % 4):128 * (m % 4) + 128],
                                       xnT_own[:, k, 512 * st:512 * st + 512],
                                       start=(k == 0), stop=(k == KC - 1))
                S_.bump(ins, "d_phi")
                if m >= 1:
                    _res_mm(t, m - 1)
            _res_mm(t, KC - 1)

        @block.scalar
        def _(s):
            S_.wait(s, "c_oev", 1)
            for m in range(KC):
                S_.wait(s, "d_phi", m + 1)
                if m >= 2:
                    S_.wait(s, "d_ctx", m - 1)
                s.copy(xphi[m % 2], psph[m % 2][:])
                S_.bump(s.drain(), "d_xev")

        @block.vector
        def _(v):
            for m in range(KC):
                S_.wait(v, "d_res", m + 1)
                v.tensor_tensor(ctxb[:, m, :], psrs[m % 2][:],
                                xphi[m % 2], ALU.mult)
                S_.bump(v.drain(), "d_ctx")

        barrier()
        if nph < 7:
            _finish()
            return nc
        psD.pop_all().close()

        psbcS = contextlib.ExitStack()
        psbc = [psbcS.enter_context(nc.psum_tensor(f"psbc{i}", [P, 512], F32))
                for i in range(2)]

        @block.sync
        def _(q):
            S_.wait(q, "d_ctx", KC)
            S_.bump(q.dma_start(out=scale_bc[0:1, :], in_=scale_d), "s_x", 16)
        SCROW = S_.c["s_x"]

        @block.vector
        def _(v):
            v.memset(scale_bc, 0.0)
            v.memset(wbuf[:, 0:128], 1.0)
            S_.bump(v.nop(), "f_bc2")
            for st in range(4):
                S_.wait(v, "f_bc", st + 1)
                S_.bump(v.tensor_copy(scale_bc[:, 512 * st:512 * st + 512],
                                      psbc[st % 2][:]), "f_bc2")
            for tt in range(OWN):
                S_.wait(v, "f_yev", OWN + tt + 1)
                S_.bump(v.tensor_tensor(wbuf, ysb[:, tt, :], scale_bc,
                                        ALU.mult), "f_v")
                S_.wait(v, "f_a", 2 * tt + 1)
                S_.bump(v.tensor_scalar(smalls[:, 17:18], smalls[:, 16:17],
                                        1e-14, None, ALU.max), "f_v")
                S_.wait(v, "f_a", 2 * tt + 2)
                v.tensor_scalar(smalls[:, 22:23], smalls[:, 18:19], 1e-7,
                                None, ALU.add)
                v.reciprocal(smalls[:, 26:27], smalls[:, 18:19])
                v.reciprocal(smalls[:, 24:25], smalls[:, 22:23])
                v.tensor_scalar(smalls[:, 28:29], smalls[:, 26:27], 0.99,
                                None, ALU.mult)
                S_.bump(v.tensor_tensor(smalls[:, 25:26], smalls[:, 19:20],
                                        smalls[:, 24:25], ALU.mult),
                        "v_rc")
                S_.wait(v, "v_rc", S_.c["v_rc"])
                S_.bump(v.tensor_tensor(smalls[:, 29:30], smalls[:, 25:26],
                                        smalls[:, 28:29], ALU.min),
                        "v_rc")
                S_.wait(v, "v_rc", S_.c["v_rc"])
                S_.bump(v.tensor_scalar(ysb[:, tt, :], wbuf, smalls[:, 29:30],
                                        None, ALU.mult), "f_v")

        @block.tensor
        def _(t):
            t.wait_ge(S_.h["s_x"], SCROW)
            for st in range(4):
                S_.wait(t, "f_bc2", st + 1)
                S_.bump(t.matmul(psbc[st % 2][:], wbuf[:, 0:128],
                                 scale_bc[:, 512 * st:512 * st + 512],
                                 start=True, stop=True), "f_bc")

        psbcS.pop_all().close()
        psE = contextlib.ExitStack()
        psf = [psE.enter_context(nc.psum_tensor(f"psf{i}", [P, 1024], F32))
               for i in range(4)]

        @block.tensor
        def _(t):
            for half in range(2):
                t.wait_ge(S_.h["s_w"], 192 + 32 * half)
                t.wait_ge(S_.h["f_bc2"], 5)
                for tt in range(OWN):
                    i = half * OWN + tt
                    if i >= 3:
                        S_.wait(t, "f_yev", i - 2)
                    for k in range(KC):
                        for st in range(2):
                            sl = SS[2 * half + st]
                            ins = t.matmul(psf[i % 4][:, 512 * st:512 * st + 512],
                                           xnT_own[:, k, tt * P:(tt + 1) * P],
                                           sl[:, k, :], start=(k == 0),
                                           stop=(k == KC - 1))
                    S_.bump(ins, "f_mm")

        def _epi_act(s, tt):
            S_.wait(s, "f_v", 3 * tt + 1)
            s.activation(dump, wbuf, AF.Square,
                         accum_out=smalls[:, 16:17])
            S_.bump(s.drain(), "f_a")
            S_.wait(s, "f_v", 3 * tt + 2)
            s.activation(smalls[:, 31:32], smalls[:, 17:18], AF.Ln)
            s.activation(smalls[:, 18:19], smalls[:, 31:32], AF.Exp,
                         scale=0.5)
            S_.bump(s.activation(smalls[:, 19:20], smalls[:, 18:19], AF.Tanh),
                    "f_a")
            S_.wait(s, "f_v", 3 * tt + 3)
            S_.bump(s.activation(ysb[:, tt, :], ysb[:, tt, :], AF.Silu),
                    "g_si")

        @block.scalar
        def _(s):
            for tt in range(OWN):
                S_.wait(s, "f_mm", tt + 1)
                S_.bump(s.copy(ysb[:, tt, 0:1024], psf[tt % 4][:]), "f_yev")
            for tt in range(OWN):
                i = OWN + tt
                S_.wait(s, "f_mm", i + 1)
                cp = s.copy(ysb[:, tt, 1024:2048], psf[i % 4][:])
                if tt == OWN - 1:
                    S_.bump(s.drain(), "f_yev")
                else:
                    S_.bump(cp, "f_yev")
                if tt >= 1:
                    _epi_act(s, tt - 1)
            _epi_act(s, OWN - 1)

        if nph < 8:
            barrier()
            _finish()
            return nc
        psE.pop_all().close()

        psF = contextlib.ExitStack()
        pswo = [psF.enter_context(nc.psum_tensor(f"pswo{i}", [P, 1024], F32))
                for i in range(4)]

        @block.tensor
        def _(t):
            t.wait_ge(S_.h["f_yev"], 2 * OWN)
            t.wait_ge(S_.h["f_bc2"], 5)
            for wh in range(2):
                t.wait_ge(S_.h["s_w"], 256 + 32 * wh)
                for tt in range(OWN):
                    i = wh * OWN + tt
                    if i >= 4:
                        S_.wait(t, "wo_fin", i - 3)
                    for m in range(KC):
                        for st in range(2):
                            sl = SS[2 * wh + st]
                            ins = t.matmul(pswo[i % 4][:, 512 * st:512 * st + 512],
                                           ctxb[:, m, tt * P:(tt + 1) * P],
                                           sl[:, m, :], start=(m == 0),
                                           stop=(m == KC - 1))
                    S_.bump(ins, "wo_mm")

        @block.sync
        def _(q):
            S_.wait(q, "f_mm", 2 * OWN)
            for i in range(2 * OWN):
                wh, tt = divmod(i, OWN)
                if i >= 2:
                    S_.wait(q, "wo_fin", i - 1)
                S_.bump(q.dma_start(
                    out=xrel[i % 2],
                    in_=x_d[tt * P:(tt + 1) * P,
                            wh * 1024:wh * 1024 + 1024]), "s_x", 16)
                if i >= 1:
                    S_.wait(q, "wo_fin", i)
                    wh2, tt2 = divmod(i - 1, OWN)
                    S_.bump(q.dma_start(
                        out=out_d[tt2 * P:(tt2 + 1) * P,
                                  wh2 * 1024:wh2 * 1024 + 1024],
                        in_=outst[(i - 1) % 2]), "s_out", 16)
            S_.wait(q, "wo_fin", 2 * OWN)
            S_.bump(q.dma_start(
                out=out_d[(OWN - 1) * P:OWN * P, 1024:2048],
                in_=outst[(2 * OWN - 1) % 2]), "s_out", 16)
            q.wait_ge(S_.h["s_out"], 16 * 2 * OWN)

        XRBASE = SCROW

        @block.vector
        def _(v):
            for i in range(2 * OWN):
                wh, tt = divmod(i, OWN)
                S_.wait(v, "g_si", tt + 1)
                S_.wait(v, "wo_mm", i + 1)
                S_.wait(v, "s_x", XRBASE + 16 * (i + 1))
                if i >= 2:
                    S_.wait(v, "s_out", 16 * (i - 1))
                v.tensor_tensor(sgbuf, pswo[i % 4][:],
                                ysb[:, tt, wh * 1024:wh * 1024 + 1024],
                                ALU.mult)
                S_.bump(v.tensor_tensor(outst[i % 2], sgbuf, xrel[i % 2],
                                        ALU.add), "wo_fin")

        psF.pop_all().close()

    return nc


def kernel(L0_attn_hidden_states, L0_attn_Wq, L0_attn_Wk, L0_attn_Wv,
           L0_attn_Wphi, L0_attn_Wo, L0_ffn_weight, L0_ffn_scale):
    x = np.ascontiguousarray(np.asarray(L0_attn_hidden_states, dtype=np.float32))
    wq = np.ascontiguousarray(np.asarray(L0_attn_Wq, dtype=np.float32))
    wk = np.ascontiguousarray(np.asarray(L0_attn_Wk, dtype=np.float32))
    wv = np.ascontiguousarray(np.asarray(L0_attn_Wv, dtype=np.float32))
    wphi = np.ascontiguousarray(np.asarray(L0_attn_Wphi, dtype=np.float32))
    wo = np.ascontiguousarray(np.asarray(L0_attn_Wo, dtype=np.float32))
    wffn = np.ascontiguousarray(np.asarray(L0_ffn_weight, dtype=np.float32))
    scale = np.ascontiguousarray(
        np.asarray(L0_ffn_scale, dtype=np.float32)).reshape(1, D)

    identbf = np.eye(P, dtype=ml_dtypes.bfloat16)
    identf = np.eye(P, dtype=np.float32)
    onesbf = np.ones((P, 1), dtype=ml_dtypes.bfloat16)
    onesf = np.ones((1, P), dtype=np.float32)
    consts = np.zeros((P, 8), dtype=np.float32)
    consts[:, 0] = 1e-5

    inv = 1.0 / (10000.0 ** (np.arange(0, HD, 2, dtype=np.float64) / HD))
    tables = []
    for h in (0, 1):
        pos = (np.arange(S) + 1024 * h) % S
        ang = pos[:, None].astype(np.float64) * inv[None, :]
        cos = np.concatenate([np.cos(ang), np.cos(ang)], axis=1)
        sinp = np.concatenate([-np.sin(ang), np.sin(ang)], axis=1)
        cosr = cos.reshape(NT, P, HD).transpose(1, 0, 2).reshape(P, NT * HD)
        sinr = sinp.reshape(NT, P, HD).transpose(1, 0, 2).reshape(P, NT * HD)
        tables.append((np.ascontiguousarray(cosr.astype(np.float32)),
                       np.ascontiguousarray(sinr.astype(np.float32))))

    nc = _build()
    in_maps = []
    for c in range(8):
        b, h = divmod(c, 2)
        xc = np.ascontiguousarray(np.roll(x[b], -1024 * h, axis=0))
        in_maps.append({
            "x": xc, "wq": wq, "wk": wk, "wv": wv, "wphi": wphi,
            "wffn": wffn, "wo": wo, "scale": scale,
            "cosr": tables[h][0], "sinr": tables[h][1],
            "identbf": identbf, "identf": identf, "onesbf": onesbf,
            "onesf": onesf, "consts": consts,
        })
    res = run_bass_kernel_spmd(nc, in_maps, core_ids=list(range(8)))
    out = np.empty((4, S, D), dtype=np.float32)
    for c in range(8):
        b, h = divmod(c, 2)
        out[b, 1024 * h:1024 * h + 1024, :] = res.results[c]["out"]
    return out


# version 15 (final)
# speedup vs baseline: 1.1343x; 1.0542x over previous
import contextlib

import numpy as np
import ml_dtypes

import concourse.bass as bass
import concourse.mybir as mybir
from concourse.bass_utils import run_bass_kernel_spmd

F32 = mybir.dt.float32
BF16 = mybir.dt.bfloat16
AF = mybir.ActivationFunctionType
ALU = mybir.AluOpType
AX = mybir.AxisListType

P = 128
S = 2048
D = 2048
NT = 16
OWN = 8
HD = 64
KC = 16

O_IDENTBF = 0
O_IDENTF = 256
O_ONESBF = 768
O_ONESF = 800
O_CONSTS = 1312
O_SMALLS = 1376
O_LNST = 1632
O_QGF = 1760
O_QGBD = 1888
O_ALPHT = 2912
O_ALPHF = 4960

O_A = 16128
O_B = 48896
O_C = 81664
O_D = 114432
O_E = 147200
O_F = 179968
O_G = 196352
ARENA = 212736

OC_XT0, OC_XT1 = O_C, O_C + 8192
OC_XNB0, OC_XNB1 = O_C + 16384, O_C + 20480
OC_SC1, OC_SC2, OC_SC3 = O_C, O_C + 4096, O_C + 8192
OC_QB0, OC_QB1 = O_C + 12288, O_C + 13312
OC_WS = O_C + 14848
OD_W, OD_W2, OD_DUMP, OD_SCB = O_D, O_D + 8192, O_D + 16384, O_D + 20480
OE_XR0, OE_XR1 = O_A, O_A + 4096
OE_OS0, OE_OS1 = O_A + 8192, O_A + 12288
OF_COS, OF_SIN = O_F, O_F + 4096
OF_KKF = O_F
OF_AK0, OF_AK1 = O_F, O_F + 4096
OF_XP0, OF_XP1 = O_F, O_F + 2048
O_YSB = O_F


class SemL:
    def __init__(self, ctx, nc, names):
        self.h = {n: ctx.enter_context(nc.semaphore(n)) for n in names}
        self.c = {n: 0 for n in names}

    def bump(self, instr, name, amt=1):
        instr.then_inc(self.h[name], amt)
        self.c[name] += amt
        return self.c[name]

    def wait(self, eng, name, v):
        if v > 0:
            eng.wait_ge(self.h[name], v)


SEMS = [
    "s_x", "s_w", "s_out", "s_bar",
    "p1_st", "p1_mv", "p1_rs", "p1_ln", "p1_tp", "p1_ev",
    "kk_mm", "kk_pf", "kk_m0", "kk_e", "kk_kk",
    "kv_mm", "kv_vev",
    "q_mm", "q_pf", "q_m0", "q_e", "q_qk", "q_qg", "q_tp", "q_ev",
    "b_tpk", "b_tke", "b_bd", "b_lg", "b_mx", "b_ex", "b_al", "b_atp", "b_aev",
    "c_ak", "c_mm", "c_oev",
    "d_phi", "d_xev", "d_res", "d_ctx",
    "f_mm", "f_yev", "f_v", "f_a", "f_bc", "f_bc2",
    "g_si", "wo_mm", "wo_fin", "v_rc",
]


def _build(nph=99):
    nc = bass.Bass("TRN2", target_bir_lowering=False)

    x_d = nc.dram_tensor("x", [S, D], F32, kind="ExternalInput").ap()
    wq_d = nc.dram_tensor("wq", [D, D], F32, kind="ExternalInput").ap()
    wk_d = nc.dram_tensor("wk", [D, 512], F32, kind="ExternalInput").ap()
    wv_d = nc.dram_tensor("wv", [D, 512], F32, kind="ExternalInput").ap()
    wphi_d = nc.dram_tensor("wphi", [D, D], F32, kind="ExternalInput").ap()
    wffn_d = nc.dram_tensor("wffn", [D, D], F32, kind="ExternalInput").ap()
    wo_d = nc.dram_tensor("wo", [D, D], F32, kind="ExternalInput").ap()
    scale_d = nc.dram_tensor("scale", [1, D], F32, kind="ExternalInput").ap()
    cosr_d = nc.dram_tensor("cosr", [P, NT * HD], F32, kind="ExternalInput").ap()
    sinr_d = nc.dram_tensor("sinr", [P, NT * HD], F32, kind="ExternalInput").ap()
    identbf_d = nc.dram_tensor("identbf", [P, P], BF16, kind="ExternalInput").ap()
    identf_d = nc.dram_tensor("identf", [P, P], F32, kind="ExternalInput").ap()
    onesbf_d = nc.dram_tensor("onesbf", [P, 1], BF16, kind="ExternalInput").ap()
    onesf_d = nc.dram_tensor("onesf", [1, P], F32, kind="ExternalInput").ap()
    consts_d = nc.dram_tensor("consts", [P, 8], F32, kind="ExternalInput").ap()
    out_d = nc.dram_tensor("out", [S // 2, D], F32, kind="ExternalOutput").ap()

    wq_v = wq_d.rearrange("(k p) n -> p k n", p=P)
    wk_v = wk_d.rearrange("(k p) n -> p k n", p=P)
    wv_v = wv_d.rearrange("(k p) n -> p k n", p=P)
    wphi_v = wphi_d.rearrange("(k p) n -> p k n", p=P)
    wffn_v = wffn_d.rearrange("(k p) n -> p k n", p=P)
    wo_v = wo_d.rearrange("(k p) n -> p k n", p=P)

    ctx = contextlib.ExitStack()
    with ctx:
        ar = ctx.enter_context(nc.sbuf_tensor("arena", [P, ARENA], mybir.dt.uint8))

        def vw(off, dt, nelem, rows=None):
            nb = nelem * mybir.dt.size(dt)
            a = ar[:, off:off + nb] if rows is None else ar[0:rows, off:off + nb]
            return a.bitcast(dt)

        identbf = vw(O_IDENTBF, BF16, 128)
        identf = vw(O_IDENTF, F32, 128)
        onesbf = vw(O_ONESBF, BF16, 1)
        onesf = vw(O_ONESF, F32, 128, rows=1)
        constsv = vw(O_CONSTS, F32, 8)
        smalls = vw(O_SMALLS, F32, 64)
        lnst = vw(O_LNST, F32, 24)
        QgF = vw(O_QGF, F32, 16)
        qgbdp = vw(O_QGBD, BF16, 512)
        alpha_tm = vw(O_ALPHT, F32, 512)
        alpha_pad = vw(O_ALPHF, F32, 2048)
        alpha_fm = vw(O_ALPHF, F32, 2048, rows=8)
        outer_pad = vw(O_ALPHF, BF16, 2048)

        xnT_own = vw(O_A, BF16, KC * 1024).rearrange("p (k n) -> p k n", n=1024)
        xnT_oth = vw(O_B, BF16, KC * 1024).rearrange("p (k n) -> p k n", n=1024)
        QkF = vw(O_D, BF16, KC * 1024).rearrange("p (c n) -> p c n", n=1024)
        KkT = vw(O_E, BF16, NT * 512).rearrange("p (t n) -> p t n", n=512)
        VT = vw(O_E + 16384, BF16, NT * 512).rearrange("p (t n) -> p t n", n=512)
        KkF = vw(OF_KKF, BF16, 4 * 2048).rearrange("p (c n) -> p c n", n=2048)
        ctxb = vw(O_E, BF16, KC * 1024).rearrange("p (c n) -> p c n", n=1024)
        ysb = vw(O_YSB, BF16, OWN * D).rearrange("p (t n) -> p t n", n=D)

        xt = [vw(OC_XT0, F32, D), vw(OC_XT1, F32, D)]
        xnb = [vw(OC_XNB0, BF16, D), vw(OC_XNB1, BF16, D)]
        sc1 = vw(OC_SC1, F32, 1024)
        sc2 = vw(OC_SC2, F32, 1024)
        sc3 = vw(OC_SC3, F32, 1024)
        qbuf = [vw(OC_QB0, BF16, 512), vw(OC_QB1, BF16, 512)]
        slotG = vw(O_G, BF16, KC * 512).rearrange("p (k n) -> p k n", n=512)
        slotC = vw(OC_WS, BF16, KC * 512).rearrange("p (k n) -> p k n", n=512)
        slotD = vw(O_D + 8192, BF16, KC * 512).rearrange("p (k n) -> p k n",
                                                         n=512)
        S0 = vw(O_B, BF16, KC * 512).rearrange("p (k n) -> p k n", n=512)
        S1 = vw(O_B + 16384, BF16, KC * 512).rearrange("p (k n) -> p k n", n=512)
        S2 = vw(O_C, BF16, KC * 512).rearrange("p (k n) -> p k n", n=512)
        S3 = vw(O_C + 16384, BF16, KC * 512).rearrange("p (k n) -> p k n", n=512)
        SS = [S0, S1, S2, S3]
        cosr = vw(OF_COS, F32, NT * HD)
        sinr = vw(OF_SIN, F32, NT * HD)
        akbuf = [vw(OF_AK0, BF16, 2048), vw(OF_AK1, BF16, 2048)]
        xphi = [vw(OF_XP0, BF16, 1024), vw(OF_XP1, BF16, 1024)]
        wbuf = vw(OD_W, F32, D)
        w2buf = vw(OD_W2, F32, D)
        dump = vw(OD_DUMP, BF16, D)
        scale_bc = vw(OD_SCB, F32, D)
        sgbuf = vw(OD_W, F32, 1024)
        kc1 = [vw(O_D, F32, 512), vw(O_D + 2048, F32, 512)]
        kc2 = vw(O_D + 4096, F32, 512)
        kc3 = vw(O_D + 6144, F32, 512)
        xrel = [vw(OE_XR0, F32, 1024), vw(OE_XR1, F32, 1024)]
        outst = [vw(OE_OS0, F32, 1024), vw(OE_OS1, F32, 1024)]

        S_ = SemL(ctx, nc, SEMS)
        block = ctx.enter_context(nc.Block())
        nbar = [0]

        def barrier():
            nbar[0] += 3

            @block.tensor
            def _(t):
                t.drain().then_inc(S_.h["s_bar"], 1)
                t.wait_ge(S_.h["s_bar"], nbar[0])

            @block.vector
            def _(v):
                v.drain().then_inc(S_.h["s_bar"], 1)
                v.wait_ge(S_.h["s_bar"], nbar[0])

            @block.scalar
            def _(s):
                s.drain().then_inc(S_.h["s_bar"], 1)
                s.wait_ge(S_.h["s_bar"], nbar[0])

        def _finish():
            @block.sync
            def _(q):
                S_.wait(q, "s_bar", nbar[0])
                for r in range(8):
                    S_.bump(q.dma_start(
                        out=out_d[r * P:(r + 1) * P, :],
                        in_=ar[:, O_A:O_A + D * 4].bitcast(F32)), "s_out", 16)
                q.wait_ge(S_.h["s_out"], S_.c["s_out"])

        def xnT_k_t(k, t):
            if t < OWN:
                return xnT_own[:, k, t * P:(t + 1) * P]
            return xnT_oth[:, k, (t - OWN) * P:(t - OWN) * P + P]

        def rope_kappa_front(v, ps, t, pf_sem, b1=None, b2=None):
            if b1 is None:
                b1, b2 = sc1[:, 0:512], sc2[:, 0:512]
            ps3 = ps.ap()[:, 0:512].rearrange("p (h d) -> p h d", d=HD)
            cosb = cosr[:, t * HD:(t + 1) * HD].unsqueeze(1) \
                .broadcast_to([P, 8, HD])
            sin_a = sinr[:, t * HD:t * HD + 32].unsqueeze(1) \
                .broadcast_to([P, 8, 32])
            sin_b = sinr[:, t * HD + 32:t * HD + 64].unsqueeze(1) \
                .broadcast_to([P, 8, 32])
            s1 = b1.rearrange("p (h d) -> p h d", d=HD)
            s2 = b2.rearrange("p (h d) -> p h d", d=HD)
            v.tensor_tensor(s1, ps3, cosb, ALU.mult)
            v.tensor_tensor(s2[:, :, 0:32], ps3[:, :, 32:64], sin_a, ALU.mult)
            S_.bump(v.tensor_tensor(s2[:, :, 32:64], ps3[:, :, 0:32], sin_b,
                                    ALU.mult), pf_sem)
            v.tensor_tensor(b1, b1, b2, ALU.add)

        psA = contextlib.ExitStack()
        tpb = [psA.enter_context(nc.psum_tensor(f"tpb{i}", [P, 1024], BF16))
               for i in range(4)]
        psqg = psA.enter_context(nc.psum_tensor("psqg", [P, 16], F32))
        pp = [psA.enter_context(nc.psum_tensor(f"pp{i}", [P, 512], F32))
              for i in range(3)]

        @block.sync
        def _(q):
            S_.bump(q.dma_start(out=identbf, in_=identbf_d), "s_x", 16)
            S_.bump(q.dma_start(out=identf, in_=identf_d), "s_x", 16)
            S_.bump(q.dma_start(out=onesbf, in_=onesbf_d), "s_x", 16)
            S_.bump(q.dma_start(out=onesf, in_=onesf_d), "s_x", 16)
            S_.bump(q.dma_start(out=constsv, in_=consts_d), "s_x", 16)
            S_.bump(q.dma_start(out=cosr, in_=cosr_d), "s_x", 16)
            S_.bump(q.dma_start(out=sinr, in_=sinr_d), "s_x", 16)
            for t in range(NT):
                if t >= 2:
                    S_.wait(q, "p1_ln", t - 1)
                S_.bump(q.dma_start(out=xt[t % 2],
                                    in_=x_d[t * P:(t + 1) * P, :]), "s_x", 16)
        XBASE = 7 * 16

        @block.gpsimd
        def _(g):
            if nph >= 2:
                S_.bump(g.dma_start(out=slotG, in_=wk_v), "s_w", 16)
                S_.bump(g.dma_start(out=slotD, in_=wv_v), "s_w", 16)
            if nph >= 3:
                S_.wait(g, "kk_mm", NT)
                S_.bump(g.dma_start(out=slotG, in_=wq_v[:, :, 0:512]), "s_w", 16)
                S_.wait(g, "kv_mm", NT)
                S_.bump(g.dma_start(out=slotC, in_=wq_v[:, :, 512:1024]),
                        "s_w", 16)
                S_.wait(g, "q_mm", NT)
                S_.bump(g.dma_start(out=slotG, in_=wq_v[:, :, 1024:1536]),
                        "s_w", 16)
                S_.wait(g, "q_mm", 2 * NT)
                S_.bump(g.dma_start(out=slotC, in_=wq_v[:, :, 1536:2048]),
                        "s_w", 16)
            if nph >= 6:
                S_.wait(g, "q_mm", 4 * NT)
                S_.wait(g, "q_qg", 4 * NT)
                S_.wait(g, "q_qk", 4 * NT)
                for s in range(4):
                    S_.bump(g.dma_start(out=SS[s],
                                        in_=wphi_v[:, :, 512 * s:512 * s + 512]),
                            "s_w", 16)
            if nph >= 7:
                for s in range(4):
                    S_.wait(g, "d_phi", 4 * (s + 1))
                    S_.bump(g.dma_start(out=SS[s],
                                        in_=wffn_v[:, :, 512 * s:512 * s + 512]),
                            "s_w", 16)
            if nph >= 8:
                for s in range(4):
                    S_.wait(g, "f_mm", 8 * (s // 2 + 1))
                    S_.bump(g.dma_start(out=SS[s],
                                        in_=wo_v[:, :, 512 * s:512 * s + 512]),
                            "s_w", 16)

        def _ln_apply(v, t):
            par = t % 2
            S_.wait(v, "p1_rs", t + 1)
            S_.bump(v.tensor_scalar(xnb[t % 2], xt[t % 2],
                                    smalls[:, 4 * par:4 * par + 1],
                                    smalls[:, 4 * par + 2:4 * par + 3],
                                    ALU.subtract, ALU.mult), "p1_ln")
            for gidx in range(2):
                S_.wait(v, "p1_tp", 2 * t + gidx + 1)
                dst3 = xnT_own if t < OWN else xnT_oth
                tl = t if t < OWN else t - OWN
                dst = dst3[:, 8 * gidx:8 * gidx + 8, tl * P:tl * P + P]
                srcp = tpb[(t % 2) * 2 + gidx].ap() \
                    .rearrange("p (c n) -> p c n", n=P)
                S_.bump(v.tensor_copy(dst, srcp), "p1_ev")

        def _kk_front(v, j):
            S_.wait(v, "kk_mm", j + 1)
            rope_kappa_front(v, pp[j % 3], j, "kk_pf", kc1[j % 2], kc2)
            S_.bump(v.tensor_scalar(kc2, kc1[j % 2], 0.0, None, ALU.min),
                    "kk_m0")

        def _kk_kk(v, j):
            S_.wait(v, "kk_e", j + 1)
            S_.bump(v.scalar_tensor_tensor(KkT[:, j, :], kc1[j % 2], 0.0,
                                           kc3, ALU.max, ALU.add), "kk_kk")

        def _kk_sca(s, j):
            S_.wait(s, "kk_m0", j + 1)
            if j >= 1:
                S_.wait(s, "kk_kk", j)
            S_.bump(s.activation(kc3, kc2, AF.Exp), "kk_e")

        def _kk_mm(t, j):
            S_.wait(t, "p1_ev", 2 * j + 3 if j < NT - 1 else 2 * NT + 1)
            if j >= 2:
                S_.wait(t, "kk_pf", j - 1)
            for k in range(KC):
                ins = t.matmul(pp[j % 3][:], xnT_k_t(k, j),
                               slotG[:, k, :], start=(k == 0),
                               stop=(k == KC - 1))
            S_.bump(ins, "kk_mm")

        @block.vector
        def _(v):
            for t in range(NT):
                S_.wait(v, "s_x", XBASE + 16 * (t + 1))
                if t >= 2:
                    S_.wait(v, "p1_tp", 2 * (t - 2) + 2)
                par = t % 2
                lslot = lnst[:, 0:24] if par == 0 else smalls[:, 32:56]
                xb = xt[t % 2]
                for j in range(4):
                    ins = v.bn_stats(lslot[:, 6 * j:6 * j + 6],
                                     xb[:, 512 * j:512 * j + 512])
                S_.bump(ins, "p1_st")
                S_.wait(v, "p1_st", t + 1)
                S_.bump(v.bn_aggr(smalls[:, 4 * par:4 * par + 2], lslot),
                        "p1_mv")
                if t >= 1:
                    _ln_apply(v, t - 1)
                if t >= 4:
                    _kk_kk(v, t - 4)
                if t >= 3:
                    _kk_front(v, t - 3)
            _ln_apply(v, NT - 1)
            S_.bump(v.drain(), "p1_ev")
            for j in range(NT - 3, NT):
                _kk_kk(v, j - 1)
                _kk_front(v, j)
            _kk_kk(v, NT - 1)

        @block.scalar
        def _(s):
            for t in range(NT):
                par = t % 2
                S_.wait(s, "p1_mv", t + 1)
                s.activation(smalls[:, 3:4], smalls[:, 4 * par + 1:4 * par + 2],
                             AF.Ln, bias=constsv[:, 0:1])
                S_.bump(s.activation(smalls[:, 4 * par + 2:4 * par + 3],
                                     smalls[:, 3:4], AF.Exp, scale=-0.5),
                        "p1_rs")
                if t >= 3:
                    _kk_sca(s, t - 3)
            for j in range(NT - 3, NT):
                _kk_sca(s, j)

        @block.tensor
        def _(t):
            t.wait_ge(S_.h["s_w"], 16)
            for tt in range(NT):
                if tt >= 2:
                    _kk_mm(t, tt - 2)
                S_.wait(t, "p1_ln", tt + 1)
                for gidx in range(2):
                    S_.wait(t, "p1_ev", max(0, 2 * (tt - 2) + gidx + 2))
                    for c in range(8):
                        cc = 8 * gidx + c
                        t.transpose(
                            tpb[(tt % 2) * 2 + gidx][:, c * P:(c + 1) * P],
                            xnb[tt % 2][:, cc * P:(cc + 1) * P], identbf)
                    S_.bump(t.drain(), "p1_tp")
            _kk_mm(t, NT - 2)
            _kk_mm(t, NT - 1)

        @block.tensor
        def _(t):
            t.wait_ge(S_.h["s_w"], 32)
            S_.wait(t, "p1_ev", 2 * NT + 1)
            S_.wait(t, "kk_m0", NT)
            for tt in range(NT):
                if tt >= 2:
                    S_.wait(t, "kv_vev", tt - 1)
                for k in range(KC):
                    ins = t.matmul(pp[tt % 3][:], xnT_k_t(k, tt),
                                   slotD[:, k, :], start=(k == 0),
                                   stop=(k == KC - 1))
                S_.bump(ins, "kv_mm")

        @block.scalar
        def _(s):
            for tt in range(NT):
                S_.wait(s, "kv_mm", tt + 1)
                S_.bump(s.copy(VT[:, tt, :], pp[tt % 3][:]), "kv_vev")

        barrier()
        if nph < 3:
            _finish()
            return nc

        QTH = [48, 64, 80, 96]
        QSL = [slotG, slotC, slotG, slotC]

        def _q_back(t, i, nown):
            Q, tt = divmod(i, NT)
            S_.wait(t, "q_qk", i + 1)
            ins = None
            for c in range(4):
                col = 4 * Q + c
                ins = t.matmul(psqg[:, col:col + 1],
                               qbuf[i % 2][:, c * P:(c + 1) * P], onesbf,
                               start=(tt == 0 and c == 0),
                               stop=(tt == NT - 1 and c == 3),
                               skip_group_check=True)
            if tt < OWN:
                S_.wait(t, "q_ev", max(0, nown - 1))
                for c in range(4):
                    t.transpose(tpb[nown % 4][:, c * P:(c + 1) * P],
                                qbuf[i % 2][:, c * P:(c + 1) * P],
                                identbf)
                S_.bump(t.drain(), "q_tp")
                nown += 1
                S_.bump(t.nop(), "q_qg")
            else:
                S_.bump(ins, "q_qg")
            return nown

        @block.tensor
        def _(t):
            nown = 0
            for Q in range(4):
                t.wait_ge(S_.h["s_w"], QTH[Q])
                for tt in range(NT):
                    i = Q * NT + tt
                    if i < 2:
                        S_.wait(t, "kv_vev", 14 + i)
                    else:
                        S_.wait(t, "q_pf", i - 1)
                    for k in range(KC):
                        ins = t.matmul(pp[(i + 1) % 3][:], xnT_k_t(k, tt),
                                       QSL[Q][:, k, :], start=(k == 0),
                                       stop=(k == KC - 1))
                    S_.bump(ins, "q_mm")
                    if i >= 1:
                        nown = _q_back(t, i - 1, nown)
            _q_back(t, 4 * NT - 1, nown)

        @block.vector
        def _(v):
            nown = 0
            for Q in range(4):
                for tt in range(NT):
                    i = Q * NT + tt
                    S_.wait(v, "q_mm", i + 1)
                    rope_kappa_front(v, pp[(i + 1) % 3], tt, "q_pf")
                    S_.bump(v.tensor_scalar(sc2[:, 0:512], sc1[:, 0:512], 0.0,
                                            None, ALU.min), "q_m0")
                    S_.wait(v, "q_e", i + 1)
                    S_.wait(v, "q_qg", max(0, i - 1))
                    S_.bump(v.scalar_tensor_tensor(qbuf[i % 2], sc1[:, 0:512],
                                                   0.0, sc3[:, 0:512], ALU.max,
                                                   ALU.add), "q_qk")
                    if tt < OWN:
                        S_.wait(v, "q_tp", nown + 1)
                        dst = QkF[:, 4 * Q:4 * Q + 4, tt * P:tt * P + P]
                        src = tpb[nown % 4].ap()[:, 0:512] \
                            .rearrange("p (c n) -> p c n", n=P)
                        S_.bump(v.tensor_copy(dst, src), "q_ev")
                        nown += 1

        @block.scalar
        def _(s):
            for i in range(4 * NT):
                S_.wait(s, "q_m0", i + 1)
                S_.wait(s, "q_qk", i)
                S_.bump(s.activation(sc3[:, 0:512], sc2[:, 0:512], AF.Exp),
                        "q_e")
            s.wait_ge(S_.h["q_qg"], 4 * NT)
            s.copy(QgF, psqg[:])
            s.mul(QgF, QgF, 1.0 / float(S))

        barrier()
        if nph < 4:
            _finish()
            return nc
        psA.pop_all().close()

        psB = contextlib.ExitStack()
        pslg = psB.enter_context(nc.psum_tensor("pslg", [P, 2048], F32))
        psal = [psB.enter_context(nc.psum_tensor(f"psal{i}", [P, 512], F32))
                for i in range(2)]
        tpk = psB.enter_context(nc.psum_tensor("tpk", [P, 2048], BF16))

        def _a_tp(t, g, jb):
            m = 4 * g + jb
            S_.wait(t, "b_al", g + 1)
            S_.wait(t, "b_aev", m)
            for i in range(4):
                tt = 4 * jb + i
                t.transpose(psal[m % 2][:, i * P:(i + 1) * P],
                            alpha_pad[:, tt * P:(tt + 1) * P], identf)
            S_.bump(t.drain(), "b_atp")

        @block.tensor
        def _(t):
            for j in range(8):
                S_.wait(t, "b_tke", j)
                base = 1024 * (j % 2)
                for half in range(2):
                    tt = 2 * j + half
                    for c in range(4):
                        off = base + half * 512 + c * P
                        t.transpose(tpk[:, off:off + P],
                                    KkT[:, tt, c * P:(c + 1) * P],
                                    identbf)
                S_.bump(t.drain(), "b_tpk")
            S_.wait(t, "b_bd", 1)
            S_.wait(t, "b_tke", 8)
            for g in range(4):
                S_.wait(t, "b_ex", g)
                for st in range(4):
                    ins = t.matmul(pslg[:, 512 * st:512 * st + 512],
                                   qgbdp[:, 128 * g:128 * g + 128],
                                   KkF[:, g, 512 * st:512 * st + 512],
                                   start=True, stop=True)
                S_.bump(ins, "b_lg")
                if g >= 1:
                    for jb in range(4):
                        _a_tp(t, g - 1, jb)
            for jb in range(4):
                _a_tp(t, 3, jb)

        @block.vector
        def _(v):
            for j in range(8):
                S_.wait(v, "b_tpk", j + 1)
                base = 1024 * (j % 2)
                for half in range(2):
                    tt = 2 * j + half
                    dst = KkF[:, 0:4, tt * P:tt * P + P]
                    srcv = tpk.ap()[:, base + half * 512:base + half * 512 + 512] \
                        .rearrange("p (c n) -> p c n", n=P)
                    ins = v.tensor_copy(dst, srcv)
                S_.bump(ins, "b_tke")
            v.memset(qgbdp, 0.0)
            v.memset(alpha_pad, 0.0)
            ins = None
            for g in range(4):
                for j in range(8):
                    h = 8 * g + j
                    par = h % 2
                    srcq = QgF[64 * par:64 * par + 64, h // 2:h // 2 + 1]
                    dst = qgbdp[64 * (j // 4):64 * (j // 4) + 64,
                                128 * g + j:128 * g + j + 1]
                    ins = v.tensor_copy(dst, srcq)
            S_.bump(ins, "b_bd")
            for g in range(4):
                S_.wait(v, "b_lg", g + 1)
                S_.bump(v.reduce_max(smalls[0:8, 8:9], pslg[0:8, :], axis=AX.X,
                                     negate=True), "b_mx")
                S_.wait(v, "b_ex", g + 1)
                S_.bump(v.reciprocal(smalls[0:8, 10:11], smalls[0:8, 9:10]),
                        "v_rc")
                S_.wait(v, "v_rc", S_.c["v_rc"])
                S_.bump(v.tensor_scalar(alpha_fm, alpha_fm, smalls[0:8, 10:11],
                                        float(S), ALU.mult, ALU.mult), "b_al")
                for jb in range(4):
                    m = 4 * g + jb
                    S_.wait(v, "b_atp", m + 1)
                    srca = psal[m % 2].ap() \
                        .rearrange("p (i n) -> p i n", n=P)[:, :, 0:8]
                    dstv = alpha_tm.rearrange("p (t c) -> p t c", c=32)[
                        :, 4 * jb:4 * jb + 4, 8 * g:8 * g + 8]
                    S_.bump(v.tensor_copy(dstv, srca), "b_aev")

        @block.scalar
        def _(s):
            for g in range(4):
                S_.wait(s, "b_mx", g + 1)
                if g >= 1:
                    S_.wait(s, "b_al", g)
                    S_.wait(s, "b_atp", 4 * g)
                s.activation(alpha_fm, pslg[0:8, :], AF.Exp,
                             bias=smalls[0:8, 8:9],
                             accum_out=smalls[0:8, 9:10])
                S_.bump(s.drain(), "b_ex")

        barrier()
        if nph < 5:
            _finish()
            return nc
        psB.pop_all().close()

        psC = contextlib.ExitStack()
        psot = psC.enter_context(nc.psum_tensor("psot", [P, 1024], F32))

        @block.vector
        def _(v):
            v.memset(outer_pad, 0.0)
            for tt in range(NT):
                if tt >= 2:
                    S_.wait(v, "c_mm", tt - 1)
                kk = KkT[:, tt, :].rearrange("p (h d) -> p h d", d=HD) \
                    .unsqueeze(2).broadcast_to([P, 8, 4, HD])
                al = alpha_tm[:, tt * 32:(tt + 1) * 32] \
                    .rearrange("p (h j) -> p h j", j=4) \
                    .unsqueeze(3).broadcast_to([P, 8, 4, HD])
                dst = akbuf[tt % 2].rearrange("p (h j d) -> p h j d", j=4, d=HD)
                S_.bump(v.tensor_tensor(dst, kk, al, ALU.mult), "c_ak")

        @block.tensor
        def _(t):
            for tt in range(NT):
                S_.wait(t, "c_ak", tt + 1)
                for j in range(16):
                    ins = t.matmul(
                        psot[:, 64 * j:64 * j + 64],
                        akbuf[tt % 2][:, 128 * j:128 * j + 128],
                        VT[:, tt, 64 * (j // 2):64 * (j // 2) + 64],
                        start=(tt == 0 and j in (0, 8)),
                        stop=(tt == NT - 1), skip_group_check=True)
                S_.bump(ins, "c_mm")

        @block.scalar
        def _(s):
            S_.wait(s, "c_mm", NT)
            ope = outer_pad[0:64, :].rearrange("p (h e d) -> p h e d",
                                               e=2, d=HD)[:, :, 0, :]
            opo = outer_pad[64:128, :].rearrange("p (h e d) -> p h e d",
                                                 e=2, d=HD)[:, :, 1, :]
            s.copy(ope, psot[0:64, :].rearrange("p (h d) -> p h d", d=HD))
            S_.bump(s.copy(opo, psot[64:128, :]
                           .rearrange("p (h d) -> p h d", d=HD)), "c_oev")

        barrier()
        if nph < 6:
            _finish()
            return nc
        psC.pop_all().close()

        psD = contextlib.ExitStack()
        psph = [psD.enter_context(nc.psum_tensor(f"psph{i}", [P, 1024], F32))
                for i in range(2)]
        psrs = [psD.enter_context(nc.psum_tensor(f"psrs{i}", [P, 1024], F32))
                for i in range(2)]

        def _res_mm(t, m):
            S_.wait(t, "d_ctx", max(0, m - 1))
            S_.wait(t, "d_xev", m + 1)
            ins = None
            for st in range(2):
                ins = t.matmul(psrs[m % 2][:, 512 * st:512 * st + 512],
                               outer_pad[:, 128 * m:128 * m + 128],
                               QkF[:, m, 512 * st:512 * st + 512],
                               start=True, stop=True)
            S_.bump(ins, "d_res")

        @block.tensor
        def _(t):
            for m in range(KC):
                S_.wait(t, "s_w", 112 + 16 * (m // 4))
                S_.wait(t, "d_xev", max(0, m - 1))
                for k in range(KC):
                    for st in range(2):
                        ins = t.matmul(psph[m % 2][:, 512 * st:512 * st + 512],
                                       SS[m // 4][:, k,
                                                  128 * (m % 4):128 * (m % 4) + 128],
                                       xnT_own[:, k, 512 * st:512 * st + 512],
                                       start=(k == 0), stop=(k == KC - 1))
                S_.bump(ins, "d_phi")
                if m >= 1:
                    _res_mm(t, m - 1)
            _res_mm(t, KC - 1)

        @block.scalar
        def _(s):
            S_.wait(s, "c_oev", 1)
            for m in range(KC):
                S_.wait(s, "d_phi", m + 1)
                if m >= 2:
                    S_.wait(s, "d_ctx", m - 1)
                s.copy(xphi[m % 2], psph[m % 2][:])
                S_.bump(s.drain(), "d_xev")

        @block.vector
        def _(v):
            for m in range(KC):
                S_.wait(v, "d_res", m + 1)
                v.tensor_tensor(ctxb[:, m, :], psrs[m % 2][:],
                                xphi[m % 2], ALU.mult)
                S_.bump(v.drain(), "d_ctx")

        barrier()
        if nph < 7:
            _finish()
            return nc
        psD.pop_all().close()

        psbcS = contextlib.ExitStack()
        psbc = [psbcS.enter_context(nc.psum_tensor(f"psbc{i}", [P, 512], F32))
                for i in range(2)]

        @block.sync
        def _(q):
            S_.wait(q, "d_ctx", KC)
            S_.bump(q.dma_start(out=scale_bc[0:1, :], in_=scale_d), "s_x", 16)
        SCROW = S_.c["s_x"]

        @block.vector
        def _(v):
            v.memset(scale_bc, 0.0)
            v.memset(wbuf[:, 0:128], 1.0)
            S_.bump(v.nop(), "f_bc2")
            for st in range(4):
                S_.wait(v, "f_bc", st + 1)
                S_.bump(v.tensor_copy(scale_bc[:, 512 * st:512 * st + 512],
                                      psbc[st % 2][:]), "f_bc2")
            for tt in range(OWN):
                S_.wait(v, "f_yev", OWN + tt + 1)
                S_.bump(v.tensor_tensor(wbuf, ysb[:, tt, :], scale_bc,
                                        ALU.mult), "f_v")
                S_.wait(v, "f_a", 2 * tt + 1)
                S_.bump(v.tensor_scalar(smalls[:, 17:18], smalls[:, 16:17],
                                        1e-14, None, ALU.max), "f_v")
                S_.wait(v, "f_a", 2 * tt + 2)
                v.tensor_scalar(smalls[:, 22:23], smalls[:, 18:19], 1e-7,
                                None, ALU.add)
                v.reciprocal(smalls[:, 26:27], smalls[:, 18:19])
                v.reciprocal(smalls[:, 24:25], smalls[:, 22:23])
                v.tensor_scalar(smalls[:, 28:29], smalls[:, 26:27], 0.99,
                                None, ALU.mult)
                S_.bump(v.tensor_tensor(smalls[:, 25:26], smalls[:, 19:20],
                                        smalls[:, 24:25], ALU.mult),
                        "v_rc")
                S_.wait(v, "v_rc", S_.c["v_rc"])
                S_.bump(v.tensor_tensor(smalls[:, 29:30], smalls[:, 25:26],
                                        smalls[:, 28:29], ALU.min),
                        "v_rc")
                S_.wait(v, "v_rc", S_.c["v_rc"])
                S_.bump(v.tensor_scalar(ysb[:, tt, :], wbuf, smalls[:, 29:30],
                                        None, ALU.mult), "f_v")

        @block.tensor
        def _(t):
            t.wait_ge(S_.h["s_x"], SCROW)
            for st in range(4):
                S_.wait(t, "f_bc2", st + 1)
                S_.bump(t.matmul(psbc[st % 2][:], wbuf[:, 0:128],
                                 scale_bc[:, 512 * st:512 * st + 512],
                                 start=True, stop=True), "f_bc")

        psbcS.pop_all().close()
        psE = contextlib.ExitStack()
        psf = [psE.enter_context(nc.psum_tensor(f"psf{i}", [P, 1024], F32))
               for i in range(4)]

        @block.tensor
        def _(t):
            for half in range(2):
                t.wait_ge(S_.h["s_w"], 192 + 32 * half)
                t.wait_ge(S_.h["f_bc2"], 5)
                for tt in range(OWN):
                    i = half * OWN + tt
                    if i >= 3:
                        S_.wait(t, "f_yev", i - 2)
                    for k in range(KC):
                        for st in range(2):
                            sl = SS[2 * half + st]
                            ins = t.matmul(psf[i % 4][:, 512 * st:512 * st + 512],
                                           xnT_own[:, k, tt * P:(tt + 1) * P],
                                           sl[:, k, :], start=(k == 0),
                                           stop=(k == KC - 1))
                    S_.bump(ins, "f_mm")

        def _epi_act(s, tt):
            S_.wait(s, "f_v", 3 * tt + 1)
            s.activation(dump, wbuf, AF.Square,
                         accum_out=smalls[:, 16:17])
            S_.bump(s.drain(), "f_a")
            S_.wait(s, "f_v", 3 * tt + 2)
            s.activation(smalls[:, 31:32], smalls[:, 17:18], AF.Ln)
            s.activation(smalls[:, 18:19], smalls[:, 31:32], AF.Exp,
                         scale=0.5)
            S_.bump(s.activation(smalls[:, 19:20], smalls[:, 18:19], AF.Tanh),
                    "f_a")
            S_.wait(s, "f_v", 3 * tt + 3)
            S_.bump(s.activation(ysb[:, tt, :], ysb[:, tt, :], AF.Silu),
                    "g_si")

        @block.scalar
        def _(s):
            for tt in range(OWN):
                S_.wait(s, "f_mm", tt + 1)
                S_.bump(s.copy(ysb[:, tt, 0:1024], psf[tt % 4][:]), "f_yev")
            for tt in range(OWN):
                i = OWN + tt
                S_.wait(s, "f_mm", i + 1)
                cp = s.copy(ysb[:, tt, 1024:2048], psf[i % 4][:])
                if tt == OWN - 1:
                    S_.bump(s.drain(), "f_yev")
                else:
                    S_.bump(cp, "f_yev")
                if tt >= 2:
                    _epi_act(s, tt - 2)
            _epi_act(s, OWN - 2)
            _epi_act(s, OWN - 1)

        if nph < 8:
            barrier()
            _finish()
            return nc
        psE.pop_all().close()

        psF = contextlib.ExitStack()
        pswo = [psF.enter_context(nc.psum_tensor(f"pswo{i}", [P, 1024], F32))
                for i in range(4)]

        @block.tensor
        def _(t):
            t.wait_ge(S_.h["f_yev"], 2 * OWN)
            t.wait_ge(S_.h["f_bc2"], 5)
            for wh in range(2):
                t.wait_ge(S_.h["s_w"], 256 + 32 * wh)
                for tt in range(OWN):
                    i = wh * OWN + tt
                    if i >= 4:
                        S_.wait(t, "wo_fin", i - 3)
                    for m in range(KC):
                        for st in range(2):
                            sl = SS[2 * wh + st]
                            ins = t.matmul(pswo[i % 4][:, 512 * st:512 * st + 512],
                                           ctxb[:, m, tt * P:(tt + 1) * P],
                                           sl[:, m, :], start=(m == 0),
                                           stop=(m == KC - 1))
                    S_.bump(ins, "wo_mm")

        @block.sync
        def _(q):
            S_.wait(q, "f_mm", 2 * OWN)
            for i in range(2 * OWN):
                wh, tt = divmod(i, OWN)
                if i >= 2:
                    S_.wait(q, "wo_fin", i - 1)
                S_.bump(q.dma_start(
                    out=xrel[i % 2],
                    in_=x_d[tt * P:(tt + 1) * P,
                            wh * 1024:wh * 1024 + 1024]), "s_x", 16)
                if i >= 1:
                    S_.wait(q, "wo_fin", i)
                    wh2, tt2 = divmod(i - 1, OWN)
                    S_.bump(q.dma_start(
                        out=out_d[tt2 * P:(tt2 + 1) * P,
                                  wh2 * 1024:wh2 * 1024 + 1024],
                        in_=outst[(i - 1) % 2]), "s_out", 16)
            S_.wait(q, "wo_fin", 2 * OWN)
            S_.bump(q.dma_start(
                out=out_d[(OWN - 1) * P:OWN * P, 1024:2048],
                in_=outst[(2 * OWN - 1) % 2]), "s_out", 16)
            q.wait_ge(S_.h["s_out"], 16 * 2 * OWN)

        XRBASE = SCROW

        @block.vector
        def _(v):
            for i in range(2 * OWN):
                wh, tt = divmod(i, OWN)
                S_.wait(v, "g_si", tt + 1)
                S_.wait(v, "wo_mm", i + 1)
                S_.wait(v, "s_x", XRBASE + 16 * (i + 1))
                if i >= 2:
                    S_.wait(v, "s_out", 16 * (i - 1))
                v.tensor_tensor(sgbuf, pswo[i % 4][:],
                                ysb[:, tt, wh * 1024:wh * 1024 + 1024],
                                ALU.mult)
                S_.bump(v.tensor_tensor(outst[i % 2], sgbuf, xrel[i % 2],
                                        ALU.add), "wo_fin")

        psF.pop_all().close()

    return nc


def kernel(L0_attn_hidden_states, L0_attn_Wq, L0_attn_Wk, L0_attn_Wv,
           L0_attn_Wphi, L0_attn_Wo, L0_ffn_weight, L0_ffn_scale):
    x = np.ascontiguousarray(np.asarray(L0_attn_hidden_states, dtype=np.float32))
    wq = np.ascontiguousarray(np.asarray(L0_attn_Wq, dtype=np.float32))
    wk = np.ascontiguousarray(np.asarray(L0_attn_Wk, dtype=np.float32))
    wv = np.ascontiguousarray(np.asarray(L0_attn_Wv, dtype=np.float32))
    wphi = np.ascontiguousarray(np.asarray(L0_attn_Wphi, dtype=np.float32))
    wo = np.ascontiguousarray(np.asarray(L0_attn_Wo, dtype=np.float32))
    wffn = np.ascontiguousarray(np.asarray(L0_ffn_weight, dtype=np.float32))
    scale = np.ascontiguousarray(
        np.asarray(L0_ffn_scale, dtype=np.float32)).reshape(1, D)

    identbf = np.eye(P, dtype=ml_dtypes.bfloat16)
    identf = np.eye(P, dtype=np.float32)
    onesbf = np.ones((P, 1), dtype=ml_dtypes.bfloat16)
    onesf = np.ones((1, P), dtype=np.float32)
    consts = np.zeros((P, 8), dtype=np.float32)
    consts[:, 0] = 1e-5

    inv = 1.0 / (10000.0 ** (np.arange(0, HD, 2, dtype=np.float64) / HD))
    tables = []
    for h in (0, 1):
        pos = (np.arange(S) + 1024 * h) % S
        ang = pos[:, None].astype(np.float64) * inv[None, :]
        cos = np.concatenate([np.cos(ang), np.cos(ang)], axis=1)
        sinp = np.concatenate([-np.sin(ang), np.sin(ang)], axis=1)
        cosr = cos.reshape(NT, P, HD).transpose(1, 0, 2).reshape(P, NT * HD)
        sinr = sinp.reshape(NT, P, HD).transpose(1, 0, 2).reshape(P, NT * HD)
        tables.append((np.ascontiguousarray(cosr.astype(np.float32)),
                       np.ascontiguousarray(sinr.astype(np.float32))))

    nc = _build()
    in_maps = []
    for c in range(8):
        b, h = divmod(c, 2)
        xc = np.ascontiguousarray(np.roll(x[b], -1024 * h, axis=0))
        in_maps.append({
            "x": xc, "wq": wq, "wk": wk, "wv": wv, "wphi": wphi,
            "wffn": wffn, "wo": wo, "scale": scale,
            "cosr": tables[h][0], "sinr": tables[h][1],
            "identbf": identbf, "identf": identf, "onesbf": onesbf,
            "onesf": onesf, "consts": consts,
        })
    res = run_bass_kernel_spmd(nc, in_maps, core_ids=list(range(8)))
    out = np.empty((4, S, D), dtype=np.float32)
    for c in range(8):
        b, h = divmod(c, 2)
        out[b, 1024 * h:1024 * h + 1024, :] = res.results[c]["out"]
    return out
